# version 03
# baseline (speedup 1.0000x reference)
import contextlib
import numpy as np
import concourse.bass as bass
import concourse.mybir as mybir
from concourse.bass_utils import run_bass_kernel_spmd

F32 = mybir.dt.float32
I32 = mybir.dt.int32
U32 = mybir.dt.uint32
AX = mybir.AxisListType
ALU = mybir.AluOpType
ACT = mybir.ActivationFunctionType

NCORES = 8
NB = 2
N = 2048
C = 1024
H = 16
D = 64
S = 32
P = 64
K = 4
FC = 8
TK = 19
NSUB = 5
SCALE = D ** -0.5
LN_EPS = 1e-5
NKEY = 256
NEXP = 65536
BN_EPS = 1e-5
NTOK = NB * N
NTILE = NTOK // 128
DCH = C // 128
NSEL = S * TK
NBUF = 8


def _kv_ranges(ci):
    out = []
    for ncl in range(K):
        nf = FC if (ncl == 0 or ncl == ci) else NSUB
        out.append((ncl * FC * TK, nf * TK))
    return out


def _range_tiles(ci):
    out = []
    for r, (off, ln) in enumerate(_kv_ranges(ci)):
        c0 = 0
        bi = 0
        while c0 < ln:
            nr = min(128, ln - c0)
            out.append((r, bi, off + c0, nr))
            c0 += nr
            bi += 1
    return out


def _asm_pieces(sel_off, nrows):
    out = []
    done = 0
    while done < nrows:
        g = sel_off + done
        nr = min(nrows - done, 128 - (g % 128))
        out.append((g // 128, g % 128, done, nr))
        done += nr
    return out


class MK:
    def __init__(self, n_cores=NCORES, taps=(), stop_after=None, dbg_dram=False):
        self.nc = bass.Bass("TRN2", target_bir_lowering=False,
                            detect_race_conditions=False)
        self.n_cores = n_cores
        self.tap_names = set(taps)
        self.taps = {}
        self.stop_after = stop_after
        nc = self.nc
        self.x = nc.dram_tensor("x", [NB, N, C], F32, kind="ExternalInput").ap()
        self.wqkv = nc.dram_tensor("wqkv", [C, 3 * C], F32, kind="ExternalInput").ap()
        self.bqkv = nc.dram_tensor("bqkv", [3 * C], F32, kind="ExternalInput").ap()
        self.wproj = nc.dram_tensor("wproj", [C, C], F32, kind="ExternalInput").ap()
        self.bproj = nc.dram_tensor("bproj", [C], F32, kind="ExternalInput").ap()
        self.wq_in = nc.dram_tensor("wq_in", [C, 2 * C], F32, kind="ExternalInput").ap()
        self.gamma = nc.dram_tensor("gamma", [2 * C], F32, kind="ExternalInput").ap()
        self.beta = nc.dram_tensor("beta", [2 * C], F32, kind="ExternalInput").ap()
        self.keys = nc.dram_tensor("keys", [128, 16, NKEY], F32, kind="ExternalInput").ap()
        self.tbl = nc.dram_tensor("tbl", [NEXP, 2048], F32, kind="ExternalInput").ap()
        self.out = nc.dram_tensor("out", [NB, N, C], F32, kind="ExternalOutput").ap()
        dbk = dict(kind="ExternalOutput") if dbg_dram else {}
        self.xnT_s = nc.dram_tensor("xnT_s", [NB, DCH, 128, N], F32, **dbk).ap()
        self.q_s = nc.dram_tensor("q_s", [16, 128, NTOK], F32, **dbk).ap()
        self.resid_s = nc.dram_tensor("resid_s", [NB, N, C], F32, **dbk).ap()
        self.cc_in = nc.dram_tensor("cc_in", [128, 36], F32).ap()
        self.cc_out = nc.dram_tensor("cc_out", [128, 36], F32).ap()

        A = nc.alloc_sbuf_tensor
        self.eye = A("eye", [128, 128], F32)
        self.iota19 = A("iota19", [64, TK], F32)
        self.iota8 = A("iota8", [128, 8], F32)
        self.bqkv24 = A("bqkv24", [128, 24], F32)
        self.bq_rep = None
        self.bk_rep = None
        self.bv_rep = None
        self.bproj_rep = None
        self.gamma16 = A("gamma16", [128, 16], F32)
        self.beta16 = A("beta16", [128, 16], F32)
        self.keys_sb = A("keys_sb", [128, 16 * NKEY], F32)
        self.rstd_all = A("rstd_all", [128, NTILE], F32)
        self.nmr_all = A("nmr_all", [128, NTILE], F32)
        self.bnacc = A("bnacc", [128, 128], F32)
        self.bnacc2 = A("bnacc2", [128, 128], F32)
        self.bnstat = A("bnstat", [128, 36], F32)
        self.allr = A("allr", [128, 36], F32)
        self.bn_a = A("bn_a", [128, 16], F32)
        self.bn_b = A("bn_b", [128, 16], F32)
        self.sel2 = A("sel2", [128, K * TK], F32)
        self.idx_all = A("idx_all", [128, NTILE * 64], I32)
        self.g_all = A("g_all", [128, NTILE * 64], F32)

        self.sems = {}
        for s in ["ld", "ld2", "g", "st", "st2", "mm", "v", "a", "gp", "cc", "fr"]:
            self.sems[s] = [nc.alloc_semaphore(s), 0]

    def inc(self, name, instr, k=1):
        st = self.sems[name]
        instr.then_inc(st[0], k)
        st[1] += k
        return st[1]

    def n(self, name):
        return self.sems[name][1]

    def wt(self, eng, name, val=None):
        st = self.sems[name]
        eng.wait_ge(st[0], st[1] if val is None else val)

    def sep(self, eng):
        eng.drain()

    def tap(self, blk, name, src_ap):
        if name in self.tap_names and name not in self.taps:
            t = self.nc.dram_tensor("tap_" + name, list(src_ap.shape),
                                    src_ap.dtype, kind="ExternalOutput").ap()
            self.taps[name] = t

            def f(sy, t=t, src=src_ap):
                for s in ("ld", "ld2", "v", "a", "mm", "g"):
                    self.wt(sy, s)
                self.inc("st2", sy.dma_start(
                    out=t[tuple([slice(None)] * len(t.shape))], in_=src), 16)
            blk.scalar(f)

    def b0(self):
        nc = self.nc
        with (
            nc.sbuf_tensor("c_i32", [128, 128], I32) as c_i32,
            nc.sbuf_tensor("c_m24", [24, 128], F32) as c_m24,
            nc.sbuf_tensor("c_g16", [16, 128], F32) as c_g16,
            nc.sbuf_tensor("c_b16", [16, 128], F32) as c_b16,
            nc.psum_tensor("p_c", [128, 128], F32) as p_c,
            nc.psum_tensor("p_c2", [128, 16], F32) as p_c2,
            nc.psum_tensor("p_c3", [128, 16], F32) as p_c3,
            nc.Block("b0") as blk,
        ):
            def s_load(sy):
                self.inc("ld", sy.dma_start(out=c_m24[:, :], in_=self.bqkv.rearrange("(a b) -> a b", a=24)), 16)
                self.inc("ld", sy.dma_start(out=c_g16[:, :], in_=self.gamma.rearrange("(a b) -> a b", a=16)), 16)
                self.inc("ld", sy.dma_start(out=c_b16[:, :], in_=self.beta.rearrange("(a b) -> a b", a=16)), 16)
                self.inc("ld", sy.dma_start(out=self.keys_sb[:, :], in_=self.keys.rearrange("p a k -> p (a k)")), 16)
            blk.sync(s_load)

            def s_iota(g):
                g.iota(c_i32[:, :], pattern=[[1, 128]], base=0, channel_multiplier=-1)
                g.iota(self.iota19[:, :], pattern=[[1, TK]], base=1,
                       channel_multiplier=0, allow_small_or_imprecise_dtypes=True)
                self.inc("gp", g.iota(self.iota8[:, :], pattern=[[1, 8]], base=0,
                                      channel_multiplier=0,
                                      allow_small_or_imprecise_dtypes=True))
            blk.gpsimd(s_iota)

            def s_eye(v):
                self.wt(v, "gp")
                v.tensor_scalar(out=self.eye[:, :], in0=c_i32[:, :], scalar1=0,
                                scalar2=None, op0=ALU.is_equal)
                v.memset(self.bnstat[:, :], 0.0)
                v.memset(self.bnacc[:, :], 0.0)
                self.inc("v", v.memset(self.bnacc2[:, :], 0.0))
            blk.vector(s_eye)

            def s_tr(t):
                self.wt(t, "ld")
                self.wt(t, "v")
                t.transpose(p_c[:, 0:24], c_m24[:, :], self.eye[0:24, 0:24])
                t.transpose(p_c2[:, 0:16], c_g16[:, :], self.eye[0:16, 0:16])
                self.inc("mm", t.transpose(p_c3[:, 0:16], c_b16[:, :], self.eye[0:16, 0:16]))
            blk.tensor(s_tr)

            def s_ev(sc):
                self.wt(sc, "mm")
                sc.activation(out=self.bqkv24[:, :], in_=p_c[:, 0:24], func=ACT.Copy)
                sc.activation(out=self.gamma16[:, :], in_=p_c2[:, 0:16], func=ACT.Copy)
                self.inc("a", sc.activation(out=self.beta16[:, :], in_=p_c3[:, 0:16], func=ACT.Copy))
            blk.scalar(s_ev)

    def b1(self, b, wqk, xnT):
        nc = self.nc
        with (
            nc.sbuf_tensor(f"xnr{b}", [128, 2 * C], F32) as xnr,
            nc.sbuf_tensor(f"lnt{b}", [128, 8], F32) as lnt,
            nc.sbuf_tensor(f"sq_scr{b}", [128, C], F32) as sq_scr,
            nc.psum_tensor(f"ptA{b}", [128, 128], F32) as ptA,
            nc.psum_tensor(f"ptB{b}", [128, 128], F32) as ptB,
            nc.Block(f"b1_ln{b}") as blk,
        ):
            slot_guard = [0, 0]
            bank_guard = [0, 0]
            for tt in range(16):
                col = b * 16 + tt
                slot = tt % 2
                xt = xnr[:, slot * C:(slot + 1) * C]

                def s_load(sy, xt=xt, tt=tt, g=slot_guard[slot]):
                    self.wt(sy, "mm", g)
                    self.inc("ld", sy.dma_start(out=xt, in_=self.x[b, tt * 128:(tt + 1) * 128, :]), 16)
                blk.sync(s_load)

                def s_sums(sc, xt=xt):
                    self.wt(sc, "ld")
                    sc.activation(out=sq_scr[:, :], in_=xt, func=ACT.Copy,
                                  accum_out=lnt[:, 0:1])
                    self.inc("a", sc.activation(out=sq_scr[:, :], in_=xt, func=ACT.Square,
                                                accum_out=lnt[:, 1:2]))
                blk.scalar(s_sums)

                def s_stats(v):
                    self.wt(v, "a")
                    v.tensor_scalar_mul(lnt[:, 2:3], lnt[:, 0:1], 1.0 / C)
                    v.tensor_scalar_mul(lnt[:, 3:4], lnt[:, 1:2], 1.0 / C)
                    v.tensor_mul(lnt[:, 4:5], lnt[:, 2:3], lnt[:, 2:3])
                    self.sep(v)
                    v.tensor_sub(lnt[:, 5:6], lnt[:, 3:4], lnt[:, 4:5])
                    self.sep(v)
                    self.inc("v", v.tensor_scalar_add(lnt[:, 5:6], lnt[:, 5:6], LN_EPS))
                blk.vector(s_stats)

                def s_rstd(sc, col=col):
                    self.wt(sc, "v")
                    self.inc("a", sc.activation(out=lnt[:, 6:7],
                                                in_=lnt[:, 5:6], func=ACT.Sqrt))
                blk.scalar(s_rstd)

                def s_nmr(v, col=col):
                    self.wt(v, "a")
                    v.reciprocal(out=self.rstd_all[:, col:col + 1], in_=lnt[:, 6:7])
                    self.sep(v)
                    self.inc("v", v.scalar_tensor_tensor(
                        out=self.nmr_all[:, col:col + 1], in0=lnt[:, 2:3], scalar=-1.0,
                        in1=self.rstd_all[:, col:col + 1], op0=ALU.mult, op1=ALU.mult))
                blk.vector(s_nmr)

                def s_norm(v, xt=xt, col=col):
                    self.sep(v)
                    self.inc("v", v.tensor_scalar(
                        out=xt, in0=xt, scalar1=self.rstd_all[:, col:col + 1],
                        scalar2=self.nmr_all[:, col:col + 1],
                        op0=ALU.mult, op1=ALU.add))
                blk.vector(s_norm)

                for dch in range(DCH):
                    pbank = (tt * DCH + dch) % 2
                    ps = ptA if pbank == 0 else ptB

                    def s_tp(t, xt=xt, dch=dch, ps=ps, g=bank_guard[pbank]):
                        self.wt(t, "v")
                        self.wt(t, "a", g)
                        self.inc("mm", t.transpose(
                            ps[:, :], xt[:, dch * 128:(dch + 1) * 128],
                            self.eye[:, :]))
                    blk.tensor(s_tp)

                    def s_ev(sc, tt=tt, dch=dch, ps=ps):
                        self.wt(sc, "mm")
                        self.inc("a", sc.activation(
                            out=xnT[dch][:, tt * 128:(tt + 1) * 128],
                            in_=ps[:, :], func=ACT.Copy))
                    blk.scalar(s_ev)
                    bank_guard[pbank] = self.n("a")
                slot_guard[slot] = self.n("mm")

            def s_spill(sy):
                self.wt(sy, "a")
                for dch in range(DCH):
                    self.inc("st", sy.dma_start(out=self.xnT_s[b, dch, :, :],
                                                in_=xnT[dch][:, :]), 16)
            blk.sync(s_spill)

        with (
            nc.sbuf_tensor(f"kfl{b}", [128, 16 * 128], F32) as kfl,
            nc.sbuf_tensor(f"bqr{b}", [128, C], F32) as bqr,
            nc.sbuf_tensor(f"bkr{b}", [128, C], F32) as bkr,
            nc.sbuf_tensor(f"qkf{b}", [128, C], F32) as qkf,
            nc.sbuf_tensor(f"kkf{b}", [128, C], F32) as kkf,
            nc.sbuf_tensor(f"dotb{b}", [128, C + 32], F32) as dotb,
            nc.psum_tensor(f"pqA{b}", [128, 512], F32) as pqA,
            nc.psum_tensor(f"pqB{b}", [128, 512], F32) as pqB,
            nc.Block(f"b1_kf{b}") as blk,
        ):
            def s_brep(sy):
                self.inc("ld", sy.dma_start(out=bqr[:, :], in_=self.bqkv[0:C].unsqueeze(0).to_broadcast([128, C])), 16)
                self.inc("ld", sy.dma_start(out=bkr[:, :], in_=self.bqkv[C:2 * C].unsqueeze(0).to_broadcast([128, C])), 16)
            blk.sync(s_brep)

            def s_kfl(v):
                for kft in range(2):
                    for dch in range(DCH):
                        lhs = xnT[dch][:, :].rearrange(
                            "p (g r) -> p g r", g=4)[:, kft * 2:(kft + 1) * 2, 0:64]
                        v.tensor_copy(
                            out=kfl[:, (kft * DCH + dch) * 128:(kft * DCH + dch + 1) * 128],
                            in_=lhs)
                self.inc("v", v.tensor_copy(out=kfl[:, 0:1], in_=kfl[:, 0:1]))
            blk.vector(s_kfl)
            for kft in range(2):
                for qk in range(2):
                    dst = qkf if qk == 0 else kkf
                    rep = bqr if qk == 0 else bkr
                    for half in range(2):
                        ps = pqA if half == 0 else pqB

                        def s_mm(t, kft=kft, qk=qk, half=half, ps=ps):
                            self.wt(t, "ld")
                            self.wt(t, "v")
                            for dch in range(DCH):
                                lhs = kfl[:, (kft * DCH + dch) * 128:(kft * DCH + dch + 1) * 128]
                                self.inc("mm", t.matmul(
                                    ps[:, :], lhs,
                                    wqk[dch][:, qk * C + half * 512: qk * C + (half + 1) * 512],
                                    start=(dch == 0), stop=(dch == DCH - 1)))
                        blk.tensor(s_mm)

                        def s_ev(v, half=half, ps=ps, dst=dst, rep=rep):
                            self.wt(v, "mm")
                            self.inc("v", v.tensor_add(
                                dst[:, half * 512:(half + 1) * 512], ps[:, :],
                                rep[:, half * 512:(half + 1) * 512]))
                        blk.vector(s_ev)

                def s_dot(v, kft=kft):
                    self.sep(v)
                    v.tensor_mul(dotb[:, 0:C], qkf[:, :], kkf[:, :])
                    self.sep(v)
                    v.reduce_sum(out=dotb[:, C:C + 16],
                                 in_=dotb[:, 0:C].rearrange("p (h d) -> p h d", h=H),
                                 axis=AX.X)
                    self.sep(v)
                    v.reduce_max(out=dotb[:, C + 16:C + 17], in_=dotb[:, C:C + 16],
                                 axis=AX.X)
                    self.sep(v)
                    self.inc("v", v.tensor_add(self.bnstat[:, 32 + kft:33 + kft],
                                               self.bnstat[:, 32 + kft:33 + kft],
                                               dotb[:, C + 16:C + 17]))
                blk.vector(s_dot)
            self.tap(blk, f"qkf_{b}", qkf[:, :])
            self.tap(blk, f"kkf_{b}", kkf[:, :])
            self.tap(blk, f"dotb_{b}", dotb[:, :])

        with (
            nc.sbuf_tensor(f"wqjc{b}", [128, 2 * DCH * 128], F32) as wqjc,
            nc.sbuf_tensor(f"qev{b}", [128, 2 * 512], F32) as qev,
            nc.sbuf_tensor(f"qsq{b}", [128, 512], F32) as qsq,
            nc.psum_tensor(f"pfA{b}", [128, 512], F32) as pfA,
            nc.psum_tensor(f"pfB{b}", [128, 512], F32) as pfB,
            nc.Block(f"b1_q{b}") as blk,
        ):
            ev_guard = [0, 0]
            w_guard = [0, 0]
            for jc in range(16):
                wslot = jc % 2

                def s_wload(sy, jc=jc, wslot=wslot, g=w_guard[wslot]):
                    self.wt(sy, "mm", g)
                    src = self.wq_in.rearrange("(dc p) (jcc j) -> p dc jcc j",
                                               dc=DCH, jcc=16)[:, :, jc, :]
                    self.inc("ld", sy.dma_start(
                        out=wqjc[:, wslot * DCH * 128:(wslot + 1) * DCH * 128],
                        in_=src), 16)
                blk.sync(s_wload)

                for t4 in range(4):
                    ps = pfA if t4 % 2 == 0 else pfB
                    eslot = t4 % 2

                    def s_mm(t, jc=jc, t4=t4, ps=ps, wslot=wslot):
                        self.wt(t, "ld")
                        self.wt(t, "a")
                        for dch in range(DCH):
                            self.inc("mm", t.matmul(
                                ps[:, :],
                                wqjc[:, wslot * DCH * 128 + dch * 128:
                                     wslot * DCH * 128 + (dch + 1) * 128],
                                xnT[dch][:, t4 * 512:(t4 + 1) * 512],
                                start=(dch == 0), stop=(dch == DCH - 1)))
                    blk.tensor(s_mm)

                    def s_ev(sc, jc=jc, t4=t4, ps=ps, eslot=eslot, g=ev_guard[eslot]):
                        self.wt(sc, "mm")
                        self.wt(sc, "st", g)
                        acol = jc * 8 + b * 4 + t4
                        sc.activation(out=qev[:, eslot * 512:(eslot + 1) * 512],
                                      in_=ps[:, :], func=ACT.Copy,
                                      accum_out=self.bnacc[:, acol:acol + 1])
                        self.inc("a", sc.activation(
                            out=qsq[:, :], in_=qev[:, eslot * 512:(eslot + 1) * 512],
                            func=ACT.Square, accum_out=self.bnacc2[:, acol:acol + 1]))
                    blk.scalar(s_ev)

                    def s_spill(sy, jc=jc, t4=t4, eslot=eslot):
                        self.wt(sy, "a")
                        self.inc("st", sy.dma_start(
                            out=self.q_s[jc, :, b * N + t4 * 512: b * N + (t4 + 1) * 512],
                            in_=qev[:, eslot * 512:(eslot + 1) * 512]), 16)
                    blk.sync(s_spill)
                    ev_guard[eslot] = self.n("st")
                w_guard[wslot] = self.n("mm")

    def b2(self):
        nc = self.nc
        with (
            nc.sbuf_tensor("sc2", [2, 128], F32) as sc2,
            nc.sbuf_tensor("sc4", [4, 64], F32) as sc4,
            nc.sbuf_tensor("thr", [4, 80], F32) as thr,
            nc.sbuf_tensor("csA", [4, 64], F32) as csA,
            nc.sbuf_tensor("csB", [4, 64], F32) as csB,
            nc.sbuf_tensor("selT", [64, 8], F32) as selT,
            nc.sbuf_tensor("b2t", [128, 16], F32) as b2t,
            nc.psum_tensor("p_b2", [128, 128], F32) as p_b2,
            nc.psum_tensor("p_b2b", [128, 16], F32) as p_b2b,
            nc.Block("b2") as blk,
        ):
            def s_red(v):
                v.reduce_sum(out=self.bnstat[:, 0:16],
                             in_=self.bnacc[:, :].rearrange("p (jc t) -> p jc t", jc=16),
                             axis=AX.X)
                self.inc("v", v.reduce_sum(
                    out=self.bnstat[:, 16:32],
                    in_=self.bnacc2[:, :].rearrange("p (jc t) -> p jc t", jc=16),
                    axis=AX.X))
            blk.vector(s_red)

            def s_cc0(sy):
                self.wt(sy, "v")
                self.inc("st", sy.dma_start(out=self.cc_in[:, :], in_=self.bnstat[:, :]), 16)
            blk.sync(s_cc0)

            def s_cc(g):
                self.wt(g, "st")
                self.inc("cc", g.collective_compute(
                    "AllReduce", ALU.add,
                    replica_groups=[list(range(self.n_cores))],
                    ins=[self.cc_in[:, :]], outs=[self.cc_out[:, :]]))
            blk.gpsimd(s_cc)

            def s_cc1(sy):
                self.wt(sy, "cc")
                self.inc("ld", sy.dma_start(out=self.allr[:, :], in_=self.cc_out[:, :]), 16)
            blk.sync(s_cc1)

            def s_bn(v):
                self.wt(v, "ld")
                mean = b2t[:, 0:16]
                v.tensor_scalar_mul(mean, self.allr[:, 0:16], 1.0 / (NTOK * self.n_cores))
                v.tensor_scalar_mul(self.bn_b[:, :], self.allr[:, 16:32], 1.0 / (NTOK * self.n_cores))
                v.tensor_mul(self.bn_a[:, :], mean, mean)
                self.sep(v)
                v.tensor_sub(self.bn_b[:, :], self.bn_b[:, :], self.bn_a[:, :])
                self.sep(v)
                self.inc("v", v.tensor_scalar_add(self.bn_b[:, :], self.bn_b[:, :], BN_EPS))
            blk.vector(s_bn)

            def s_bn2(sc):
                self.wt(sc, "v")
                self.inc("a", sc.activation(out=self.bn_a[:, :], in_=self.bn_b[:, :],
                                            func=ACT.Sqrt))
            blk.scalar(s_bn2)

            def s_bn3(v):
                self.wt(v, "a")
                v.reciprocal(out=self.bn_a[:, :], in_=self.bn_a[:, :])
                self.sep(v)
                v.tensor_mul(self.bn_a[:, :], self.bn_a[:, :], self.gamma16[:, :])
                self.sep(v)
                v.tensor_mul(self.bn_b[:, :], b2t[:, 0:16], self.bn_a[:, :])
                self.sep(v)
                self.inc("v", v.tensor_sub(self.bn_b[:, :], self.beta16[:, :], self.bn_b[:, :]))
            blk.vector(s_bn3)

            def s_tr(t):
                self.wt(t, "v")
                self.inc("mm", t.transpose(p_b2[0:2, 0:128],
                                           self.allr[:, 32:34], self.eye[:, :]))
            blk.tensor(s_tr)

            def s_ev(sc):
                self.wt(sc, "mm")
                self.inc("a", sc.activation(out=sc2[:, :], in_=p_b2[0:2, 0:128],
                                            func=ACT.Copy))
            blk.scalar(s_ev)

            def s_mv(sy):
                self.wt(sy, "a")
                for kf in range(4):
                    self.inc("ld", sy.dma_start(
                        out=sc4[kf:kf + 1, :],
                        in_=sc2[kf // 2:kf // 2 + 1, (kf % 2) * 64:(kf % 2) * 64 + 64]), 16)
            blk.sync(s_mv)

            def s_thr(v):
                self.wt(v, "ld")
                v.max(out=thr[:, 0:8], in_=sc4[:, :])
                self.sep(v)
                v.match_replace(out=thr[:, 16:80], in_to_replace=thr[:, 0:8],
                                in_values=sc4[:, :], imm_value=-1e30)
                self.sep(v)
                v.max(out=thr[:, 8:16], in_=thr[:, 16:80])
                self.sep(v)
                v.match_replace(out=thr[:, 16:80], in_to_replace=thr[:, 8:16],
                                in_values=thr[:, 16:80], imm_value=-1e30)
                self.sep(v)
                v.max(out=thr[:, 0:8], in_=thr[:, 16:80])
                self.sep(v)
                v.tensor_scalar(out=csA[:, :], in0=sc4[:, :], scalar1=thr[:, 2:3],
                                scalar2=None, op0=ALU.is_ge)
                self.sep(v)
                self.inc("v", v.tensor_copy(out=csB[:, :], in_=csA[:, :]))
            blk.vector(s_thr)

            def s_cum(v):
                a, bb = csB, thr[:, 16:80]
                for sh in [1, 2, 4, 8, 16, 32]:
                    self.sep(v)
                    v.tensor_copy(out=bb[:, 0:sh], in_=a[:, 0:sh])
                    v.tensor_add(bb[:, sh:64], a[:, sh:64], a[:, 0:64 - sh])
                    a, bb = bb, a
                self.sep(v)
                self.inc("v", v.tensor_copy(out=csB[:, :], in_=a[:, :]))
            blk.vector(s_cum)

            def s_tr2(t):
                self.wt(t, "v")
                t.transpose(p_b2[0:64, 4:8], csB[:, :], self.eye[0:4, 0:4])
                self.inc("mm", t.transpose(p_b2b[0:64, 0:4], csA[:, :], self.eye[0:4, 0:4]))
            blk.tensor(s_tr2)

            def s_ev2(sc):
                self.wt(sc, "mm")
                sc.activation(out=selT[:, 0:4], in_=p_b2[0:64, 4:8], func=ACT.Copy)
                self.inc("a", sc.activation(out=selT[:, 4:8], in_=p_b2b[0:64, 0:4],
                                            func=ACT.Copy))
            blk.scalar(s_ev2)

            def s_sel(v):
                self.wt(v, "a")
                for ncl in range(K):
                    v.tensor_tensor(
                        out=self.sel2[0:64, ncl * TK:(ncl + 1) * TK],
                        in0=selT[:, ncl:ncl + 1].to_broadcast([64, TK]),
                        in1=self.iota19[:, :], op=ALU.is_equal)
                    self.sep(v)
                    v.tensor_scalar_mul(self.sel2[0:64, ncl * TK:(ncl + 1) * TK],
                                        self.sel2[0:64, ncl * TK:(ncl + 1) * TK],
                                        selT[:, 4 + ncl:5 + ncl])
                self.inc("v", v.tensor_copy(out=csA[:, :], in_=csA[:, :]))
            blk.vector(s_sel)

            def s_dup(sy):
                self.wt(sy, "v")
                self.inc("ld", sy.dma_start(out=self.sel2[64:128, :],
                                            in_=self.sel2[0:64, :]), 16)
            blk.sync(s_dup)
            self.tap(blk, "gamma16", self.gamma16[:, :])
            self.tap(blk, "beta16", self.beta16[:, :])
            self.tap(blk, "bqkv24", self.bqkv24[:, :])
            self.tap(blk, "sel2", self.sel2[:, :])
            self.tap(blk, "bn_a", self.bn_a[:, :])
            self.tap(blk, "bn_b", self.bn_b[:, :])
            self.tap(blk, "sc4", sc4[:, :])
            self.tap(blk, "csA", csA[:, :])
            self.tap(blk, "csB", csB[:, :])
            self.tap(blk, "selT", selT[:, :])
            self.tap(blk, "allr", self.allr[:, :])
            self.tap(blk, "b2t", b2t[:, :])
            self.tap(blk, "thr", thr[:, :])

    def b3(self, b):
        nc = self.nc
        es = contextlib.ExitStack()
        klocT = [es.enter_context(nc.sbuf_tensor(f"klocT{b}_{i}", [128, NSEL], F32)) for i in range(DCH)]
        vloc = [es.enter_context(nc.sbuf_tensor(f"vloc{b}_{i}", [128, C], F32)) for i in range(5)]
        bvr = es.enter_context(nc.sbuf_tensor(f"bvr{b}", [128, C], F32))
        bpr = es.enter_context(nc.sbuf_tensor(f"bpr{b}", [128, C], F32))
        es2 = contextlib.ExitStack()
        wbuf = [es2.enter_context(nc.sbuf_tensor(f"wb{b}_{i}", [128, C], F32)) for i in range(DCH)]
        xselT = [es2.enter_context(nc.sbuf_tensor(f"xselT{b}_{i}", [128, NSEL], F32)) for i in range(DCH)]
        with nc.Block(f"b3pre{b}") as blk:
            def s_brep(sy):
                self.inc("ld", sy.dma_start(out=bvr[:, :], in_=self.bqkv[2 * C:3 * C].unsqueeze(0).to_broadcast([128, C])), 16)
                self.inc("ld", sy.dma_start(out=bpr[:, :], in_=self.bproj[0:C].unsqueeze(0).to_broadcast([128, C])), 16)
            blk.sync(s_brep)

        with (
            nc.sbuf_tensor(f"xc{b}", [128, 2 * C], F32) as xc,
            nc.sbuf_tensor(f"xsf{b}", [32, 2 * C], F32) as xsf,
            nc.psum_tensor(f"pcmp{b}", [32, C], F32) as pcmp,
            nc.psum_tensor(f"ptrA{b}", [128, 32], F32) as ptrA,
            nc.psum_tensor(f"ptrB{b}", [128, 32], F32) as ptrB,
            nc.Block(f"b3s{b}") as blk,
        ):
            xc_guard = [0, 0]
            xs_guard = [0, 0]
            tr_guard = [0, 0]
            for tt in range(16):
                slot = tt % 2
                xt = xc[:, slot * C:(slot + 1) * C]

                def s_load(sy, tt=tt, xt=xt, g=xc_guard[slot]):
                    self.wt(sy, "mm", g)
                    self.inc("ld", sy.dma_start(out=xt, in_=self.x[b, tt * 128:(tt + 1) * 128, :]), 16)
                blk.sync(s_load)

                def s_norm(v, tt=tt, xt=xt):
                    self.wt(v, "ld")
                    col = b * 16 + tt
                    self.inc("v", v.tensor_scalar(
                        out=xt, in0=xt, scalar1=self.rstd_all[:, col:col + 1],
                        scalar2=self.nmr_all[:, col:col + 1],
                        op0=ALU.mult, op1=ALU.add))
                blk.vector(s_norm)

                for fr in range(2):
                    s = 2 * tt + fr
                    ncl = s // FC
                    fslot = s % 2

                    def s_cmp(t, xt=xt, fr=fr, ncl=ncl, fslot=fslot, g=xs_guard[fslot]):
                        self.wt(t, "v")
                        self.wt(t, "a", g)
                        for hcol in range(2):
                            self.inc("mm", t.matmul(
                                pcmp[0:TK, hcol * 512:(hcol + 1) * 512],
                                self.sel2[fr * 64:(fr + 1) * 64, ncl * TK:(ncl + 1) * TK],
                                xt[fr * 64:(fr + 1) * 64, hcol * 512:(hcol + 1) * 512],
                                start=True, stop=True))
                    blk.tensor(s_cmp)

                    def s_evc(sc, fslot=fslot):
                        self.wt(sc, "mm")
                        self.inc("a", sc.activation(
                            out=xsf[0:TK, fslot * C:(fslot + 1) * C],
                            in_=pcmp[0:TK, :], func=ACT.Copy))
                    blk.scalar(s_evc)

                    for dch in range(DCH):
                        tslot = dch % 2
                        ptr = ptrA if tslot == 0 else ptrB

                        def s_tp(t, dch=dch, fslot=fslot, ptr=ptr, g=tr_guard[tslot]):
                            self.wt(t, "a")
                            self.wt(t, "a", g)
                            self.inc("mm", t.transpose(
                                ptr[:, 0:TK],
                                xsf[0:TK, fslot * C + dch * 128: fslot * C + (dch + 1) * 128],
                                self.eye[0:TK, 0:TK]))
                        blk.tensor(s_tp)

                        def s_evt(sc, s=s, dch=dch, ptr=ptr):
                            self.wt(sc, "mm")
                            self.inc("a", sc.activation(
                                out=xselT[dch][:, s * TK:(s + 1) * TK],
                                in_=ptr[:, 0:TK], func=ACT.Copy))
                        blk.scalar(s_evt)
                        tr_guard[tslot] = self.n("a")
                    xs_guard[fslot] = self.n("a")
                xc_guard[slot] = self.n("mm")
            self.tap(blk, f"xselT0_{b}", xselT[0][:, :])

        with nc.Block(f"b3k{b}") as blk:
            def s_wk(sy):
                for dch in range(DCH):
                    self.inc("ld", sy.dma_start(out=wbuf[dch][:, :],
                                                in_=self.wqkv[dch * 128:(dch + 1) * 128, C:2 * C]), 16)
            blk.sync(s_wk)
        with (
            nc.psum_tensor(f"pk1{b}", [128, 512], F32) as pk1,
            nc.psum_tensor(f"pk2{b}", [128, 512], F32) as pk2,
            nc.Block(f"b3k2{b}") as blk,
        ):
            for hdc in range(DCH):
                for cpart, (c0, cl) in enumerate([(0, 512), (512, NSEL - 512)]):
                    ps = pk1 if cpart == 0 else pk2

                    def s_mm(t, hdc=hdc, c0=c0, cl=cl, ps=ps):
                        self.wt(t, "ld")
                        self.wt(t, "a")
                        for dch in range(DCH):
                            self.inc("mm", t.matmul(
                                ps[:, 0:cl], wbuf[dch][:, hdc * 128:(hdc + 1) * 128],
                                xselT[dch][:, c0:c0 + cl],
                                start=(dch == 0), stop=(dch == DCH - 1)))
                    blk.tensor(s_mm)

                    def s_ev(sc, hdc=hdc, c0=c0, cl=cl, ps=ps):
                        self.wt(sc, "mm")
                        self.inc("a", sc.activation(
                            out=klocT[hdc][:, c0:c0 + cl], in_=ps[:, 0:cl],
                            func=ACT.Identity, bias=self.bqkv24[:, 8 + hdc:9 + hdc]))
                    blk.scalar(s_ev)
        with nc.Block(f"b3v{b}") as blk:
            def s_wv(sy):
                self.wt(sy, "a")
                for dch in range(DCH):
                    self.inc("ld", sy.dma_start(out=wbuf[dch][:, :],
                                                in_=self.wqkv[dch * 128:(dch + 1) * 128, 2 * C:3 * C]), 16)
            blk.sync(s_wv)
        with (
            nc.psum_tensor(f"pv1{b}", [128, 512], F32) as pv1,
            nc.psum_tensor(f"pv2{b}", [128, 512], F32) as pv2,
            nc.Block(f"b3v2{b}") as blk,
        ):
            for st_ in range(5):
                rows = min(128, NSEL - st_ * 128)
                for half in range(2):
                    ps = pv1 if half == 0 else pv2

                    def s_mm(t, st_=st_, rows=rows, half=half, ps=ps):
                        self.wt(t, "ld")
                        self.wt(t, "v")
                        for dch in range(DCH):
                            self.inc("mm", t.matmul(
                                ps[0:rows, :], xselT[dch][:, st_ * 128:st_ * 128 + rows],
                                wbuf[dch][:, half * 512:(half + 1) * 512],
                                start=(dch == 0), stop=(dch == DCH - 1)))
                    blk.tensor(s_mm)

                    def s_ev(v, st_=st_, rows=rows, half=half, ps=ps):
                        self.wt(v, "mm")
                        self.inc("v", v.tensor_add(
                            vloc[st_][0:rows, half * 512:(half + 1) * 512],
                            ps[0:rows, :],
                            bvr[0:rows, half * 512:(half + 1) * 512]))
                    blk.vector(s_ev)
            self.tap(blk, f"kloc0_{b}", klocT[0][:, :])
            self.tap(blk, f"vloc0_{b}", vloc[0][:, :])

        es2.close()
        es3 = contextlib.ExitStack()
        wbuf = [es3.enter_context(nc.sbuf_tensor(f"wc{b}_{i}", [128, C], F32)) for i in range(DCH)]
        for ci in range(K):
            self._b3_ci(b, ci, klocT, vloc, wbuf, bpr)
        es3.close()
        es.close()

    def _b3_ci(self, b, ci, klocT, vloc, wbuf, bpr):
        nc = self.nc
        ranges = _kv_ranges(ci)
        L = sum(l for _, l in ranges)
        rtiles = _range_tiles(ci)
        poff = {}
        acc = 0
        for r, (off, ln) in enumerate(ranges):
            poff[r] = acc
            acc += ln
        with (
            nc.sbuf_tensor(f"qT{b}{ci}", [128, DCH * 512], F32) as qTc,
            nc.sbuf_tensor(f"xns{b}{ci}", [128, DCH * 512], F32) as xns,
            nc.sbuf_tensor(f"vr{b}{ci}", [128, len(rtiles) * C], F32) as vr,
            nc.sbuf_tensor(f"probs{b}{ci}", [128, 2 * 512], F32) as probs,
            nc.sbuf_tensor(f"probsT{b}{ci}", [128, 2 * 128], F32) as probsT,
            nc.sbuf_tensor(f"attn{b}{ci}", [128, 4 * C], F32) as attn,
            nc.sbuf_tensor(f"smax{b}{ci}", [128, 16], F32) as smax,
            nc.sbuf_tensor(f"rs{b}{ci}", [128, 8], F32) as rs,
            nc.sbuf_tensor(f"xres{b}{ci}", [128, 2 * C], F32) as xres,
            nc.psum_tensor(f"plA{b}{ci}", [128, 512], F32) as plA,
            nc.psum_tensor(f"plB{b}{ci}", [128, 512], F32) as plB,
            nc.psum_tensor(f"ptA{b}{ci}", [128, 128], F32) as ptA,
            nc.psum_tensor(f"ptB{b}{ci}", [128, 128], F32) as ptB,
            nc.psum_tensor(f"paA{b}{ci}", [128, 64], F32) as paA,
            nc.psum_tensor(f"paB{b}{ci}", [128, 64], F32) as paB,
            nc.psum_tensor(f"ppA{b}{ci}", [128, 512], F32) as ppA,
            nc.psum_tensor(f"ppB{b}{ci}", [128, 512], F32) as ppB,
            nc.Block(f"b3c{b}{ci}") as blk,
        ):
            def s_ld(sy):
                self.wt(sy, "v")
                self.wt(sy, "a")
                for dch in range(DCH):
                    self.inc("ld", sy.dma_start(
                        out=xns[:, dch * 512:(dch + 1) * 512],
                        in_=self.xnT_s[b, dch, :, ci * 512:(ci + 1) * 512]), 16)
                    self.inc("ld", sy.dma_start(
                        out=wbuf[dch][:, :],
                        in_=self.wqkv[dch * 128:(dch + 1) * 128, 0:C]), 16)
            blk.sync(s_ld)

            def s_vr(sy):
                for ti, (r, bi, soff, nr) in enumerate(rtiles):
                    for (stile, srow, drow, n2) in _asm_pieces(soff, nr):
                        self.inc("ld2", sy.dma_start(
                            out=vr[:, ti * C:(ti + 1) * C][drow:drow + n2, :],
                            in_=vloc[stile][srow:srow + n2, :]), 16)
            blk.scalar(s_vr)

            pg = [pk for pk in (plA, plB)]
            for hdc in range(DCH):
                ps = pg[hdc % 2]

                def s_mm(t, hdc=hdc, ps=ps):
                    self.wt(t, "ld")
                    self.wt(t, "a")
                    for dch in range(DCH):
                        self.inc("mm", t.matmul(
                            ps[:, :], wbuf[dch][:, hdc * 128:(hdc + 1) * 128],
                            xns[:, dch * 512:(dch + 1) * 512],
                            start=(dch == 0), stop=(dch == DCH - 1)))
                blk.tensor(s_mm)

                def s_ev(sc, hdc=hdc, ps=ps):
                    self.wt(sc, "mm")
                    self.inc("a", sc.activation(
                        out=qTc[:, hdc * 512:(hdc + 1) * 512], in_=ps[:, :],
                        func=ACT.Identity, bias=self.bqkv24[:, hdc:hdc + 1]))
                blk.scalar(s_ev)

            gd = dict(pa=[0, 0], pl=[self.n("a"), self.n("a")], pt=[0, 0],
                      pr=[0, 0], prT=[0, 0])
            for h in range(H):
                for qt in range(4):
                    self._b3_hqt(blk, b, ci, h, qt, gd, ranges, poff, L, rtiles,
                                 qTc, klocT, vr, probs, probsT, attn, smax, rs,
                                 plA, plB, ptA, ptB, paA, paB)
            pt_guard = gd["pt"]

            def s_wp(sy):
                self.wt(sy, "mm")
                for dch in range(DCH):
                    self.inc("ld", sy.dma_start(out=wbuf[dch][:, :],
                                                in_=self.wproj[dch * 128:(dch + 1) * 128, :]), 16)
            blk.sync(s_wp)

            attnT = xns
            res_guard = [0, 0]
            pp_guard = [0, 0]
            for qt in range(4):
                for dch in range(DCH):
                    pt = ptA if dch % 2 == 0 else ptB

                    def s_tp(t, qt=qt, dch=dch, pt=pt, g=pt_guard[dch % 2]):
                        self.wt(t, "v")
                        self.wt(t, "a", g)
                        self.inc("mm", t.transpose(
                            pt[:, :], attn[:, qt * C + dch * 128: qt * C + (dch + 1) * 128],
                            self.eye[:, :]))
                    blk.tensor(s_tp)

                    def s_evt(sc, dch=dch, pt=pt):
                        self.wt(sc, "mm")
                        self.inc("a", sc.activation(
                            out=attnT[:, dch * 128:(dch + 1) * 128], in_=pt[:, :],
                            func=ACT.Copy))
                    blk.scalar(s_evt)
                    pt_guard[dch % 2] = self.n("a")

                if qt == 0:
                    self.tap(blk, f"attnT0_{b}{ci}", attnT[:, 0:DCH * 128])

                def s_xld(sy, qt=qt, g=res_guard[qt % 2]):
                    self.wt(sy, "st", g)
                    self.inc("ld", sy.dma_start(
                        out=xres[:, (qt % 2) * C:(qt % 2 + 1) * C],
                        in_=self.x[b, ci * 512 + qt * 128: ci * 512 + (qt + 1) * 128, :]), 16)
                blk.sync(s_xld)

                for half in range(2):
                    pp = ppA if half == 0 else ppB

                    def s_pj(t, qt=qt, half=half, pp=pp, g=pp_guard[half]):
                        self.wt(t, "ld")
                        self.wt(t, "a")
                        self.wt(t, "v", g)
                        for dch in range(DCH):
                            self.inc("mm", t.matmul(
                                pp[:, :], attnT[:, dch * 128:(dch + 1) * 128],
                                wbuf[dch][:, half * 512:(half + 1) * 512],
                                start=(dch == 0), stop=(dch == DCH - 1)))
                    blk.tensor(s_pj)

                    def s_res(v, qt=qt, half=half, pp=pp):
                        self.wt(v, "mm")
                        self.wt(v, "ld")
                        v.tensor_add(
                            xres[:, (qt % 2) * C + half * 512:(qt % 2) * C + (half + 1) * 512],
                            xres[:, (qt % 2) * C + half * 512:(qt % 2) * C + (half + 1) * 512],
                            bpr[:, half * 512:(half + 1) * 512])
                        self.sep(v)
                        self.inc("v", v.tensor_add(
                            xres[:, (qt % 2) * C + half * 512:(qt % 2) * C + (half + 1) * 512],
                            xres[:, (qt % 2) * C + half * 512:(qt % 2) * C + (half + 1) * 512],
                            pp[:, :]))
                    blk.vector(s_res)

                pp_guard[0] = pp_guard[1] = self.n("v")
                if qt == 0:
                    self.tap(blk, f"xres0_{b}{ci}", xres[:, 0:C])

                def s_sp(sy, qt=qt):
                    self.wt(sy, "v")
                    self.inc("st", sy.dma_start(
                        out=self.resid_s[b, ci * 512 + qt * 128: ci * 512 + (qt + 1) * 128, :],
                        in_=xres[:, (qt % 2) * C:(qt % 2 + 1) * C]), 16)
                blk.sync(s_sp)
                res_guard[qt % 2] = self.n("st")
            if ci == 0:
                self.tap(blk, f"qT0_{b}", qTc[:, :])
                self.tap(blk, f"attn0_{b}", attn[:, :])


    def _b3_hqt(self, blk, b, ci, h, qt, gd, ranges, poff, L, rtiles,
                qTc, klocT, vr, probs, probsT, attn, smax, rs,
                plA, plB, ptA, ptB, paA, paB):
        chunk, hof = h // 2, (h % 2) * 64
        pa = paA if qt % 2 == 0 else paB
        pl = plA if qt % 2 == 0 else plB
        pslot = qt % 2

        def s_qk(t, g=gd["pl"][qt % 2]):
            self.wt(t, "a", g)
            for r, (off, ln) in enumerate(ranges):
                self.inc("mm", t.matmul(
                    pl[:, poff[r]:poff[r] + ln],
                    qTc[:, chunk * 512 + qt * 128: chunk * 512 + (qt + 1) * 128][hof:hof + 64, :],
                    klocT[chunk][hof:hof + 64, off:off + ln],
                    start=(r == 0), stop=(r == len(ranges) - 1)))
        blk.tensor(s_qk)

        def s_max(v):
            self.wt(v, "mm")
            v.reduce_max(out=smax[:, qt:qt + 1], in_=pl[:, 0:L],
                         axis=AX.X, negate=True)
            self.sep(v)
            self.inc("v", v.tensor_scalar_mul(smax[:, qt:qt + 1],
                                              smax[:, qt:qt + 1], SCALE))
        blk.vector(s_max)

        def s_exp(sc, g=gd["pr"][pslot]):
            self.wt(sc, "v")
            self.wt(sc, "mm", g)
            self.inc("a", sc.activation(
                out=probs[:, pslot * 512: pslot * 512 + L], in_=pl[:, 0:L],
                func=ACT.Exp, scale=SCALE, bias=smax[:, qt:qt + 1],
                accum_out=rs[:, qt:qt + 1]))
        blk.scalar(s_exp)

        npieces = len(rtiles)
        for pi, (r, bi, soff, nr) in enumerate(rtiles):
            pk0 = poff[r] + bi * 128
            pt = ptA if pi % 2 == 0 else ptB
            tslot = pi % 2

            def s_tp(t, pk0=pk0, nr=nr, pt=pt, g=gd["pt"][tslot]):
                self.wt(t, "a")
                self.wt(t, "a", g)
                self.inc("mm", t.transpose(
                    pt[0:nr, 0:128],
                    probs[:, pslot * 512 + pk0: pslot * 512 + pk0 + nr],
                    self.eye[:, :]))
            blk.tensor(s_tp)

            def s_evt(sc, nr=nr, pt=pt, tslot=tslot, g=gd["prT"][tslot]):
                self.wt(sc, "mm")
                self.wt(sc, "mm", g)
                self.inc("a", sc.activation(
                    out=probsT[0:nr, tslot * 128:tslot * 128 + 128],
                    in_=pt[0:nr, 0:128], func=ACT.Copy))
            blk.scalar(s_evt)
            gd["pt"][tslot] = self.n("a")

            def s_pv(t, pi=pi, nr=nr, ti=pi, tslot=tslot,
                     g=gd["pa"][qt % 2]):
                self.wt(t, "a")
                if pi == 0:
                    self.wt(t, "a", g)
                    self.wt(t, "ld2")
                self.inc("mm", t.matmul(
                    pa[:, 0:64],
                    probsT[0:nr, tslot * 128:tslot * 128 + 128],
                    vr[:, ti * C:(ti + 1) * C][0:nr, h * 64:(h + 1) * 64],
                    start=(pi == 0), stop=(pi == npieces - 1)))
            blk.tensor(s_pv)
            gd["prT"][tslot] = self.n("mm")
        gd["pl"][qt % 2] = self.n("a")
        gd["pr"][pslot] = self.n("mm")

        if b == 0 and ci == 0 and h == 0 and qt == 0:
            self.tap(blk, "probs00", probs[:, 0:512])
            self.tap(blk, "probsT00", probsT[:, :])
            self.tap(blk, "smax00", smax[:, :])
            self.tap(blk, "rs00", rs[:, :])

        def s_attn_ev(v):
            self.wt(v, "a")
            self.inc("v", v.reciprocal(out=rs[:, 4 + qt:5 + qt], in_=rs[:, qt:qt + 1]))
        blk.vector(s_attn_ev)

        def s_attn_ev2(sc):
            self.wt(sc, "v")
            self.wt(sc, "mm")
            self.inc("a", sc.activation(
                out=attn[:, qt * C + h * 64: qt * C + (h + 1) * 64],
                in_=pa[:, 0:64], func=ACT.Identity,
                scale=rs[:, 4 + qt:5 + qt]))
        blk.scalar(s_attn_ev2)
        gd["pa"][qt % 2] = self.n("a")

    def b4(self):
        nc = self.nc
        with (
            nc.sbuf_tensor("qf", [128, 2 * 2048], F32) as qf,
            nc.sbuf_tensor("simb", [128, 2 * 4096], F32) as simb,
            nc.sbuf_tensor("tkb", [128, 256], F32) as tkb,
            nc.sbuf_tensor("tku", [128, 64], U32) as tku,
            nc.psum_tensor("psA", [128, 256], F32) as psA,
            nc.psum_tensor("psB", [128, 256], F32) as psB,
            nc.Block("b4") as blk,
        ):
            qf_guard = [0, 0]
            sim_guard = [0, 0]
            ps_guard = [0, 0]
            for tg in range(NTILE):
                b, tt = tg // 16, tg % 16
                qslot = tg % 2
                sslot = tg % 2

                def s_ld(sy, tg=tg, qslot=qslot, g=qf_guard[qslot]):
                    self.wt(sy, "a", g)
                    for jc in range(16):
                        self.inc("ld", sy.dma_start(
                            out=qf[:, qslot * 2048 + jc * 128: qslot * 2048 + (jc + 1) * 128],
                            in_=self.q_s[jc, :, tg * 128:(tg + 1) * 128]), 16)
                blk.sync(s_ld)

                def s_bn(v, qslot=qslot):
                    self.wt(v, "ld")
                    for jc in range(16):
                        sl = qf[:, qslot * 2048 + jc * 128: qslot * 2048 + (jc + 1) * 128]
                        ins = v.tensor_scalar(out=sl, in0=sl,
                                              scalar1=self.bn_a[:, jc:jc + 1],
                                              scalar2=self.bn_b[:, jc:jc + 1],
                                              op0=ALU.mult, op1=ALU.add)
                    self.inc("v", ins)
                blk.vector(s_bn)

                for jc in range(16):
                    ps = psA if jc % 2 == 0 else psB

                    def s_mm(t, jc=jc, qslot=qslot, ps=ps, g=ps_guard[jc % 2]):
                        self.wt(t, "v")
                        self.wt(t, "a", g)
                        self.inc("mm", t.matmul(
                            ps[:, :],
                            qf[:, qslot * 2048 + jc * 128: qslot * 2048 + (jc + 1) * 128],
                            self.keys_sb[:, jc * NKEY:(jc + 1) * NKEY],
                            start=True, stop=True))
                    blk.tensor(s_mm)

                    def s_ev(sc, jc=jc, sslot=sslot, ps=ps, g=sim_guard[sslot] if jc == 0 else None):
                        self.wt(sc, "mm")
                        if g is not None:
                            self.wt(sc, "v", g)
                        self.inc("a", sc.activation(
                            out=simb[:, sslot * 4096 + jc * NKEY: sslot * 4096 + (jc + 1) * NKEY],
                            in_=ps[:, :], func=ACT.Copy))
                    blk.scalar(s_ev)
                    ps_guard[jc % 2] = self.n("a")
                qf_guard[qslot] = self.n("a")

                for hf in range(8):
                    def s_tk1(v, tg=tg, hf=hf, sslot=sslot):
                        self.wt(v, "a")
                        sx = simb[:, sslot * 4096 + hf * NKEY: sslot * 4096 + (hf + 1) * NKEY]
                        sy_ = simb[:, sslot * 4096 + (8 + hf) * NKEY: sslot * 4096 + (9 + hf) * NKEY]
                        v.max(out=tkb[:, 0:8], in_=sx)
                        v.max(out=tkb[:, 8:16], in_=sy_)
                        v.max_index(out=tku[:, 0:8], in_max=tkb[:, 0:8], in_values=sx)
                        v.max_index(out=tku[:, 8:16], in_max=tkb[:, 8:16], in_values=sy_)
                        v.tensor_copy(out=tkb[:, 16:24], in_=tku[:, 0:8])
                        v.tensor_copy(out=tkb[:, 24:32], in_=tku[:, 8:16])
                        v.tensor_tensor(
                            out=tkb[:, 32:96].rearrange("p (i j) -> p i j", i=8),
                            in0=tkb[:, 0:8].unsqueeze(2).to_broadcast([128, 8, 8]),
                            in1=tkb[:, 8:16].unsqueeze(1).to_broadcast([128, 8, 8]),
                            op=ALU.add)
                        self.sep(v)
                        v.max(out=tkb[:, 96:104], in_=tkb[:, 32:96])
                        self.sep(v)
                        v.max_index(out=tku[:, 16:24], in_max=tkb[:, 96:104],
                                    in_values=tkb[:, 32:96])
                        self.sep(v)
                        v.tensor_scalar(out=tku[:, 24:32], in0=tku[:, 16:24],
                                        scalar1=3, scalar2=None,
                                        op0=ALU.logical_shift_right)
                        v.tensor_scalar(out=tku[:, 32:40], in0=tku[:, 16:24],
                                        scalar1=7, scalar2=None, op0=ALU.bitwise_and)
                        self.sep(v)
                        v.tensor_copy(out=tkb[:, 104:112], in_=tku[:, 24:32])
                        v.tensor_copy(out=tkb[:, 112:120], in_=tku[:, 32:40])
                        v.tensor_tensor(
                            out=tkb[:, 128:192].rearrange("p (s j) -> p s j", s=8),
                            in0=tkb[:, 104:112].unsqueeze(2).to_broadcast([128, 8, 8]),
                            in1=self.iota8[:, :].unsqueeze(1).to_broadcast([128, 8, 8]),
                            op=ALU.is_equal)
                        self.sep(v)
                        v.tensor_tensor(
                            out=tkb[:, 192:256].rearrange("p (s j) -> p s j", s=8),
                            in0=tkb[:, 128:192].rearrange("p (s j) -> p s j", s=8),
                            in1=tkb[:, 16:24].unsqueeze(1).to_broadcast([128, 8, 8]),
                            op=ALU.mult)
                        self.sep(v)
                        v.reduce_sum(out=tkb[:, 120:128],
                                     in_=tkb[:, 192:256].rearrange("p (s j) -> p s j", s=8),
                                     axis=AX.X)
                        v.tensor_tensor(
                            out=tkb[:, 128:192].rearrange("p (s j) -> p s j", s=8),
                            in0=tkb[:, 112:120].unsqueeze(2).to_broadcast([128, 8, 8]),
                            in1=self.iota8[:, :].unsqueeze(1).to_broadcast([128, 8, 8]),
                            op=ALU.is_equal)
                        self.sep(v)
                        v.tensor_tensor(
                            out=tkb[:, 192:256].rearrange("p (s j) -> p s j", s=8),
                            in0=tkb[:, 128:192].rearrange("p (s j) -> p s j", s=8),
                            in1=tkb[:, 24:32].unsqueeze(1).to_broadcast([128, 8, 8]),
                            op=ALU.mult)
                        self.sep(v)
                        v.reduce_sum(out=tkb[:, 136:144],
                                     in_=tkb[:, 192:256].rearrange("p (s j) -> p s j", s=8),
                                     axis=AX.X)
                        self.sep(v)
                        v.scalar_tensor_tensor(out=tkb[:, 144:152], in0=tkb[:, 120:128],
                                               scalar=256.0, in1=tkb[:, 136:144],
                                               op0=ALU.mult, op1=ALU.add)
                        self.sep(v)
                        v.tensor_copy(out=self.idx_all[:, tg * 64 + hf * 8: tg * 64 + hf * 8 + 8],
                                      in_=tkb[:, 144:152])
                        self.inc("v", v.tensor_scalar_mul(tkb[:, 152:153], tkb[:, 96:97], -1.0))
                    blk.vector(s_tk1)

                    def s_tk2(sc):
                        self.wt(sc, "v")
                        self.inc("a", sc.activation(out=tkb[:, 160:168], in_=tkb[:, 96:104],
                                                    func=ACT.Exp, bias=tkb[:, 152:153],
                                                    accum_out=tkb[:, 153:154]))
                    blk.scalar(s_tk2)

                    def s_tk3(v, tg=tg, hf=hf):
                        self.wt(v, "a")
                        v.reciprocal(out=tkb[:, 154:155], in_=tkb[:, 153:154])
                        self.sep(v)
                        self.inc("v", v.tensor_scalar_mul(
                            self.g_all[:, tg * 64 + hf * 8: tg * 64 + hf * 8 + 8],
                            tkb[:, 160:168], tkb[:, 154:155]))
                    blk.vector(s_tk3)
                sim_guard[sslot] = self.n("v")
            self.tap(blk, "idx_all", self.idx_all[:, :])
            self.tap(blk, "g_all", self.g_all[:, :])
            self.tap(blk, "sim0", simb[:, 0:4096])

    def b5(self):
        nc = self.nc
        acc_vcount = {}
        with (
            nc.sbuf_tensor("gring", [128, NBUF * 2048], F32) as gring,
            nc.sbuf_tensor("xn5", [128, 2 * C], F32) as xn5,
            nc.sbuf_tensor("acc5", [128, 2 * C], F32) as acc5,
            nc.sbuf_tensor("scr5", [128, C], F32) as scr5,
            nc.sbuf_tensor("hb5", [128, 16], F32) as hb5,
            nc.Block("b5") as blk,
        ):
            xn_guard = [0, 0]
            acc_guard = [0, 0]
            for tg in range(NTILE):
                b, tt = tg // 16, tg % 16
                slot = tg % 2

                def s_ld(sy, tg=tg, b=b, tt=tt, slot=slot, gx=xn_guard[slot],
                         ga=acc_guard[slot]):
                    self.wt(sy, "v", gx)
                    self.inc("ld", sy.dma_start(
                        out=xn5[:, slot * C:(slot + 1) * C],
                        in_=self.x[b, tt * 128:(tt + 1) * 128, :]), 16)
                    self.wt(sy, "st", ga)
                    self.inc("ld", sy.dma_start(
                        out=acc5[:, slot * C:(slot + 1) * C],
                        in_=self.resid_s[b, tt * 128:(tt + 1) * 128, :]), 16)
                blk.sync(s_ld)

                def s_norm(v, tg=tg, slot=slot):
                    self.wt(v, "ld")
                    self.inc("v", v.tensor_scalar(
                        out=xn5[:, slot * C:(slot + 1) * C],
                        in0=xn5[:, slot * C:(slot + 1) * C],
                        scalar1=self.rstd_all[:, tg:tg + 1],
                        scalar2=self.nmr_all[:, tg:tg + 1],
                        op0=ALU.mult, op1=ALU.add))
                blk.vector(s_norm)

                for grp in range(16):
                    for i in range(4):
                        k = grp * 4 + i
                        gk = tg * 64 + k
                        buf = gk % NBUF

                        def s_g(g, gk=gk, buf=buf,
                                gv=acc_vcount.get(tg * 64 + grp * 4 + i - NBUF, 0)):
                            if gv:
                                self.wt(g, "v", gv)
                            self.inc("g", g.indirect_dma_start(
                                gring[:, buf * 2048:(buf + 1) * 2048], None,
                                self.tbl[:, :],
                                bass.IndirectOffsetOnAxis(
                                    self.idx_all[:, gk:gk + 1], 0)), 16)
                        blk.gpsimd(s_g)

                        def s_dot(v, gk=gk, buf=buf, i=i, slot=slot):
                            self.wt(v, "g")
                            self.inc("v", v.scalar_tensor_tensor(
                                out=scr5[:, :],
                                in0=gring[:, buf * 2048: buf * 2048 + 1024],
                                scalar=1.0,
                                in1=xn5[:, slot * C:(slot + 1) * C],
                                op0=ALU.bypass, op1=ALU.mult,
                                accum_out=hb5[:, i:i + 1]))
                        blk.vector(s_dot)

                    def s_gelu(sc, grp=grp):
                        self.wt(sc, "v")
                        self.inc("a", sc.activation(out=hb5[:, 4:8], in_=hb5[:, 0:4],
                                                    func=ACT.Gelu_apprx_tanh))
                    blk.scalar(s_gelu)

                    def s_w(v, tg=tg, grp=grp):
                        self.wt(v, "a")
                        self.inc("v", v.tensor_mul(
                            hb5[:, 8:12], hb5[:, 4:8],
                            self.g_all[:, tg * 64 + grp * 4: tg * 64 + grp * 4 + 4]))
                        self.sep(v)
                    blk.vector(s_w)

                    for i in range(4):
                        k = grp * 4 + i
                        gk = tg * 64 + k
                        buf = gk % NBUF

                        def s_acc(v, i=i, buf=buf, slot=slot):
                            self.inc("v", v.scalar_tensor_tensor(
                                out=acc5[:, slot * C:(slot + 1) * C],
                                in0=gring[:, buf * 2048 + 1024: (buf + 1) * 2048],
                                scalar=hb5[:, 8 + i:9 + i],
                                in1=acc5[:, slot * C:(slot + 1) * C],
                                op0=ALU.mult, op1=ALU.add))
                            self.sep(v)
                        blk.vector(s_acc)
                        acc_vcount[gk] = self.n("v")

                def s_out(sy, tg=tg, b=b, tt=tt, slot=slot):
                    self.wt(sy, "v")
                    self.inc("st", sy.dma_start(
                        out=self.out[b, tt * 128:(tt + 1) * 128, :],
                        in_=acc5[:, slot * C:(slot + 1) * C]), 16)
                blk.sync(s_out)
                xn_guard[slot] = self.n("v")
                acc_guard[slot] = self.n("st")

            def s_fin(sy):
                self.wt(sy, "st")
                self.wt(sy, "st2")
            blk.sync(s_fin)

    def build(self):
        nc = self.nc
        self.b0()
        if self.stop_after == "b0":
            return self._fin()
        es1 = contextlib.ExitStack()
        xnT = [es1.enter_context(nc.sbuf_tensor(f"xnTc{i}", [128, N], F32)) for i in range(DCH)]
        wqk = [es1.enter_context(nc.sbuf_tensor(f"wqk{i}", [128, 2 * C], F32)) for i in range(DCH)]
        with nc.Block("b1w") as blk:
            def s_w(sy):
                for dch in range(DCH):
                    self.inc("ld", sy.dma_start(out=wqk[dch][:, :],
                                                in_=self.wqkv[dch * 128:(dch + 1) * 128, 0:2 * C]), 16)
            blk.sync(s_w)
        for b in range(NB):
            self.b1(b, wqk, xnT)
        es1.close()
        if self.stop_after == "b1":
            return self._fin()
        self.b2()
        if self.stop_after == "b2":
            return self._fin()
        for b in range(NB):
            self.b3(b)
        if self.stop_after == "b3":
            return self._fin()
        self.b4()
        if self.stop_after == "b4":
            return self._fin()
        self.b5()
        return self._fin()

    def _fin(self):
        nc = self.nc
        if self.taps:
            with nc.Block("tapw") as blk:
                def f(sy):
                    self.wt(sy, "st2")
                blk.sync(f)
        return nc


def build_nc(n_cores=NCORES, taps=(), stop_after=None):
    mk = MK(n_cores=n_cores, taps=taps, stop_after=stop_after)
    mk.build()
    return mk


def make_in_map(x_shard, inputs):
    keys_r = np.ascontiguousarray(
        np.transpose(np.asarray(inputs["L0_ffn_keys"], dtype=np.float32), (3, 2, 0, 1))
    ).reshape(128, 16, NKEY)
    tbl = np.concatenate([
        np.asarray(inputs["L0_ffn_down_embed"], dtype=np.float32),
        np.asarray(inputs["L0_ffn_up_embed"], dtype=np.float32)], axis=1)
    return {
        "x": np.ascontiguousarray(x_shard),
        "wqkv": np.ascontiguousarray(np.asarray(inputs["L0_attn_W_qkv"], np.float32)),
        "bqkv": np.ascontiguousarray(np.asarray(inputs["L0_attn_b_qkv"], np.float32)),
        "wproj": np.ascontiguousarray(np.asarray(inputs["L0_attn_W_proj"], np.float32)),
        "bproj": np.ascontiguousarray(np.asarray(inputs["L0_attn_b_proj"], np.float32)),
        "wq_in": np.ascontiguousarray(np.asarray(inputs["L0_ffn_w_q"], np.float32)),
        "gamma": np.ascontiguousarray(np.asarray(inputs["L0_ffn_bn_gamma"], np.float32)),
        "beta": np.ascontiguousarray(np.asarray(inputs["L0_ffn_bn_beta"], np.float32)),
        "keys": keys_r,
        "tbl": np.ascontiguousarray(tbl),
    }


def kernel(**inputs):
    x = np.asarray(inputs["L0_attn_x"], dtype=np.float32)
    mk = MK(n_cores=NCORES)
    mk.build()
    base = make_in_map(x[0:2], inputs)
    in_maps = []
    for c in range(NCORES):
        m = dict(base)
        m["x"] = np.ascontiguousarray(x[2 * c:2 * c + 2])
        in_maps.append(m)
    res = run_bass_kernel_spmd(mk.nc, in_maps, core_ids=list(range(NCORES)))
    out = np.concatenate([r["out"] for r in res.results], axis=0)
    return out.astype(np.float32)


# version 04
# speedup vs baseline: 1.1115x; 1.1115x over previous
import contextlib
import numpy as np
import concourse.bass as bass
import concourse.mybir as mybir
from concourse.bass_utils import run_bass_kernel_spmd

F32 = mybir.dt.float32
I32 = mybir.dt.int32
U32 = mybir.dt.uint32
AX = mybir.AxisListType
ALU = mybir.AluOpType
ACT = mybir.ActivationFunctionType

NCORES = 8
NB = 2
N = 2048
C = 1024
H = 16
D = 64
S = 32
P = 64
K = 4
FC = 8
TK = 19
NSUB = 5
SCALE = D ** -0.5
LN_EPS = 1e-5
NKEY = 256
NEXP = 65536
BN_EPS = 1e-5
NTOK = NB * N
NTILE = NTOK // 128
DCH = C // 128
NSEL = S * TK
NBUF = 8


def _kv_ranges(ci):
    out = []
    for ncl in range(K):
        nf = FC if (ncl == 0 or ncl == ci) else NSUB
        out.append((ncl * FC * TK, nf * TK))
    return out


def _range_tiles(ci):
    out = []
    for r, (off, ln) in enumerate(_kv_ranges(ci)):
        c0 = 0
        bi = 0
        while c0 < ln:
            nr = min(128, ln - c0)
            out.append((r, bi, off + c0, nr))
            c0 += nr
            bi += 1
    return out


def _asm_pieces(sel_off, nrows):
    out = []
    done = 0
    while done < nrows:
        g = sel_off + done
        nr = min(nrows - done, 128 - (g % 128))
        out.append((g // 128, g % 128, done, nr))
        done += nr
    return out


class MK:
    def __init__(self, n_cores=NCORES, taps=(), stop_after=None, dbg_dram=False):
        self.nc = bass.Bass("TRN2", target_bir_lowering=False,
                            detect_race_conditions=False)
        self.n_cores = n_cores
        self.tap_names = set(taps)
        self.taps = {}
        self.stop_after = stop_after
        nc = self.nc
        self.x = nc.dram_tensor("x", [NB, N, C], F32, kind="ExternalInput").ap()
        self.wqkv = nc.dram_tensor("wqkv", [C, 3 * C], F32, kind="ExternalInput").ap()
        self.bqkv = nc.dram_tensor("bqkv", [3 * C], F32, kind="ExternalInput").ap()
        self.wproj = nc.dram_tensor("wproj", [C, C], F32, kind="ExternalInput").ap()
        self.bproj = nc.dram_tensor("bproj", [C], F32, kind="ExternalInput").ap()
        self.wq_in = nc.dram_tensor("wq_in", [C, 2 * C], F32, kind="ExternalInput").ap()
        self.gamma = nc.dram_tensor("gamma", [2 * C], F32, kind="ExternalInput").ap()
        self.beta = nc.dram_tensor("beta", [2 * C], F32, kind="ExternalInput").ap()
        self.keys = nc.dram_tensor("keys", [128, 16, NKEY], F32, kind="ExternalInput").ap()
        self.tbl = nc.dram_tensor("tbl", [NEXP, 2048], F32, kind="ExternalInput").ap()
        self.out = nc.dram_tensor("out", [NB, N, C], F32, kind="ExternalOutput").ap()
        dbk = dict(kind="ExternalOutput") if dbg_dram else {}
        self.xnT_s = nc.dram_tensor("xnT_s", [NB, DCH, 128, N], F32, **dbk).ap()
        self.q_s = nc.dram_tensor("q_s", [16, 128, NTOK], F32, **dbk).ap()
        self.resid_s = nc.dram_tensor("resid_s", [NB, N, C], F32, **dbk).ap()
        self.cc_in = nc.dram_tensor("cc_in", [128, 36], F32).ap()
        self.cc_out = nc.dram_tensor("cc_out", [128, 36], F32).ap()

        A = nc.alloc_sbuf_tensor
        self.eye = A("eye", [128, 128], F32)
        self.iota19 = A("iota19", [64, TK], F32)
        self.iota8 = A("iota8", [128, 8], F32)
        self.bqkv24 = A("bqkv24", [128, 24], F32)
        self.bq_rep = None
        self.bk_rep = None
        self.bv_rep = None
        self.bproj_rep = None
        self.gamma16 = A("gamma16", [128, 16], F32)
        self.beta16 = A("beta16", [128, 16], F32)
        self.keys_sb = A("keys_sb", [128, 16 * NKEY], F32)
        self.rstd_all = A("rstd_all", [128, NTILE], F32)
        self.nmr_all = A("nmr_all", [128, NTILE], F32)
        self.bnacc = A("bnacc", [128, 128], F32)
        self.bnacc2 = A("bnacc2", [128, 128], F32)
        self.bnstat = A("bnstat", [128, 36], F32)
        self.allr = A("allr", [128, 36], F32)
        self.bn_a = A("bn_a", [128, 16], F32)
        self.bn_b = A("bn_b", [128, 16], F32)
        self.sel2 = A("sel2", [128, K * TK], F32)
        self.idx_all = A("idx_all", [128, NTILE * 64], I32)
        self.g_all = A("g_all", [128, NTILE * 64], F32)

        self.sems = {}
        for s in ["ld", "ld2", "g", "st", "st2", "mm", "v", "a", "gp", "cc", "fr"]:
            self.sems[s] = [nc.alloc_semaphore(s), 0]

    def inc(self, name, instr, k=1):
        st = self.sems[name]
        instr.then_inc(st[0], k)
        st[1] += k
        return st[1]

    def n(self, name):
        return self.sems[name][1]

    def wt(self, eng, name, val=None):
        st = self.sems[name]
        eng.wait_ge(st[0], st[1] if val is None else val)

    def sep(self, eng):
        eng.drain()

    def tap(self, blk, name, src_ap):
        if name in self.tap_names and name not in self.taps:
            t = self.nc.dram_tensor("tap_" + name, list(src_ap.shape),
                                    src_ap.dtype, kind="ExternalOutput").ap()
            self.taps[name] = t

            def f(sy, t=t, src=src_ap):
                for s in ("ld", "ld2", "v", "a", "mm", "g"):
                    self.wt(sy, s)
                self.inc("st2", sy.dma_start(
                    out=t[tuple([slice(None)] * len(t.shape))], in_=src), 16)
            blk.scalar(f)

    def b0(self):
        nc = self.nc
        with (
            nc.sbuf_tensor("c_i32", [128, 128], I32) as c_i32,
            nc.sbuf_tensor("c_m24", [24, 128], F32) as c_m24,
            nc.sbuf_tensor("c_g16", [16, 128], F32) as c_g16,
            nc.sbuf_tensor("c_b16", [16, 128], F32) as c_b16,
            nc.psum_tensor("p_c", [128, 128], F32) as p_c,
            nc.psum_tensor("p_c2", [128, 16], F32) as p_c2,
            nc.psum_tensor("p_c3", [128, 16], F32) as p_c3,
            nc.Block("b0") as blk,
        ):
            def s_load(sy):
                self.inc("ld", sy.dma_start(out=c_m24[:, :], in_=self.bqkv.rearrange("(a b) -> a b", a=24)), 16)
                self.inc("ld", sy.dma_start(out=c_g16[:, :], in_=self.gamma.rearrange("(a b) -> a b", a=16)), 16)
                self.inc("ld", sy.dma_start(out=c_b16[:, :], in_=self.beta.rearrange("(a b) -> a b", a=16)), 16)
                self.inc("ld", sy.dma_start(out=self.keys_sb[:, :], in_=self.keys.rearrange("p a k -> p (a k)")), 16)
            blk.sync(s_load)

            def s_iota(g):
                g.iota(c_i32[:, :], pattern=[[1, 128]], base=0, channel_multiplier=-1)
                g.iota(self.iota19[:, :], pattern=[[1, TK]], base=1,
                       channel_multiplier=0, allow_small_or_imprecise_dtypes=True)
                self.inc("gp", g.iota(self.iota8[:, :], pattern=[[1, 8]], base=0,
                                      channel_multiplier=0,
                                      allow_small_or_imprecise_dtypes=True))
            blk.gpsimd(s_iota)

            def s_eye(v):
                self.wt(v, "gp")
                v.tensor_scalar(out=self.eye[:, :], in0=c_i32[:, :], scalar1=0,
                                scalar2=None, op0=ALU.is_equal)
                v.memset(self.bnstat[:, :], 0.0)
                v.memset(self.bnacc[:, :], 0.0)
                self.inc("v", v.memset(self.bnacc2[:, :], 0.0))
            blk.vector(s_eye)

            def s_tr(t):
                self.wt(t, "ld")
                self.wt(t, "v")
                t.transpose(p_c[:, 0:24], c_m24[:, :], self.eye[0:24, 0:24])
                t.transpose(p_c2[:, 0:16], c_g16[:, :], self.eye[0:16, 0:16])
                self.inc("mm", t.transpose(p_c3[:, 0:16], c_b16[:, :], self.eye[0:16, 0:16]))
            blk.tensor(s_tr)

            def s_ev(sc):
                self.wt(sc, "mm")
                sc.activation(out=self.bqkv24[:, :], in_=p_c[:, 0:24], func=ACT.Copy)
                sc.activation(out=self.gamma16[:, :], in_=p_c2[:, 0:16], func=ACT.Copy)
                self.inc("a", sc.activation(out=self.beta16[:, :], in_=p_c3[:, 0:16], func=ACT.Copy))
            blk.scalar(s_ev)

    def b1(self, b, wqk, xnT):
        nc = self.nc
        with (
            nc.sbuf_tensor(f"xnr{b}", [128, 2 * C], F32) as xnr,
            nc.sbuf_tensor(f"lnt{b}", [128, 8], F32) as lnt,
            nc.sbuf_tensor(f"sq_scr{b}", [128, C], F32) as sq_scr,
            nc.psum_tensor(f"ptA{b}", [128, 128], F32) as ptA,
            nc.psum_tensor(f"ptB{b}", [128, 128], F32) as ptB,
            nc.Block(f"b1_ln{b}") as blk,
        ):
            slot_guard = [0, 0]
            bank_guard = [0, 0]
            for tt in range(16):
                col = b * 16 + tt
                slot = tt % 2
                xt = xnr[:, slot * C:(slot + 1) * C]

                def s_load(sy, xt=xt, tt=tt, g=slot_guard[slot]):
                    self.wt(sy, "mm", g)
                    self.inc("ld", sy.dma_start(out=xt, in_=self.x[b, tt * 128:(tt + 1) * 128, :]), 16)
                blk.sync(s_load)

                def s_sums(sc, xt=xt):
                    self.wt(sc, "ld")
                    sc.activation(out=sq_scr[:, :], in_=xt, func=ACT.Copy,
                                  accum_out=lnt[:, 0:1])
                    self.inc("a", sc.activation(out=sq_scr[:, :], in_=xt, func=ACT.Square,
                                                accum_out=lnt[:, 1:2]))
                blk.scalar(s_sums)

                def s_stats(v):
                    self.wt(v, "a")
                    v.tensor_scalar_mul(lnt[:, 2:3], lnt[:, 0:1], 1.0 / C)
                    v.tensor_scalar_mul(lnt[:, 3:4], lnt[:, 1:2], 1.0 / C)
                    v.tensor_mul(lnt[:, 4:5], lnt[:, 2:3], lnt[:, 2:3])
                    self.sep(v)
                    v.tensor_sub(lnt[:, 5:6], lnt[:, 3:4], lnt[:, 4:5])
                    self.sep(v)
                    self.inc("v", v.tensor_scalar_add(lnt[:, 5:6], lnt[:, 5:6], LN_EPS))
                blk.vector(s_stats)

                def s_rstd(sc, col=col):
                    self.wt(sc, "v")
                    self.inc("a", sc.activation(out=lnt[:, 6:7],
                                                in_=lnt[:, 5:6], func=ACT.Sqrt))
                blk.scalar(s_rstd)

                def s_nmr(v, col=col):
                    self.wt(v, "a")
                    v.reciprocal(out=self.rstd_all[:, col:col + 1], in_=lnt[:, 6:7])
                    self.sep(v)
                    self.inc("v", v.scalar_tensor_tensor(
                        out=self.nmr_all[:, col:col + 1], in0=lnt[:, 2:3], scalar=-1.0,
                        in1=self.rstd_all[:, col:col + 1], op0=ALU.mult, op1=ALU.mult))
                blk.vector(s_nmr)

                def s_norm(v, xt=xt, col=col):
                    self.sep(v)
                    self.inc("v", v.tensor_scalar(
                        out=xt, in0=xt, scalar1=self.rstd_all[:, col:col + 1],
                        scalar2=self.nmr_all[:, col:col + 1],
                        op0=ALU.mult, op1=ALU.add))
                blk.vector(s_norm)

                for dch in range(DCH):
                    pbank = (tt * DCH + dch) % 2
                    ps = ptA if pbank == 0 else ptB

                    def s_tp(t, xt=xt, dch=dch, ps=ps, g=bank_guard[pbank]):
                        self.wt(t, "v")
                        self.wt(t, "a", g)
                        self.inc("mm", t.transpose(
                            ps[:, :], xt[:, dch * 128:(dch + 1) * 128],
                            self.eye[:, :]))
                    blk.tensor(s_tp)

                    def s_ev(sc, tt=tt, dch=dch, ps=ps):
                        self.wt(sc, "mm")
                        self.inc("a", sc.activation(
                            out=xnT[dch][:, tt * 128:(tt + 1) * 128],
                            in_=ps[:, :], func=ACT.Copy))
                    blk.scalar(s_ev)
                    bank_guard[pbank] = self.n("a")
                slot_guard[slot] = self.n("mm")

            def s_spill(sy):
                self.wt(sy, "a")
                for dch in range(DCH):
                    self.inc("st", sy.dma_start(out=self.xnT_s[b, dch, :, :],
                                                in_=xnT[dch][:, :]), 16)
            blk.sync(s_spill)

        with (
            nc.sbuf_tensor(f"kfl{b}", [128, 16 * 128], F32) as kfl,
            nc.sbuf_tensor(f"bqr{b}", [128, C], F32) as bqr,
            nc.sbuf_tensor(f"bkr{b}", [128, C], F32) as bkr,
            nc.sbuf_tensor(f"qkf{b}", [128, C], F32) as qkf,
            nc.sbuf_tensor(f"kkf{b}", [128, C], F32) as kkf,
            nc.sbuf_tensor(f"dotb{b}", [128, C + 32], F32) as dotb,
            nc.psum_tensor(f"pqA{b}", [128, 512], F32) as pqA,
            nc.psum_tensor(f"pqB{b}", [128, 512], F32) as pqB,
            nc.Block(f"b1_kf{b}") as blk,
        ):
            def s_brep(sy):
                self.inc("ld", sy.dma_start(out=bqr[:, :], in_=self.bqkv[0:C].unsqueeze(0).to_broadcast([128, C])), 16)
                self.inc("ld", sy.dma_start(out=bkr[:, :], in_=self.bqkv[C:2 * C].unsqueeze(0).to_broadcast([128, C])), 16)
            blk.sync(s_brep)

            def s_kfl(v):
                for kft in range(2):
                    for dch in range(DCH):
                        lhs = xnT[dch][:, :].rearrange(
                            "p (g r) -> p g r", g=4)[:, kft * 2:(kft + 1) * 2, 0:64]
                        v.tensor_copy(
                            out=kfl[:, (kft * DCH + dch) * 128:(kft * DCH + dch + 1) * 128],
                            in_=lhs)
                self.inc("v", v.tensor_copy(out=kfl[:, 0:1], in_=kfl[:, 0:1]))
            blk.vector(s_kfl)
            for kft in range(2):
                for qk in range(2):
                    dst = qkf if qk == 0 else kkf
                    rep = bqr if qk == 0 else bkr
                    for half in range(2):
                        ps = pqA if half == 0 else pqB

                        def s_mm(t, kft=kft, qk=qk, half=half, ps=ps):
                            self.wt(t, "ld")
                            self.wt(t, "v")
                            for dch in range(DCH):
                                lhs = kfl[:, (kft * DCH + dch) * 128:(kft * DCH + dch + 1) * 128]
                                self.inc("mm", t.matmul(
                                    ps[:, :], lhs,
                                    wqk[dch][:, qk * C + half * 512: qk * C + (half + 1) * 512],
                                    start=(dch == 0), stop=(dch == DCH - 1)))
                        blk.tensor(s_mm)

                        def s_ev(v, half=half, ps=ps, dst=dst, rep=rep):
                            self.wt(v, "mm")
                            self.inc("v", v.tensor_add(
                                dst[:, half * 512:(half + 1) * 512], ps[:, :],
                                rep[:, half * 512:(half + 1) * 512]))
                        blk.vector(s_ev)

                def s_dot(v, kft=kft):
                    self.sep(v)
                    v.tensor_mul(dotb[:, 0:C], qkf[:, :], kkf[:, :])
                    self.sep(v)
                    v.reduce_sum(out=dotb[:, C:C + 16],
                                 in_=dotb[:, 0:C].rearrange("p (h d) -> p h d", h=H),
                                 axis=AX.X)
                    self.sep(v)
                    v.reduce_max(out=dotb[:, C + 16:C + 17], in_=dotb[:, C:C + 16],
                                 axis=AX.X)
                    self.sep(v)
                    self.inc("v", v.tensor_add(self.bnstat[:, 32 + kft:33 + kft],
                                               self.bnstat[:, 32 + kft:33 + kft],
                                               dotb[:, C + 16:C + 17]))
                blk.vector(s_dot)
            self.tap(blk, f"qkf_{b}", qkf[:, :])
            self.tap(blk, f"kkf_{b}", kkf[:, :])
            self.tap(blk, f"dotb_{b}", dotb[:, :])

        with (
            nc.sbuf_tensor(f"wqjc{b}", [128, 2 * DCH * 128], F32) as wqjc,
            nc.sbuf_tensor(f"qev{b}", [128, 2 * 512], F32) as qev,
            nc.sbuf_tensor(f"qsq{b}", [128, 512], F32) as qsq,
            nc.psum_tensor(f"pfA{b}", [128, 512], F32) as pfA,
            nc.psum_tensor(f"pfB{b}", [128, 512], F32) as pfB,
            nc.Block(f"b1_q{b}") as blk,
        ):
            ev_guard = [0, 0]
            w_guard = [0, 0]
            for jc in range(16):
                wslot = jc % 2

                def s_wload(sy, jc=jc, wslot=wslot, g=w_guard[wslot]):
                    self.wt(sy, "mm", g)
                    src = self.wq_in.rearrange("(dc p) (jcc j) -> p dc jcc j",
                                               dc=DCH, jcc=16)[:, :, jc, :]
                    self.inc("ld", sy.dma_start(
                        out=wqjc[:, wslot * DCH * 128:(wslot + 1) * DCH * 128],
                        in_=src), 16)
                blk.sync(s_wload)

                for t4 in range(4):
                    ps = pfA if t4 % 2 == 0 else pfB
                    eslot = t4 % 2

                    def s_mm(t, jc=jc, t4=t4, ps=ps, wslot=wslot):
                        self.wt(t, "ld")
                        self.wt(t, "a")
                        for dch in range(DCH):
                            self.inc("mm", t.matmul(
                                ps[:, :],
                                wqjc[:, wslot * DCH * 128 + dch * 128:
                                     wslot * DCH * 128 + (dch + 1) * 128],
                                xnT[dch][:, t4 * 512:(t4 + 1) * 512],
                                start=(dch == 0), stop=(dch == DCH - 1)))
                    blk.tensor(s_mm)

                    def s_ev(sc, jc=jc, t4=t4, ps=ps, eslot=eslot, g=ev_guard[eslot]):
                        self.wt(sc, "mm")
                        self.wt(sc, "st", g)
                        acol = jc * 8 + b * 4 + t4
                        sc.activation(out=qev[:, eslot * 512:(eslot + 1) * 512],
                                      in_=ps[:, :], func=ACT.Copy,
                                      accum_out=self.bnacc[:, acol:acol + 1])
                        self.inc("a", sc.activation(
                            out=qsq[:, :], in_=qev[:, eslot * 512:(eslot + 1) * 512],
                            func=ACT.Square, accum_out=self.bnacc2[:, acol:acol + 1]))
                    blk.scalar(s_ev)

                    def s_spill(sy, jc=jc, t4=t4, eslot=eslot):
                        self.wt(sy, "a")
                        self.inc("st", sy.dma_start(
                            out=self.q_s[jc, :, b * N + t4 * 512: b * N + (t4 + 1) * 512],
                            in_=qev[:, eslot * 512:(eslot + 1) * 512]), 16)
                    blk.sync(s_spill)
                    ev_guard[eslot] = self.n("st")
                w_guard[wslot] = self.n("mm")

    def b2(self):
        nc = self.nc
        with (
            nc.sbuf_tensor("sc2", [2, 128], F32) as sc2,
            nc.sbuf_tensor("sc4", [4, 64], F32) as sc4,
            nc.sbuf_tensor("thr", [4, 80], F32) as thr,
            nc.sbuf_tensor("csA", [4, 64], F32) as csA,
            nc.sbuf_tensor("csB", [4, 64], F32) as csB,
            nc.sbuf_tensor("selT", [64, 8], F32) as selT,
            nc.sbuf_tensor("b2t", [128, 16], F32) as b2t,
            nc.psum_tensor("p_b2", [128, 128], F32) as p_b2,
            nc.psum_tensor("p_b2b", [128, 16], F32) as p_b2b,
            nc.Block("b2") as blk,
        ):
            def s_red(v):
                v.reduce_sum(out=self.bnstat[:, 0:16],
                             in_=self.bnacc[:, :].rearrange("p (jc t) -> p jc t", jc=16),
                             axis=AX.X)
                self.inc("v", v.reduce_sum(
                    out=self.bnstat[:, 16:32],
                    in_=self.bnacc2[:, :].rearrange("p (jc t) -> p jc t", jc=16),
                    axis=AX.X))
            blk.vector(s_red)

            def s_cc0(sy):
                self.wt(sy, "v")
                self.inc("st", sy.dma_start(out=self.cc_in[:, :], in_=self.bnstat[:, :]), 16)
            blk.sync(s_cc0)

            def s_cc(g):
                self.wt(g, "st")
                self.inc("cc", g.collective_compute(
                    "AllReduce", ALU.add,
                    replica_groups=[list(range(self.n_cores))],
                    ins=[self.cc_in[:, :]], outs=[self.cc_out[:, :]]))
            blk.gpsimd(s_cc)

            def s_cc1(sy):
                self.wt(sy, "cc")
                self.inc("ld", sy.dma_start(out=self.allr[:, :], in_=self.cc_out[:, :]), 16)
            blk.sync(s_cc1)

            def s_bn(v):
                self.wt(v, "ld")
                mean = b2t[:, 0:16]
                v.tensor_scalar_mul(mean, self.allr[:, 0:16], 1.0 / (NTOK * self.n_cores))
                v.tensor_scalar_mul(self.bn_b[:, :], self.allr[:, 16:32], 1.0 / (NTOK * self.n_cores))
                v.tensor_mul(self.bn_a[:, :], mean, mean)
                self.sep(v)
                v.tensor_sub(self.bn_b[:, :], self.bn_b[:, :], self.bn_a[:, :])
                self.sep(v)
                self.inc("v", v.tensor_scalar_add(self.bn_b[:, :], self.bn_b[:, :], BN_EPS))
            blk.vector(s_bn)

            def s_bn2(sc):
                self.wt(sc, "v")
                self.inc("a", sc.activation(out=self.bn_a[:, :], in_=self.bn_b[:, :],
                                            func=ACT.Sqrt))
            blk.scalar(s_bn2)

            def s_bn3(v):
                self.wt(v, "a")
                v.reciprocal(out=self.bn_a[:, :], in_=self.bn_a[:, :])
                self.sep(v)
                v.tensor_mul(self.bn_a[:, :], self.bn_a[:, :], self.gamma16[:, :])
                self.sep(v)
                v.tensor_mul(self.bn_b[:, :], b2t[:, 0:16], self.bn_a[:, :])
                self.sep(v)
                self.inc("v", v.tensor_sub(self.bn_b[:, :], self.beta16[:, :], self.bn_b[:, :]))
            blk.vector(s_bn3)

            def s_tr(t):
                self.wt(t, "v")
                self.inc("mm", t.transpose(p_b2[0:2, 0:128],
                                           self.allr[:, 32:34], self.eye[:, :]))
            blk.tensor(s_tr)

            def s_ev(sc):
                self.wt(sc, "mm")
                self.inc("a", sc.activation(out=sc2[:, :], in_=p_b2[0:2, 0:128],
                                            func=ACT.Copy))
            blk.scalar(s_ev)

            def s_mv(sy):
                self.wt(sy, "a")
                for kf in range(4):
                    self.inc("ld", sy.dma_start(
                        out=sc4[kf:kf + 1, :],
                        in_=sc2[kf // 2:kf // 2 + 1, (kf % 2) * 64:(kf % 2) * 64 + 64]), 16)
            blk.sync(s_mv)

            def s_thr(v):
                self.wt(v, "ld")
                v.max(out=thr[:, 0:8], in_=sc4[:, :])
                self.sep(v)
                v.match_replace(out=thr[:, 16:80], in_to_replace=thr[:, 0:8],
                                in_values=sc4[:, :], imm_value=-1e30)
                self.sep(v)
                v.max(out=thr[:, 8:16], in_=thr[:, 16:80])
                self.sep(v)
                v.match_replace(out=thr[:, 16:80], in_to_replace=thr[:, 8:16],
                                in_values=thr[:, 16:80], imm_value=-1e30)
                self.sep(v)
                v.max(out=thr[:, 0:8], in_=thr[:, 16:80])
                self.sep(v)
                v.tensor_scalar(out=csA[:, :], in0=sc4[:, :], scalar1=thr[:, 2:3],
                                scalar2=None, op0=ALU.is_ge)
                self.sep(v)
                self.inc("v", v.tensor_copy(out=csB[:, :], in_=csA[:, :]))
            blk.vector(s_thr)

            def s_cum(v):
                a, bb = csB, thr[:, 16:80]
                for sh in [1, 2, 4, 8, 16, 32]:
                    self.sep(v)
                    v.tensor_copy(out=bb[:, 0:sh], in_=a[:, 0:sh])
                    v.tensor_add(bb[:, sh:64], a[:, sh:64], a[:, 0:64 - sh])
                    a, bb = bb, a
                self.sep(v)
                self.inc("v", v.tensor_copy(out=csB[:, :], in_=a[:, :]))
            blk.vector(s_cum)

            def s_tr2(t):
                self.wt(t, "v")
                t.transpose(p_b2[0:64, 4:8], csB[:, :], self.eye[0:4, 0:4])
                self.inc("mm", t.transpose(p_b2b[0:64, 0:4], csA[:, :], self.eye[0:4, 0:4]))
            blk.tensor(s_tr2)

            def s_ev2(sc):
                self.wt(sc, "mm")
                sc.activation(out=selT[:, 0:4], in_=p_b2[0:64, 4:8], func=ACT.Copy)
                self.inc("a", sc.activation(out=selT[:, 4:8], in_=p_b2b[0:64, 0:4],
                                            func=ACT.Copy))
            blk.scalar(s_ev2)

            def s_sel(v):
                self.wt(v, "a")
                for ncl in range(K):
                    v.tensor_tensor(
                        out=self.sel2[0:64, ncl * TK:(ncl + 1) * TK],
                        in0=selT[:, ncl:ncl + 1].to_broadcast([64, TK]),
                        in1=self.iota19[:, :], op=ALU.is_equal)
                    self.sep(v)
                    v.tensor_scalar_mul(self.sel2[0:64, ncl * TK:(ncl + 1) * TK],
                                        self.sel2[0:64, ncl * TK:(ncl + 1) * TK],
                                        selT[:, 4 + ncl:5 + ncl])
                self.inc("v", v.tensor_copy(out=csA[:, :], in_=csA[:, :]))
            blk.vector(s_sel)

            def s_dup(sy):
                self.wt(sy, "v")
                self.inc("ld", sy.dma_start(out=self.sel2[64:128, :],
                                            in_=self.sel2[0:64, :]), 16)
            blk.sync(s_dup)
            self.tap(blk, "gamma16", self.gamma16[:, :])
            self.tap(blk, "beta16", self.beta16[:, :])
            self.tap(blk, "bqkv24", self.bqkv24[:, :])
            self.tap(blk, "sel2", self.sel2[:, :])
            self.tap(blk, "bn_a", self.bn_a[:, :])
            self.tap(blk, "bn_b", self.bn_b[:, :])
            self.tap(blk, "sc4", sc4[:, :])
            self.tap(blk, "csA", csA[:, :])
            self.tap(blk, "csB", csB[:, :])
            self.tap(blk, "selT", selT[:, :])
            self.tap(blk, "allr", self.allr[:, :])
            self.tap(blk, "b2t", b2t[:, :])
            self.tap(blk, "thr", thr[:, :])

    def b3(self, b):
        nc = self.nc
        es = contextlib.ExitStack()
        klocT = [es.enter_context(nc.sbuf_tensor(f"klocT{b}_{i}", [128, NSEL], F32)) for i in range(DCH)]
        vloc = [es.enter_context(nc.sbuf_tensor(f"vloc{b}_{i}", [128, C], F32)) for i in range(5)]
        bvr = es.enter_context(nc.sbuf_tensor(f"bvr{b}", [128, C], F32))
        bpr = es.enter_context(nc.sbuf_tensor(f"bpr{b}", [128, C], F32))
        es2 = contextlib.ExitStack()
        wbuf = [es2.enter_context(nc.sbuf_tensor(f"wb{b}_{i}", [128, C], F32)) for i in range(DCH)]
        xselT = [es2.enter_context(nc.sbuf_tensor(f"xselT{b}_{i}", [128, NSEL], F32)) for i in range(DCH)]
        with nc.Block(f"b3pre{b}") as blk:
            def s_brep(sy):
                self.inc("ld", sy.dma_start(out=bvr[:, :], in_=self.bqkv[2 * C:3 * C].unsqueeze(0).to_broadcast([128, C])), 16)
                self.inc("ld", sy.dma_start(out=bpr[:, :], in_=self.bproj[0:C].unsqueeze(0).to_broadcast([128, C])), 16)
            blk.sync(s_brep)

        with (
            nc.sbuf_tensor(f"xc{b}", [128, 2 * C], F32) as xc,
            nc.sbuf_tensor(f"xsf{b}", [32, 2 * C], F32) as xsf,
            nc.psum_tensor(f"pcmp{b}", [32, C], F32) as pcmp,
            nc.psum_tensor(f"ptrA{b}", [128, 32], F32) as ptrA,
            nc.psum_tensor(f"ptrB{b}", [128, 32], F32) as ptrB,
            nc.Block(f"b3s{b}") as blk,
        ):
            xc_guard = [0, 0]
            xs_guard = [0, 0]
            tr_guard = [0, 0]
            for tt in range(16):
                slot = tt % 2
                xt = xc[:, slot * C:(slot + 1) * C]

                def s_load(sy, tt=tt, xt=xt, g=xc_guard[slot]):
                    self.wt(sy, "mm", g)
                    self.inc("ld", sy.dma_start(out=xt, in_=self.x[b, tt * 128:(tt + 1) * 128, :]), 16)
                blk.sync(s_load)

                def s_norm(v, tt=tt, xt=xt):
                    self.wt(v, "ld")
                    col = b * 16 + tt
                    self.inc("v", v.tensor_scalar(
                        out=xt, in0=xt, scalar1=self.rstd_all[:, col:col + 1],
                        scalar2=self.nmr_all[:, col:col + 1],
                        op0=ALU.mult, op1=ALU.add))
                blk.vector(s_norm)

                for fr in range(2):
                    s = 2 * tt + fr
                    ncl = s // FC
                    fslot = s % 2

                    def s_cmp(t, xt=xt, fr=fr, ncl=ncl, fslot=fslot, g=xs_guard[fslot]):
                        self.wt(t, "v")
                        self.wt(t, "a", g)
                        for hcol in range(2):
                            self.inc("mm", t.matmul(
                                pcmp[0:TK, hcol * 512:(hcol + 1) * 512],
                                self.sel2[fr * 64:(fr + 1) * 64, ncl * TK:(ncl + 1) * TK],
                                xt[fr * 64:(fr + 1) * 64, hcol * 512:(hcol + 1) * 512],
                                start=True, stop=True))
                    blk.tensor(s_cmp)

                    def s_evc(sc, fslot=fslot):
                        self.wt(sc, "mm")
                        self.inc("a", sc.activation(
                            out=xsf[0:TK, fslot * C:(fslot + 1) * C],
                            in_=pcmp[0:TK, :], func=ACT.Copy))
                    blk.scalar(s_evc)

                    for dch in range(DCH):
                        tslot = dch % 2
                        ptr = ptrA if tslot == 0 else ptrB

                        def s_tp(t, dch=dch, fslot=fslot, ptr=ptr, g=tr_guard[tslot]):
                            self.wt(t, "a")
                            self.wt(t, "a", g)
                            self.inc("mm", t.transpose(
                                ptr[:, 0:TK],
                                xsf[0:TK, fslot * C + dch * 128: fslot * C + (dch + 1) * 128],
                                self.eye[0:TK, 0:TK]))
                        blk.tensor(s_tp)

                        def s_evt(sc, s=s, dch=dch, ptr=ptr):
                            self.wt(sc, "mm")
                            self.inc("a", sc.activation(
                                out=xselT[dch][:, s * TK:(s + 1) * TK],
                                in_=ptr[:, 0:TK], func=ACT.Copy))
                        blk.scalar(s_evt)
                        tr_guard[tslot] = self.n("a")
                    xs_guard[fslot] = self.n("a")
                xc_guard[slot] = self.n("mm")
            self.tap(blk, f"xselT0_{b}", xselT[0][:, :])

        with nc.Block(f"b3k{b}") as blk:
            def s_wk(sy):
                for dch in range(DCH):
                    self.inc("ld", sy.dma_start(out=wbuf[dch][:, :],
                                                in_=self.wqkv[dch * 128:(dch + 1) * 128, C:2 * C]), 16)
            blk.sync(s_wk)
        with (
            nc.psum_tensor(f"pk1{b}", [128, 512], F32) as pk1,
            nc.psum_tensor(f"pk2{b}", [128, 512], F32) as pk2,
            nc.Block(f"b3k2{b}") as blk,
        ):
            for hdc in range(DCH):
                for cpart, (c0, cl) in enumerate([(0, 512), (512, NSEL - 512)]):
                    ps = pk1 if cpart == 0 else pk2

                    def s_mm(t, hdc=hdc, c0=c0, cl=cl, ps=ps):
                        self.wt(t, "ld")
                        self.wt(t, "a")
                        for dch in range(DCH):
                            self.inc("mm", t.matmul(
                                ps[:, 0:cl], wbuf[dch][:, hdc * 128:(hdc + 1) * 128],
                                xselT[dch][:, c0:c0 + cl],
                                start=(dch == 0), stop=(dch == DCH - 1)))
                    blk.tensor(s_mm)

                    def s_ev(sc, hdc=hdc, c0=c0, cl=cl, ps=ps):
                        self.wt(sc, "mm")
                        self.inc("a", sc.activation(
                            out=klocT[hdc][:, c0:c0 + cl], in_=ps[:, 0:cl],
                            func=ACT.Identity, bias=self.bqkv24[:, 8 + hdc:9 + hdc]))
                    blk.scalar(s_ev)
        with nc.Block(f"b3v{b}") as blk:
            def s_wv(sy):
                self.wt(sy, "a")
                for dch in range(DCH):
                    self.inc("ld", sy.dma_start(out=wbuf[dch][:, :],
                                                in_=self.wqkv[dch * 128:(dch + 1) * 128, 2 * C:3 * C]), 16)
            blk.sync(s_wv)
        with (
            nc.psum_tensor(f"pv1{b}", [128, 512], F32) as pv1,
            nc.psum_tensor(f"pv2{b}", [128, 512], F32) as pv2,
            nc.Block(f"b3v2{b}") as blk,
        ):
            for st_ in range(5):
                rows = min(128, NSEL - st_ * 128)
                for half in range(2):
                    ps = pv1 if half == 0 else pv2

                    def s_mm(t, st_=st_, rows=rows, half=half, ps=ps):
                        self.wt(t, "ld")
                        self.wt(t, "v")
                        for dch in range(DCH):
                            self.inc("mm", t.matmul(
                                ps[0:rows, :], xselT[dch][:, st_ * 128:st_ * 128 + rows],
                                wbuf[dch][:, half * 512:(half + 1) * 512],
                                start=(dch == 0), stop=(dch == DCH - 1)))
                    blk.tensor(s_mm)

                    def s_ev(v, st_=st_, rows=rows, half=half, ps=ps):
                        self.wt(v, "mm")
                        self.inc("v", v.tensor_add(
                            vloc[st_][0:rows, half * 512:(half + 1) * 512],
                            ps[0:rows, :],
                            bvr[0:rows, half * 512:(half + 1) * 512]))
                    blk.vector(s_ev)
            self.tap(blk, f"kloc0_{b}", klocT[0][:, :])
            self.tap(blk, f"vloc0_{b}", vloc[0][:, :])

        es2.close()
        es3 = contextlib.ExitStack()
        wbuf = [es3.enter_context(nc.sbuf_tensor(f"wc{b}_{i}", [128, C], F32)) for i in range(DCH)]
        for ci in range(K):
            self._b3_ci(b, ci, klocT, vloc, wbuf, bpr)
        es3.close()
        es.close()

    def _b3_ci(self, b, ci, klocT, vloc, wbuf, bpr):
        nc = self.nc
        ranges = _kv_ranges(ci)
        L = sum(l for _, l in ranges)
        rtiles = _range_tiles(ci)
        poff = {}
        acc = 0
        for r, (off, ln) in enumerate(ranges):
            poff[r] = acc
            acc += ln
        with (
            nc.sbuf_tensor(f"qT{b}{ci}", [128, DCH * 512], F32) as qTc,
            nc.sbuf_tensor(f"xns{b}{ci}", [128, DCH * 512], F32) as xns,
            nc.sbuf_tensor(f"vr{b}{ci}", [128, len(rtiles) * C], F32) as vr,
            nc.sbuf_tensor(f"probs{b}{ci}", [128, 2 * 512], F32) as probs,
            nc.sbuf_tensor(f"probsT{b}{ci}", [128, 2 * 128], F32) as probsT,
            nc.sbuf_tensor(f"attn{b}{ci}", [128, 4 * C], F32) as attn,
            nc.sbuf_tensor(f"smax{b}{ci}", [128, 16], F32) as smax,
            nc.sbuf_tensor(f"rs{b}{ci}", [128, 8], F32) as rs,
            nc.sbuf_tensor(f"xres{b}{ci}", [128, 2 * C], F32) as xres,
            nc.psum_tensor(f"plA{b}{ci}", [128, 512], F32) as plA,
            nc.psum_tensor(f"plB{b}{ci}", [128, 512], F32) as plB,
            nc.psum_tensor(f"plC{b}{ci}", [128, 512], F32) as plC,
            nc.psum_tensor(f"plD{b}{ci}", [128, 512], F32) as plD,
            nc.psum_tensor(f"ptA{b}{ci}", [128, 128], F32) as ptA,
            nc.psum_tensor(f"ptB{b}{ci}", [128, 128], F32) as ptB,
            nc.psum_tensor(f"paA{b}{ci}", [128, 64], F32) as paA,
            nc.psum_tensor(f"paB{b}{ci}", [128, 64], F32) as paB,
            nc.Block(f"b3c{b}{ci}") as blk,
        ):
            def s_ld(sy):
                self.wt(sy, "v")
                self.wt(sy, "a")
                for dch in range(DCH):
                    self.inc("ld", sy.dma_start(
                        out=xns[:, dch * 512:(dch + 1) * 512],
                        in_=self.xnT_s[b, dch, :, ci * 512:(ci + 1) * 512]), 16)
                    self.inc("ld", sy.dma_start(
                        out=wbuf[dch][:, :],
                        in_=self.wqkv[dch * 128:(dch + 1) * 128, 0:C]), 16)
            blk.sync(s_ld)

            def s_vr(sy):
                for ti, (r, bi, soff, nr) in enumerate(rtiles):
                    for (stile, srow, drow, n2) in _asm_pieces(soff, nr):
                        self.inc("ld2", sy.dma_start(
                            out=vr[:, ti * C:(ti + 1) * C][drow:drow + n2, :],
                            in_=vloc[stile][srow:srow + n2, :]), 16)
            blk.scalar(s_vr)

            pg = [pk for pk in (plA, plB)]
            for hdc in range(DCH):
                ps = pg[hdc % 2]

                def s_mm(t, hdc=hdc, ps=ps):
                    self.wt(t, "ld")
                    self.wt(t, "a")
                    for dch in range(DCH):
                        self.inc("mm", t.matmul(
                            ps[:, :], wbuf[dch][:, hdc * 128:(hdc + 1) * 128],
                            xns[:, dch * 512:(dch + 1) * 512],
                            start=(dch == 0), stop=(dch == DCH - 1)))
                blk.tensor(s_mm)

                def s_ev(sc, hdc=hdc, ps=ps):
                    self.wt(sc, "mm")
                    self.inc("a", sc.activation(
                        out=qTc[:, hdc * 512:(hdc + 1) * 512], in_=ps[:, :],
                        func=ACT.Identity, bias=self.bqkv24[:, hdc:hdc + 1]))
                blk.scalar(s_ev)

            a0 = self.n("a")
            gd = dict(pa=[0, 0], pl=[a0, a0, a0, a0], pt=[0, 0],
                      pr=[0, 0], prT=[0, 0], qkmm=[0, 0, 0, 0])
            pls = [plA, plB, plC, plD]
            for h in range(H):
                for qt in range(4):
                    self._b3_qk(blk, b, ci, h, qt, gd, ranges, poff,
                                qTc, klocT, pls[qt])
                for qt in range(4):
                    self._b3_sm(blk, b, ci, h, qt, gd, ranges, poff, L, rtiles,
                                vr, probs, probsT, attn, smax, rs,
                                pls[qt], ptA, ptB, paA, paB)
            pt_guard = gd["pt"]

            def s_wp(sy):
                self.wt(sy, "mm")
                for dch in range(DCH):
                    self.inc("ld", sy.dma_start(out=wbuf[dch][:, :],
                                                in_=self.wproj[dch * 128:(dch + 1) * 128, :]), 16)
            blk.sync(s_wp)

            attnT = xns
            res_guard = [0, 0]
            pp_guard = [0, 0]
            pt_guard = [0, 0]
            for qt in range(4):
                for dch in range(DCH):
                    pt = ptA if dch % 2 == 0 else ptB

                    def s_tp(t, qt=qt, dch=dch, pt=pt, g=pt_guard[dch % 2]):
                        self.wt(t, "v")
                        self.wt(t, "a", g)
                        self.inc("mm", t.transpose(
                            pt[:, :], attn[:, qt * C + dch * 128: qt * C + (dch + 1) * 128],
                            self.eye[:, :]))
                    blk.tensor(s_tp)

                    def s_evt(sc, dch=dch, pt=pt):
                        self.wt(sc, "mm")
                        self.inc("a", sc.activation(
                            out=attnT[:, dch * 128:(dch + 1) * 128], in_=pt[:, :],
                            func=ACT.Copy))
                    blk.scalar(s_evt)
                    pt_guard[dch % 2] = self.n("a")

                if qt == 0:
                    self.tap(blk, f"attnT0_{b}{ci}", attnT[:, 0:DCH * 128])

                def s_xld(sy, qt=qt, g=res_guard[qt % 2]):
                    self.wt(sy, "st", g)
                    self.inc("ld", sy.dma_start(
                        out=xres[:, (qt % 2) * C:(qt % 2 + 1) * C],
                        in_=self.x[b, ci * 512 + qt * 128: ci * 512 + (qt + 1) * 128, :]), 16)
                blk.sync(s_xld)

                for half in range(2):
                    pp = plA if half == 0 else plB

                    def s_pj(t, qt=qt, half=half, pp=pp, g=pp_guard[half]):
                        self.wt(t, "ld")
                        self.wt(t, "a")
                        self.wt(t, "v", g)
                        for dch in range(DCH):
                            self.inc("mm", t.matmul(
                                pp[:, :], attnT[:, dch * 128:(dch + 1) * 128],
                                wbuf[dch][:, half * 512:(half + 1) * 512],
                                start=(dch == 0), stop=(dch == DCH - 1)))
                    blk.tensor(s_pj)

                    def s_res(v, qt=qt, half=half, pp=pp):
                        self.wt(v, "mm")
                        self.wt(v, "ld")
                        v.tensor_add(
                            xres[:, (qt % 2) * C + half * 512:(qt % 2) * C + (half + 1) * 512],
                            xres[:, (qt % 2) * C + half * 512:(qt % 2) * C + (half + 1) * 512],
                            bpr[:, half * 512:(half + 1) * 512])
                        self.sep(v)
                        self.inc("v", v.tensor_add(
                            xres[:, (qt % 2) * C + half * 512:(qt % 2) * C + (half + 1) * 512],
                            xres[:, (qt % 2) * C + half * 512:(qt % 2) * C + (half + 1) * 512],
                            pp[:, :]))
                    blk.vector(s_res)

                pp_guard[0] = pp_guard[1] = self.n("v")
                if qt == 0:
                    self.tap(blk, f"xres0_{b}{ci}", xres[:, 0:C])

                def s_sp(sy, qt=qt):
                    self.wt(sy, "v")
                    self.inc("st", sy.dma_start(
                        out=self.resid_s[b, ci * 512 + qt * 128: ci * 512 + (qt + 1) * 128, :],
                        in_=xres[:, (qt % 2) * C:(qt % 2 + 1) * C]), 16)
                blk.sync(s_sp)
                res_guard[qt % 2] = self.n("st")
            if ci == 0:
                self.tap(blk, f"qT0_{b}", qTc[:, :])
                self.tap(blk, f"attn0_{b}", attn[:, :])


    def _b3_qk(self, blk, b, ci, h, qt, gd, ranges, poff, qTc, klocT, pl):
        chunk, hof = h // 2, (h % 2) * 64

        def s_qk(t, g=gd["pl"][qt]):
            self.wt(t, "a", g)
            for r, (off, ln) in enumerate(ranges):
                self.inc("mm", t.matmul(
                    pl[:, poff[r]:poff[r] + ln],
                    qTc[:, chunk * 512 + qt * 128: chunk * 512 + (qt + 1) * 128][hof:hof + 64, :],
                    klocT[chunk][hof:hof + 64, off:off + ln],
                    start=(r == 0), stop=(r == len(ranges) - 1)))
        blk.tensor(s_qk)
        gd["qkmm"][qt] = self.n("mm")

    def _b3_sm(self, blk, b, ci, h, qt, gd, ranges, poff, L, rtiles,
               vr, probs, probsT, attn, smax, rs, pl, ptA, ptB, paA, paB):
        pa = paA if qt % 2 == 0 else paB
        pslot = qt % 2

        def s_max(v, g=gd["qkmm"][qt]):
            self.wt(v, "mm", g)
            v.reduce_max(out=smax[:, qt:qt + 1], in_=pl[:, 0:L],
                         axis=AX.X, negate=True)
            self.sep(v)
            self.inc("v", v.tensor_scalar_mul(smax[:, qt:qt + 1],
                                              smax[:, qt:qt + 1], SCALE))
        blk.vector(s_max)

        def s_exp(sc, g=gd["pr"][pslot]):
            self.wt(sc, "v")
            self.wt(sc, "mm", g)
            self.inc("a", sc.activation(
                out=probs[:, pslot * 512: pslot * 512 + L], in_=pl[:, 0:L],
                func=ACT.Exp, scale=SCALE, bias=smax[:, qt:qt + 1],
                accum_out=rs[:, qt:qt + 1]))
        blk.scalar(s_exp)
        gd["pl"][qt] = self.n("a")

        npieces = len(rtiles)

        def emit_tp(pi):
            r, bi, soff, nr = rtiles[pi]
            pk0 = poff[r] + bi * 128
            pt = ptA if pi % 2 == 0 else ptB
            tslot = pi % 2

            def s_tp(t, pk0=pk0, nr=nr, pt=pt, g=gd["pt"][tslot]):
                self.wt(t, "a")
                self.wt(t, "v", g)
                self.inc("mm", t.transpose(
                    pt[0:nr, 0:128],
                    probs[:, pslot * 512 + pk0: pslot * 512 + pk0 + nr],
                    self.eye[:, :]))
            blk.tensor(s_tp)

            def s_evt(v, nr=nr, pt=pt, tslot=tslot, g=gd["prT"][tslot]):
                self.wt(v, "mm")
                self.wt(v, "mm", g)
                self.inc("v", v.tensor_copy(
                    out=probsT[0:nr, tslot * 128:tslot * 128 + 128],
                    in_=pt[0:nr, 0:128]))
            blk.vector(s_evt)
            gd["pt"][tslot] = self.n("v")

        def emit_pv(pi):
            r, bi, soff, nr = rtiles[pi]
            tslot = pi % 2

            def s_pv(t, pi=pi, nr=nr, ti=pi, tslot=tslot, g=gd["pa"][qt % 2]):
                self.wt(t, "v")
                if pi == 0:
                    self.wt(t, "v", g)
                    self.wt(t, "ld2")
                self.inc("mm", t.matmul(
                    pa[:, 0:64],
                    probsT[0:nr, tslot * 128:tslot * 128 + 128],
                    vr[:, ti * C:(ti + 1) * C][0:nr, h * 64:(h + 1) * 64],
                    start=(pi == 0), stop=(pi == npieces - 1)))
            blk.tensor(s_pv)
            gd["prT"][tslot] = self.n("mm")

        emit_tp(0)
        for pi in range(1, npieces):
            emit_tp(pi)
            emit_pv(pi - 1)
        emit_pv(npieces - 1)
        gd["pr"][pslot] = self.n("mm")

        def s_attn_ev(v):
            self.wt(v, "a")
            self.wt(v, "mm")
            v.reciprocal(out=rs[:, 4 + qt:5 + qt], in_=rs[:, qt:qt + 1])
            v.tensor_copy(out=attn[:, qt * C + h * 64: qt * C + (h + 1) * 64],
                          in_=pa[:, 0:64])
            self.sep(v)
            self.inc("v", v.tensor_scalar_mul(
                attn[:, qt * C + h * 64: qt * C + (h + 1) * 64],
                attn[:, qt * C + h * 64: qt * C + (h + 1) * 64],
                rs[:, 4 + qt:5 + qt]))
        blk.vector(s_attn_ev)
        gd["pa"][qt % 2] = self.n("v")

    def b4(self):
        nc = self.nc
        with (
            nc.sbuf_tensor("qf", [128, 2 * 2048], F32) as qf,
            nc.sbuf_tensor("simb", [128, 2 * 4096], F32) as simb,
            nc.sbuf_tensor("tkb", [128, 512], F32) as tkb,
            nc.sbuf_tensor("tku", [128, 128], U32) as tku,
            nc.psum_tensor("psA", [128, 256], F32) as psA,
            nc.psum_tensor("psB", [128, 256], F32) as psB,
            nc.Block("b4") as blk,
        ):
            qf_guard = [0, 0]
            sim_guard = [0, 0]
            ps_guard = [0, 0]
            for tg in range(NTILE):
                b, tt = tg // 16, tg % 16
                qslot = tg % 2
                sslot = tg % 2

                def s_ld(sy, tg=tg, qslot=qslot, g=qf_guard[qslot]):
                    self.wt(sy, "v", g)
                    for jc in range(16):
                        self.inc("ld", sy.dma_start(
                            out=qf[:, qslot * 2048 + jc * 128: qslot * 2048 + (jc + 1) * 128],
                            in_=self.q_s[jc, :, tg * 128:(tg + 1) * 128]), 16)
                blk.sync(s_ld)

                def s_bn(v, qslot=qslot):
                    self.wt(v, "ld")
                    for jc in range(16):
                        sl = qf[:, qslot * 2048 + jc * 128: qslot * 2048 + (jc + 1) * 128]
                        ins = v.tensor_scalar(out=sl, in0=sl,
                                              scalar1=self.bn_a[:, jc:jc + 1],
                                              scalar2=self.bn_b[:, jc:jc + 1],
                                              op0=ALU.mult, op1=ALU.add)
                    self.inc("v", ins)
                blk.vector(s_bn)

                for jc in range(16):
                    ps = psA if jc % 2 == 0 else psB

                    def s_mm(t, jc=jc, qslot=qslot, ps=ps, g=ps_guard[jc % 2]):
                        self.wt(t, "v")
                        self.wt(t, "v", g)
                        self.inc("mm", t.matmul(
                            ps[:, :],
                            qf[:, qslot * 2048 + jc * 128: qslot * 2048 + (jc + 1) * 128],
                            self.keys_sb[:, jc * NKEY:(jc + 1) * NKEY],
                            start=True, stop=True))
                    blk.tensor(s_mm)

                    def s_ev(v, jc=jc, sslot=sslot, ps=ps,
                             g=sim_guard[sslot] if jc == 0 else None):
                        self.wt(v, "mm")
                        if g is not None:
                            self.wt(v, "v", g)
                        self.inc("v", v.tensor_copy(
                            out=simb[:, sslot * 4096 + jc * NKEY: sslot * 4096 + (jc + 1) * NKEY],
                            in_=ps[:, :]))
                    blk.vector(s_ev)
                    ps_guard[jc % 2] = self.n("v")
                qf_guard[qslot] = self.n("v")

                for hp in range(4):
                    chains = [self._b4_chain(tg, 2 * hp + par, sslot, par, simb, tkb, tku)
                              for par in range(2)]

                    def s_tk1(v, chains=chains):
                        self.wt(v, "v")
                        for opa, opb in zip(chains[0][0], chains[1][0]):
                            opa(v)
                            opb(v)
                        self.inc("v", v.tensor_copy(out=tkb[:, 248:249], in_=tkb[:, 152:153]))
                    blk.vector(s_tk1)

                    def s_tk2(sc, chains=chains):
                        self.wt(sc, "v")
                        for par in range(2):
                            o = 256 * par
                            ins = sc.activation(out=tkb[:, o + 160:o + 168],
                                                in_=tkb[:, o + 96:o + 104],
                                                func=ACT.Exp, bias=tkb[:, o + 152:o + 153],
                                                accum_out=tkb[:, o + 153:o + 154])
                        self.inc("a", ins)
                    blk.scalar(s_tk2)

                    def s_tk3(v, chains=chains, tg=tg, hp=hp):
                        self.wt(v, "a")
                        for par in range(2):
                            o = 256 * par
                            v.reciprocal(out=tkb[:, o + 154:o + 155],
                                         in_=tkb[:, o + 153:o + 154])
                        for par in range(2):
                            o = 256 * par
                            hf = 2 * hp + par
                            self.inc("v", v.tensor_scalar_mul(
                                self.g_all[:, tg * 64 + hf * 8: tg * 64 + hf * 8 + 8],
                                tkb[:, o + 160:o + 168], tkb[:, o + 154:o + 155]))
                    blk.vector(s_tk3)
                sim_guard[sslot] = self.n("v")
            self.tap(blk, "idx_all", self.idx_all[:, :])
            self.tap(blk, "g_all", self.g_all[:, :])

    def _b4_chain(self, tg, hf, sslot, par, simb, tkb, tku):
        o = 256 * par
        u = 64 * par
        sx = simb[:, sslot * 4096 + hf * NKEY: sslot * 4096 + (hf + 1) * NKEY]
        sy_ = simb[:, sslot * 4096 + (8 + hf) * NKEY: sslot * 4096 + (9 + hf) * NKEY]
        T = tkb
        U = tku
        ops = [
            lambda v: v.max(out=T[:, o:o + 8], in_=sx),
            lambda v: v.max(out=T[:, o + 8:o + 16], in_=sy_),
            lambda v: v.max_index(out=U[:, u:u + 8], in_max=T[:, o:o + 8], in_values=sx),
            lambda v: v.max_index(out=U[:, u + 8:u + 16], in_max=T[:, o + 8:o + 16], in_values=sy_),
            lambda v: v.tensor_copy(out=T[:, o + 16:o + 24], in_=U[:, u:u + 8]),
            lambda v: v.tensor_copy(out=T[:, o + 24:o + 32], in_=U[:, u + 8:u + 16]),
            lambda v: v.tensor_tensor(
                out=T[:, o + 32:o + 96].rearrange("p (i j) -> p i j", i=8),
                in0=T[:, o:o + 8].unsqueeze(2).to_broadcast([128, 8, 8]),
                in1=T[:, o + 8:o + 16].unsqueeze(1).to_broadcast([128, 8, 8]),
                op=ALU.add),
            lambda v: v.max(out=T[:, o + 96:o + 104], in_=T[:, o + 32:o + 96]),
            lambda v: v.max_index(out=U[:, u + 16:u + 24], in_max=T[:, o + 96:o + 104],
                                  in_values=T[:, o + 32:o + 96]),
            lambda v: v.tensor_scalar(out=U[:, u + 24:u + 32], in0=U[:, u + 16:u + 24],
                                      scalar1=3, scalar2=None,
                                      op0=ALU.logical_shift_right),
            lambda v: v.tensor_scalar(out=U[:, u + 32:u + 40], in0=U[:, u + 16:u + 24],
                                      scalar1=7, scalar2=None, op0=ALU.bitwise_and),
            lambda v: v.tensor_copy(out=T[:, o + 104:o + 112], in_=U[:, u + 24:u + 32]),
            lambda v: v.tensor_copy(out=T[:, o + 112:o + 120], in_=U[:, u + 32:u + 40]),
            lambda v: v.tensor_tensor(
                out=T[:, o + 128:o + 192].rearrange("p (s j) -> p s j", s=8),
                in0=T[:, o + 104:o + 112].unsqueeze(2).to_broadcast([128, 8, 8]),
                in1=self.iota8[:, :].unsqueeze(1).to_broadcast([128, 8, 8]),
                op=ALU.is_equal),
            lambda v: v.tensor_tensor(
                out=T[:, o + 192:o + 256].rearrange("p (s j) -> p s j", s=8),
                in0=T[:, o + 128:o + 192].rearrange("p (s j) -> p s j", s=8),
                in1=T[:, o + 16:o + 24].unsqueeze(1).to_broadcast([128, 8, 8]),
                op=ALU.mult),
            lambda v: v.reduce_sum(out=T[:, o + 120:o + 128],
                                   in_=T[:, o + 192:o + 256].rearrange("p (s j) -> p s j", s=8),
                                   axis=AX.X),
            lambda v: v.tensor_tensor(
                out=T[:, o + 128:o + 192].rearrange("p (s j) -> p s j", s=8),
                in0=T[:, o + 112:o + 120].unsqueeze(2).to_broadcast([128, 8, 8]),
                in1=self.iota8[:, :].unsqueeze(1).to_broadcast([128, 8, 8]),
                op=ALU.is_equal),
            lambda v: v.tensor_tensor(
                out=T[:, o + 192:o + 256].rearrange("p (s j) -> p s j", s=8),
                in0=T[:, o + 128:o + 192].rearrange("p (s j) -> p s j", s=8),
                in1=T[:, o + 24:o + 32].unsqueeze(1).to_broadcast([128, 8, 8]),
                op=ALU.mult),
            lambda v: v.reduce_sum(out=T[:, o + 136:o + 144],
                                   in_=T[:, o + 192:o + 256].rearrange("p (s j) -> p s j", s=8),
                                   axis=AX.X),
            lambda v: v.scalar_tensor_tensor(out=T[:, o + 144:o + 152], in0=T[:, o + 120:o + 128],
                                             scalar=256.0, in1=T[:, o + 136:o + 144],
                                             op0=ALU.mult, op1=ALU.add),
            lambda v: v.tensor_copy(
                out=self.idx_all[:, tg * 64 + hf * 8: tg * 64 + hf * 8 + 8],
                in_=T[:, o + 144:o + 152]),
            lambda v: v.tensor_scalar_mul(T[:, o + 152:o + 153], T[:, o + 96:o + 97], -1.0),
        ]
        return (ops,)

    def b5(self):
        nc = self.nc
        acc_vcount = {}
        with (
            nc.sbuf_tensor("gring", [128, NBUF * 2048], F32) as gring,
            nc.sbuf_tensor("xn5", [128, 2 * C], F32) as xn5,
            nc.sbuf_tensor("acc5", [128, 2 * C], F32) as acc5,
            nc.sbuf_tensor("scr5", [128, C], F32) as scr5,
            nc.sbuf_tensor("hb5", [128, 16], F32) as hb5,
            nc.Block("b5") as blk,
        ):
            xn_guard = [0, 0]
            acc_guard = [0, 0]
            for tg in range(NTILE):
                b, tt = tg // 16, tg % 16
                slot = tg % 2

                def s_ld(sy, tg=tg, b=b, tt=tt, slot=slot, gx=xn_guard[slot],
                         ga=acc_guard[slot]):
                    self.wt(sy, "v", gx)
                    self.inc("ld", sy.dma_start(
                        out=xn5[:, slot * C:(slot + 1) * C],
                        in_=self.x[b, tt * 128:(tt + 1) * 128, :]), 16)
                    self.wt(sy, "st", ga)
                    self.inc("ld", sy.dma_start(
                        out=acc5[:, slot * C:(slot + 1) * C],
                        in_=self.resid_s[b, tt * 128:(tt + 1) * 128, :]), 16)
                blk.sync(s_ld)

                def s_norm(v, tg=tg, slot=slot):
                    self.wt(v, "ld")
                    self.inc("v", v.tensor_scalar(
                        out=xn5[:, slot * C:(slot + 1) * C],
                        in0=xn5[:, slot * C:(slot + 1) * C],
                        scalar1=self.rstd_all[:, tg:tg + 1],
                        scalar2=self.nmr_all[:, tg:tg + 1],
                        op0=ALU.mult, op1=ALU.add))
                blk.vector(s_norm)

                def emit_gathers(grp):
                    for i in range(4):
                        gk = tg * 64 + grp * 4 + i
                        buf = gk % NBUF

                        def s_g(g, gk=gk, buf=buf,
                                gv=acc_vcount.get(gk - NBUF, 0)):
                            if gv:
                                self.wt(g, "v", gv)
                            self.inc("g", g.indirect_dma_start(
                                gring[:, buf * 2048:(buf + 1) * 2048], None,
                                self.tbl[:, :],
                                bass.IndirectOffsetOnAxis(
                                    self.idx_all[:, gk:gk + 1], 0)), 16)
                        blk.gpsimd(s_g)

                def dot(v, grp, i, slot=slot, tg=tg):
                    gk = tg * 64 + grp * 4 + i
                    buf = gk % NBUF
                    hcol = (grp % 2) * 4 + i
                    self.wt(v, "g", self._b5_gcnt[gk])
                    self.inc("v", v.scalar_tensor_tensor(
                        out=scr5[:, :],
                        in0=gring[:, buf * 2048: buf * 2048 + 1024],
                        scalar=1.0,
                        in1=xn5[:, slot * C:(slot + 1) * C],
                        op0=ALU.bypass, op1=ALU.mult,
                        accum_out=hb5[:, hcol:hcol + 1]))

                def acc(v, grp, i, slot=slot, tg=tg):
                    gk = tg * 64 + grp * 4 + i
                    buf = gk % NBUF
                    wcol = 8 + (grp % 2) * 4 + i
                    self.inc("v", v.scalar_tensor_tensor(
                        out=acc5[:, slot * C:(slot + 1) * C],
                        in0=gring[:, buf * 2048 + 1024: (buf + 1) * 2048],
                        scalar=hb5[:, wcol:wcol + 1],
                        in1=acc5[:, slot * C:(slot + 1) * C],
                        op0=ALU.mult, op1=ALU.add))
                    acc_vcount[gk] = self.n("v")

                def gelu(sc, grp):
                    hc = (grp % 2) * 4
                    self.wt(sc, "v")
                    self.inc("a", sc.activation(out=hb5[:, 8 + hc:12 + hc],
                                                in_=hb5[:, hc:hc + 4],
                                                func=ACT.Gelu_apprx_tanh))

                def wmul(v, grp, tg=tg):
                    hc = (grp % 2) * 4
                    self.wt(v, "a")
                    self.inc("v", v.tensor_mul(
                        hb5[:, 8 + hc:12 + hc], hb5[:, 8 + hc:12 + hc],
                        self.g_all[:, tg * 64 + grp * 4: tg * 64 + grp * 4 + 4]))
                blk_v = blk.vector
                blk_s = blk.scalar

                if not hasattr(self, "_b5_gcnt"):
                    self._b5_gcnt = {}
                emit_gathers(0)
                for i in range(4):
                    self._b5_gcnt[tg * 64 + i] = self.n("g")

                def s_d0(v, tg=tg):
                    for i in range(4):
                        dot(v, 0, i)
                blk_v(s_d0)
                blk_s(lambda sc: gelu(sc, 0))
                for grp in range(1, 16):
                    emit_gathers(grp)
                    for i in range(4):
                        self._b5_gcnt[tg * 64 + grp * 4 + i] = self.n("g")

                    def s_body(v, grp=grp):
                        wmul(v, grp - 1)
                        for i in range(4):
                            dot(v, grp, i)
                            acc(v, grp - 1, i)
                    blk_v(s_body)
                    blk_s(lambda sc, grp=grp: gelu(sc, grp))

                def s_tail(v, tg=tg):
                    wmul(v, 15)
                    for i in range(4):
                        self.sep(v)
                        acc(v, 15, i)
                blk_v(s_tail)

                def s_out(sy, tg=tg, b=b, tt=tt, slot=slot):
                    self.wt(sy, "v")
                    self.inc("st", sy.dma_start(
                        out=self.out[b, tt * 128:(tt + 1) * 128, :],
                        in_=acc5[:, slot * C:(slot + 1) * C]), 16)
                blk.sync(s_out)
                xn_guard[slot] = self.n("v")
                acc_guard[slot] = self.n("st")

            def s_fin(sy):
                self.wt(sy, "st")
                self.wt(sy, "st2")
            blk.sync(s_fin)

    def build(self):
        nc = self.nc
        self.b0()
        if self.stop_after == "b0":
            return self._fin()
        es1 = contextlib.ExitStack()
        xnT = [es1.enter_context(nc.sbuf_tensor(f"xnTc{i}", [128, N], F32)) for i in range(DCH)]
        wqk = [es1.enter_context(nc.sbuf_tensor(f"wqk{i}", [128, 2 * C], F32)) for i in range(DCH)]
        with nc.Block("b1w") as blk:
            def s_w(sy):
                for dch in range(DCH):
                    self.inc("ld", sy.dma_start(out=wqk[dch][:, :],
                                                in_=self.wqkv[dch * 128:(dch + 1) * 128, 0:2 * C]), 16)
            blk.sync(s_w)
        for b in range(NB):
            self.b1(b, wqk, xnT)
        es1.close()
        if self.stop_after == "b1":
            return self._fin()
        self.b2()
        if self.stop_after == "b2":
            return self._fin()
        for b in range(NB):
            self.b3(b)
        if self.stop_after == "b3":
            return self._fin()
        self.b4()
        if self.stop_after == "b4":
            return self._fin()
        self.b5()
        return self._fin()

    def _fin(self):
        nc = self.nc
        if self.taps:
            with nc.Block("tapw") as blk:
                def f(sy):
                    self.wt(sy, "st2")
                blk.sync(f)
        return nc


def build_nc(n_cores=NCORES, taps=(), stop_after=None):
    mk = MK(n_cores=n_cores, taps=taps, stop_after=stop_after)
    mk.build()
    return mk


def make_in_map(x_shard, inputs):
    keys_r = np.ascontiguousarray(
        np.transpose(np.asarray(inputs["L0_ffn_keys"], dtype=np.float32), (3, 2, 0, 1))
    ).reshape(128, 16, NKEY)
    tbl = np.concatenate([
        np.asarray(inputs["L0_ffn_down_embed"], dtype=np.float32),
        np.asarray(inputs["L0_ffn_up_embed"], dtype=np.float32)], axis=1)
    return {
        "x": np.ascontiguousarray(x_shard),
        "wqkv": np.ascontiguousarray(np.asarray(inputs["L0_attn_W_qkv"], np.float32)),
        "bqkv": np.ascontiguousarray(np.asarray(inputs["L0_attn_b_qkv"], np.float32)),
        "wproj": np.ascontiguousarray(np.asarray(inputs["L0_attn_W_proj"], np.float32)),
        "bproj": np.ascontiguousarray(np.asarray(inputs["L0_attn_b_proj"], np.float32)),
        "wq_in": np.ascontiguousarray(np.asarray(inputs["L0_ffn_w_q"], np.float32)),
        "gamma": np.ascontiguousarray(np.asarray(inputs["L0_ffn_bn_gamma"], np.float32)),
        "beta": np.ascontiguousarray(np.asarray(inputs["L0_ffn_bn_beta"], np.float32)),
        "keys": keys_r,
        "tbl": np.ascontiguousarray(tbl),
    }


def kernel(**inputs):
    x = np.asarray(inputs["L0_attn_x"], dtype=np.float32)
    mk = MK(n_cores=NCORES)
    mk.build()
    base = make_in_map(x[0:2], inputs)
    in_maps = []
    for c in range(NCORES):
        m = dict(base)
        m["x"] = np.ascontiguousarray(x[2 * c:2 * c + 2])
        in_maps.append(m)
    res = run_bass_kernel_spmd(mk.nc, in_maps, core_ids=list(range(NCORES)))
    out = np.concatenate([r["out"] for r in res.results], axis=0)
    return out.astype(np.float32)


# version 07
# speedup vs baseline: 1.4231x; 1.2803x over previous
import contextlib
import numpy as np
import concourse.bass as bass
import concourse.mybir as mybir
from concourse.bass_utils import run_bass_kernel_spmd

F32 = mybir.dt.float32
I32 = mybir.dt.int32
U32 = mybir.dt.uint32
AX = mybir.AxisListType
ALU = mybir.AluOpType
ACT = mybir.ActivationFunctionType

NCORES = 8
NB = 2
N = 2048
C = 1024
H = 16
D = 64
S = 32
P = 64
K = 4
FC = 8
TK = 19
NSUB = 5
SCALE = D ** -0.5
LN_EPS = 1e-5
NKEY = 256
NEXP = 65536
BN_EPS = 1e-5
NTOK = NB * N
NTILE = NTOK // 128
DCH = C // 128
NSEL = S * TK
NBUF = 16


def _kv_ranges(ci):
    out = []
    for ncl in range(K):
        nf = FC if (ncl == 0 or ncl == ci) else NSUB
        out.append((ncl * FC * TK, nf * TK))
    return out


def _range_tiles(ci):
    out = []
    for r, (off, ln) in enumerate(_kv_ranges(ci)):
        c0 = 0
        bi = 0
        while c0 < ln:
            nr = min(128, ln - c0)
            out.append((r, bi, off + c0, nr))
            c0 += nr
            bi += 1
    return out


def _asm_pieces(sel_off, nrows):
    out = []
    done = 0
    while done < nrows:
        g = sel_off + done
        nr = min(nrows - done, 128 - (g % 128))
        out.append((g // 128, g % 128, done, nr))
        done += nr
    return out


class MK:
    def __init__(self, n_cores=NCORES, taps=(), stop_after=None, dbg_dram=False):
        self.nc = bass.Bass("TRN2", target_bir_lowering=False,
                            detect_race_conditions=False)
        self.n_cores = n_cores
        self.tap_names = set(taps)
        self.taps = {}
        self.stop_after = stop_after
        nc = self.nc
        self.x = nc.dram_tensor("x", [NB, N, C], F32, kind="ExternalInput").ap()
        self.wqkv = nc.dram_tensor("wqkv", [C, 3 * C], F32, kind="ExternalInput").ap()
        self.bqkv = nc.dram_tensor("bqkv", [3 * C], F32, kind="ExternalInput").ap()
        self.wproj = nc.dram_tensor("wproj", [C, C], F32, kind="ExternalInput").ap()
        self.bproj = nc.dram_tensor("bproj", [C], F32, kind="ExternalInput").ap()
        self.wq_in = nc.dram_tensor("wq_in", [C, 2 * C], F32, kind="ExternalInput").ap()
        self.gamma = nc.dram_tensor("gamma", [2 * C], F32, kind="ExternalInput").ap()
        self.beta = nc.dram_tensor("beta", [2 * C], F32, kind="ExternalInput").ap()
        self.keys = nc.dram_tensor("keys", [128, 16, NKEY], F32, kind="ExternalInput").ap()
        self.tbl = nc.dram_tensor("tbl", [NEXP, 2048], F32, kind="ExternalInput").ap()
        self.out = nc.dram_tensor("out", [NB, N, C], F32, kind="ExternalOutput").ap()
        dbk = dict(kind="ExternalOutput") if dbg_dram else {}
        self.xnT_s = nc.dram_tensor("xnT_s", [NB, DCH, 128, N], F32, **dbk).ap()
        self.q_s = nc.dram_tensor("q_s", [16, 128, NTOK], F32, **dbk).ap()
        self.resid_s = nc.dram_tensor("resid_s", [NB, N, C], F32, **dbk).ap()
        self.cc_in = nc.dram_tensor("cc_in", [128, 36], F32).ap()
        self.cc_out = nc.dram_tensor("cc_out", [128, 36], F32).ap()

        A = nc.alloc_sbuf_tensor
        self.eye = A("eye", [128, 128], F32)
        self.iota19 = A("iota19", [64, TK], F32)
        self.iota8 = A("iota8", [128, 8], F32)
        self.bqkv24 = A("bqkv24", [128, 24], F32)
        self.bq_rep = None
        self.bk_rep = None
        self.bv_rep = None
        self.bproj_rep = None
        self.gamma16 = A("gamma16", [128, 16], F32)
        self.beta16 = A("beta16", [128, 16], F32)
        self.keys_sb = A("keys_sb", [128, 16 * NKEY], F32)
        self.rstd_all = A("rstd_all", [128, NTILE], F32)
        self.nmr_all = A("nmr_all", [128, NTILE], F32)
        self.bnacc = A("bnacc", [128, 128], F32)
        self.bnacc2 = A("bnacc2", [128, 128], F32)
        self.bnstat = A("bnstat", [128, 36], F32)
        self.allr = A("allr", [128, 36], F32)
        self.bn_a = A("bn_a", [128, 16], F32)
        self.bn_b = A("bn_b", [128, 16], F32)
        self.sel2 = A("sel2", [128, K * TK], F32)
        self.idx_all = A("idx_all", [128, NTILE * 64], I32)
        self.g_all = A("g_all", [128, NTILE * 64], F32)

        self.sems = {}
        for s in ["ld", "ld2", "g", "st", "st2", "mm", "v", "a", "gp", "cc", "fr"]:
            self.sems[s] = [nc.alloc_semaphore(s), 0]

    def inc(self, name, instr, k=1):
        st = self.sems[name]
        instr.then_inc(st[0], k)
        st[1] += k
        return st[1]

    def n(self, name):
        return self.sems[name][1]

    def wt(self, eng, name, val=None):
        st = self.sems[name]
        eng.wait_ge(st[0], st[1] if val is None else val)

    def sep(self, eng):
        eng.drain()

    def tap(self, blk, name, src_ap):
        if name in self.tap_names and name not in self.taps:
            t = self.nc.dram_tensor("tap_" + name, list(src_ap.shape),
                                    src_ap.dtype, kind="ExternalOutput").ap()
            self.taps[name] = t

            def f(sy, t=t, src=src_ap):
                for s in ("ld", "ld2", "v", "a", "mm", "g"):
                    self.wt(sy, s)
                self.inc("st2", sy.dma_start(
                    out=t[tuple([slice(None)] * len(t.shape))], in_=src), 16)
            blk.scalar(f)

    def b0(self):
        nc = self.nc
        with (
            nc.sbuf_tensor("c_i32", [128, 128], I32) as c_i32,
            nc.sbuf_tensor("c_m24", [24, 128], F32) as c_m24,
            nc.sbuf_tensor("c_g16", [16, 128], F32) as c_g16,
            nc.sbuf_tensor("c_b16", [16, 128], F32) as c_b16,
            nc.psum_tensor("p_c", [128, 128], F32) as p_c,
            nc.psum_tensor("p_c2", [128, 16], F32) as p_c2,
            nc.psum_tensor("p_c3", [128, 16], F32) as p_c3,
            nc.Block("b0") as blk,
        ):
            def s_load(sy):
                self.inc("ld", sy.dma_start(out=c_m24[:, :], in_=self.bqkv.rearrange("(a b) -> a b", a=24)), 16)
                self.inc("ld", sy.dma_start(out=c_g16[:, :], in_=self.gamma.rearrange("(a b) -> a b", a=16)), 16)
                self.inc("ld", sy.dma_start(out=c_b16[:, :], in_=self.beta.rearrange("(a b) -> a b", a=16)), 16)
                self.inc("ld", sy.dma_start(out=self.keys_sb[:, :], in_=self.keys.rearrange("p a k -> p (a k)")), 16)
            blk.sync(s_load)

            def s_iota(g):
                g.iota(c_i32[:, :], pattern=[[1, 128]], base=0, channel_multiplier=-1)
                g.iota(self.iota19[:, :], pattern=[[1, TK]], base=1,
                       channel_multiplier=0, allow_small_or_imprecise_dtypes=True)
                self.inc("gp", g.iota(self.iota8[:, :], pattern=[[1, 8]], base=0,
                                      channel_multiplier=0,
                                      allow_small_or_imprecise_dtypes=True))
            blk.gpsimd(s_iota)

            def s_eye(v):
                self.wt(v, "gp")
                v.tensor_scalar(out=self.eye[:, :], in0=c_i32[:, :], scalar1=0,
                                scalar2=None, op0=ALU.is_equal)
                v.memset(self.bnstat[:, :], 0.0)
                v.memset(self.bnacc[:, :], 0.0)
                self.inc("v", v.memset(self.bnacc2[:, :], 0.0))
            blk.vector(s_eye)

            def s_tr(t):
                self.wt(t, "ld")
                self.wt(t, "v")
                t.transpose(p_c[:, 0:24], c_m24[:, :], self.eye[0:24, 0:24])
                t.transpose(p_c2[:, 0:16], c_g16[:, :], self.eye[0:16, 0:16])
                self.inc("mm", t.transpose(p_c3[:, 0:16], c_b16[:, :], self.eye[0:16, 0:16]))
            blk.tensor(s_tr)

            def s_ev(sc):
                self.wt(sc, "mm")
                sc.activation(out=self.bqkv24[:, :], in_=p_c[:, 0:24], func=ACT.Copy)
                sc.activation(out=self.gamma16[:, :], in_=p_c2[:, 0:16], func=ACT.Copy)
                self.inc("a", sc.activation(out=self.beta16[:, :], in_=p_c3[:, 0:16], func=ACT.Copy))
            blk.scalar(s_ev)

    def b1(self, b, wqk, xnT):
        nc = self.nc
        with (
            nc.sbuf_tensor(f"xnr{b}", [128, 2 * C], F32) as xnr,
            nc.sbuf_tensor(f"lnt{b}", [128, 8], F32) as lnt,
            nc.sbuf_tensor(f"sq_scr{b}", [128, C], F32) as sq_scr,
            nc.psum_tensor(f"ptA{b}", [128, 128], F32) as ptA,
            nc.psum_tensor(f"ptB{b}", [128, 128], F32) as ptB,
            nc.Block(f"b1_ln{b}") as blk,
        ):
            slot_guard = [0, 0]
            bank_guard = [0, 0]
            for tt in range(16):
                col = b * 16 + tt
                slot = tt % 2
                xt = xnr[:, slot * C:(slot + 1) * C]

                def s_load(sy, xt=xt, tt=tt, g=slot_guard[slot]):
                    self.wt(sy, "mm", g)
                    self.inc("ld", sy.dma_start(out=xt, in_=self.x[b, tt * 128:(tt + 1) * 128, :]), 16)
                blk.sync(s_load)

                def s_sums(sc, xt=xt):
                    self.wt(sc, "ld")
                    sc.activation(out=sq_scr[:, :], in_=xt, func=ACT.Copy,
                                  accum_out=lnt[:, 0:1])
                    self.inc("a", sc.activation(out=sq_scr[:, :], in_=xt, func=ACT.Square,
                                                accum_out=lnt[:, 1:2]))
                blk.scalar(s_sums)

                def s_stats(v):
                    self.wt(v, "a")
                    v.tensor_scalar_mul(lnt[:, 2:3], lnt[:, 0:1], 1.0 / C)
                    v.tensor_scalar_mul(lnt[:, 3:4], lnt[:, 1:2], 1.0 / C)
                    v.tensor_mul(lnt[:, 4:5], lnt[:, 2:3], lnt[:, 2:3])
                    self.sep(v)
                    v.tensor_sub(lnt[:, 5:6], lnt[:, 3:4], lnt[:, 4:5])
                    self.sep(v)
                    self.inc("v", v.tensor_scalar_add(lnt[:, 5:6], lnt[:, 5:6], LN_EPS))
                blk.vector(s_stats)

                def s_rstd(sc, col=col):
                    self.wt(sc, "v")
                    self.inc("a", sc.activation(out=lnt[:, 6:7],
                                                in_=lnt[:, 5:6], func=ACT.Sqrt))
                blk.scalar(s_rstd)

                def s_nmr(v, col=col):
                    self.wt(v, "a")
                    v.reciprocal(out=self.rstd_all[:, col:col + 1], in_=lnt[:, 6:7])
                    self.sep(v)
                    self.inc("v", v.scalar_tensor_tensor(
                        out=self.nmr_all[:, col:col + 1], in0=lnt[:, 2:3], scalar=-1.0,
                        in1=self.rstd_all[:, col:col + 1], op0=ALU.mult, op1=ALU.mult))
                blk.vector(s_nmr)

                def s_norm(v, xt=xt, col=col):
                    self.sep(v)
                    self.inc("v", v.tensor_scalar(
                        out=xt, in0=xt, scalar1=self.rstd_all[:, col:col + 1],
                        scalar2=self.nmr_all[:, col:col + 1],
                        op0=ALU.mult, op1=ALU.add))
                blk.vector(s_norm)

                for dch in range(DCH):
                    pbank = (tt * DCH + dch) % 2
                    ps = ptA if pbank == 0 else ptB

                    def s_tp(t, xt=xt, dch=dch, ps=ps, g=bank_guard[pbank]):
                        self.wt(t, "v")
                        self.wt(t, "a", g)
                        self.inc("mm", t.transpose(
                            ps[:, :], xt[:, dch * 128:(dch + 1) * 128],
                            self.eye[:, :]))
                    blk.tensor(s_tp)

                    def s_ev(sc, tt=tt, dch=dch, ps=ps):
                        self.wt(sc, "mm")
                        self.inc("a", sc.activation(
                            out=xnT[dch][:, tt * 128:(tt + 1) * 128],
                            in_=ps[:, :], func=ACT.Copy))
                    blk.scalar(s_ev)
                    bank_guard[pbank] = self.n("a")
                slot_guard[slot] = self.n("mm")

            def s_spill(sy):
                self.wt(sy, "a")
                for dch in range(DCH):
                    self.inc("st", sy.dma_start(out=self.xnT_s[b, dch, :, :],
                                                in_=xnT[dch][:, :]), 16)
            blk.sync(s_spill)

        with (
            nc.sbuf_tensor(f"kfl{b}", [128, 16 * 128], F32) as kfl,
            nc.sbuf_tensor(f"bqr{b}", [128, C], F32) as bqr,
            nc.sbuf_tensor(f"bkr{b}", [128, C], F32) as bkr,
            nc.sbuf_tensor(f"qkf{b}", [128, C], F32) as qkf,
            nc.sbuf_tensor(f"kkf{b}", [128, C], F32) as kkf,
            nc.sbuf_tensor(f"dotb{b}", [128, C + 32], F32) as dotb,
            nc.psum_tensor(f"pqA{b}", [128, 512], F32) as pqA,
            nc.psum_tensor(f"pqB{b}", [128, 512], F32) as pqB,
            nc.Block(f"b1_kf{b}") as blk,
        ):
            def s_brep(sy):
                self.inc("ld", sy.dma_start(out=bqr[:, :], in_=self.bqkv[0:C].unsqueeze(0).to_broadcast([128, C])), 16)
                self.inc("ld", sy.dma_start(out=bkr[:, :], in_=self.bqkv[C:2 * C].unsqueeze(0).to_broadcast([128, C])), 16)
            blk.sync(s_brep)

            def s_kfl(v):
                for kft in range(2):
                    for dch in range(DCH):
                        lhs = xnT[dch][:, :].rearrange(
                            "p (g r) -> p g r", g=4)[:, kft * 2:(kft + 1) * 2, 0:64]
                        v.tensor_copy(
                            out=kfl[:, (kft * DCH + dch) * 128:(kft * DCH + dch + 1) * 128],
                            in_=lhs)
                self.inc("v", v.tensor_copy(out=kfl[:, 0:1], in_=kfl[:, 0:1]))
            blk.vector(s_kfl)
            for kft in range(2):
                for qk in range(2):
                    dst = qkf if qk == 0 else kkf
                    rep = bqr if qk == 0 else bkr
                    for half in range(2):
                        ps = pqA if half == 0 else pqB

                        def s_mm(t, kft=kft, qk=qk, half=half, ps=ps):
                            self.wt(t, "ld")
                            self.wt(t, "v")
                            for dch in range(DCH):
                                lhs = kfl[:, (kft * DCH + dch) * 128:(kft * DCH + dch + 1) * 128]
                                self.inc("mm", t.matmul(
                                    ps[:, :], lhs,
                                    wqk[dch][:, qk * C + half * 512: qk * C + (half + 1) * 512],
                                    start=(dch == 0), stop=(dch == DCH - 1)))
                        blk.tensor(s_mm)

                        def s_ev(v, half=half, ps=ps, dst=dst, rep=rep):
                            self.wt(v, "mm")
                            self.inc("v", v.tensor_add(
                                dst[:, half * 512:(half + 1) * 512], ps[:, :],
                                rep[:, half * 512:(half + 1) * 512]))
                        blk.vector(s_ev)

                def s_dot(v, kft=kft):
                    self.sep(v)
                    v.tensor_mul(dotb[:, 0:C], qkf[:, :], kkf[:, :])
                    self.sep(v)
                    v.reduce_sum(out=dotb[:, C:C + 16],
                                 in_=dotb[:, 0:C].rearrange("p (h d) -> p h d", h=H),
                                 axis=AX.X)
                    self.sep(v)
                    v.reduce_max(out=dotb[:, C + 16:C + 17], in_=dotb[:, C:C + 16],
                                 axis=AX.X)
                    self.sep(v)
                    self.inc("v", v.tensor_add(self.bnstat[:, 32 + kft:33 + kft],
                                               self.bnstat[:, 32 + kft:33 + kft],
                                               dotb[:, C + 16:C + 17]))
                blk.vector(s_dot)
            self.tap(blk, f"qkf_{b}", qkf[:, :])
            self.tap(blk, f"kkf_{b}", kkf[:, :])
            self.tap(blk, f"dotb_{b}", dotb[:, :])

        with (
            nc.sbuf_tensor(f"wqjc{b}", [128, 2 * DCH * 128], F32) as wqjc,
            nc.sbuf_tensor(f"qev{b}", [128, 2 * 512], F32) as qev,
            nc.sbuf_tensor(f"qsq{b}", [128, 512], F32) as qsq,
            nc.psum_tensor(f"pfA{b}", [128, 512], F32) as pfA,
            nc.psum_tensor(f"pfB{b}", [128, 512], F32) as pfB,
            nc.Block(f"b1_q{b}") as blk,
        ):
            ev_guard = [0, 0]
            w_guard = [0, 0]
            for jc in range(16):
                wslot = jc % 2

                def s_wload(sy, jc=jc, wslot=wslot, g=w_guard[wslot]):
                    self.wt(sy, "mm", g)
                    src = self.wq_in.rearrange("(dc p) (jcc j) -> p dc jcc j",
                                               dc=DCH, jcc=16)[:, :, jc, :]
                    self.inc("ld", sy.dma_start(
                        out=wqjc[:, wslot * DCH * 128:(wslot + 1) * DCH * 128],
                        in_=src), 16)
                blk.sync(s_wload)

                for t4 in range(4):
                    ps = pfA if t4 % 2 == 0 else pfB
                    eslot = t4 % 2

                    def s_mm(t, jc=jc, t4=t4, ps=ps, wslot=wslot):
                        self.wt(t, "ld")
                        self.wt(t, "a")
                        for dch in range(DCH):
                            self.inc("mm", t.matmul(
                                ps[:, :],
                                wqjc[:, wslot * DCH * 128 + dch * 128:
                                     wslot * DCH * 128 + (dch + 1) * 128],
                                xnT[dch][:, t4 * 512:(t4 + 1) * 512],
                                start=(dch == 0), stop=(dch == DCH - 1)))
                    blk.tensor(s_mm)

                    def s_ev(sc, jc=jc, t4=t4, ps=ps, eslot=eslot, g=ev_guard[eslot]):
                        self.wt(sc, "mm")
                        self.wt(sc, "st", g)
                        acol = jc * 8 + b * 4 + t4
                        sc.activation(out=qev[:, eslot * 512:(eslot + 1) * 512],
                                      in_=ps[:, :], func=ACT.Copy,
                                      accum_out=self.bnacc[:, acol:acol + 1])
                        self.inc("a", sc.activation(
                            out=qsq[:, :], in_=qev[:, eslot * 512:(eslot + 1) * 512],
                            func=ACT.Square, accum_out=self.bnacc2[:, acol:acol + 1]))
                    blk.scalar(s_ev)

                    def s_spill(sy, jc=jc, t4=t4, eslot=eslot):
                        self.wt(sy, "a")
                        self.inc("st", sy.dma_start(
                            out=self.q_s[jc, :, b * N + t4 * 512: b * N + (t4 + 1) * 512],
                            in_=qev[:, eslot * 512:(eslot + 1) * 512]), 16)
                    blk.sync(s_spill)
                    ev_guard[eslot] = self.n("st")
                w_guard[wslot] = self.n("mm")

    def b2(self):
        nc = self.nc
        with (
            nc.sbuf_tensor("sc2", [2, 128], F32) as sc2,
            nc.sbuf_tensor("sc4", [4, 64], F32) as sc4,
            nc.sbuf_tensor("thr", [4, 80], F32) as thr,
            nc.sbuf_tensor("csA", [4, 64], F32) as csA,
            nc.sbuf_tensor("csB", [4, 64], F32) as csB,
            nc.sbuf_tensor("selT", [64, 8], F32) as selT,
            nc.sbuf_tensor("b2t", [128, 16], F32) as b2t,
            nc.psum_tensor("p_b2", [128, 128], F32) as p_b2,
            nc.psum_tensor("p_b2b", [128, 16], F32) as p_b2b,
            nc.Block("b2") as blk,
        ):
            def s_red(v):
                v.reduce_sum(out=self.bnstat[:, 0:16],
                             in_=self.bnacc[:, :].rearrange("p (jc t) -> p jc t", jc=16),
                             axis=AX.X)
                self.inc("v", v.reduce_sum(
                    out=self.bnstat[:, 16:32],
                    in_=self.bnacc2[:, :].rearrange("p (jc t) -> p jc t", jc=16),
                    axis=AX.X))
            blk.vector(s_red)

            def s_cc0(sy):
                self.wt(sy, "v")
                self.inc("st", sy.dma_start(out=self.cc_in[:, :], in_=self.bnstat[:, :]), 16)
            blk.sync(s_cc0)

            def s_cc(g):
                self.wt(g, "st")
                self.inc("cc", g.collective_compute(
                    "AllReduce", ALU.add,
                    replica_groups=[list(range(self.n_cores))],
                    ins=[self.cc_in[:, :]], outs=[self.cc_out[:, :]]))
            blk.gpsimd(s_cc)

            def s_cc1(sy):
                self.wt(sy, "cc")
                self.inc("ld", sy.dma_start(out=self.allr[:, :], in_=self.cc_out[:, :]), 16)
            blk.sync(s_cc1)

            def s_bn(v):
                self.wt(v, "ld")
                mean = b2t[:, 0:16]
                v.tensor_scalar_mul(mean, self.allr[:, 0:16], 1.0 / (NTOK * self.n_cores))
                v.tensor_scalar_mul(self.bn_b[:, :], self.allr[:, 16:32], 1.0 / (NTOK * self.n_cores))
                v.tensor_mul(self.bn_a[:, :], mean, mean)
                self.sep(v)
                v.tensor_sub(self.bn_b[:, :], self.bn_b[:, :], self.bn_a[:, :])
                self.sep(v)
                self.inc("v", v.tensor_scalar_add(self.bn_b[:, :], self.bn_b[:, :], BN_EPS))
            blk.vector(s_bn)

            def s_bn2(sc):
                self.wt(sc, "v")
                self.inc("a", sc.activation(out=self.bn_a[:, :], in_=self.bn_b[:, :],
                                            func=ACT.Sqrt))
            blk.scalar(s_bn2)

            def s_bn3(v):
                self.wt(v, "a")
                v.reciprocal(out=self.bn_a[:, :], in_=self.bn_a[:, :])
                self.sep(v)
                v.tensor_mul(self.bn_a[:, :], self.bn_a[:, :], self.gamma16[:, :])
                self.sep(v)
                v.tensor_mul(self.bn_b[:, :], b2t[:, 0:16], self.bn_a[:, :])
                self.sep(v)
                self.inc("v", v.tensor_sub(self.bn_b[:, :], self.beta16[:, :], self.bn_b[:, :]))
            blk.vector(s_bn3)

            def s_tr(t):
                self.wt(t, "v")
                self.inc("mm", t.transpose(p_b2[0:2, 0:128],
                                           self.allr[:, 32:34], self.eye[:, :]))
            blk.tensor(s_tr)

            def s_ev(sc):
                self.wt(sc, "mm")
                self.inc("a", sc.activation(out=sc2[:, :], in_=p_b2[0:2, 0:128],
                                            func=ACT.Copy))
            blk.scalar(s_ev)

            def s_mv(sy):
                self.wt(sy, "a")
                for kf in range(4):
                    self.inc("ld", sy.dma_start(
                        out=sc4[kf:kf + 1, :],
                        in_=sc2[kf // 2:kf // 2 + 1, (kf % 2) * 64:(kf % 2) * 64 + 64]), 16)
            blk.sync(s_mv)

            def s_thr(v):
                self.wt(v, "ld")
                v.max(out=thr[:, 0:8], in_=sc4[:, :])
                self.sep(v)
                v.match_replace(out=thr[:, 16:80], in_to_replace=thr[:, 0:8],
                                in_values=sc4[:, :], imm_value=-1e30)
                self.sep(v)
                v.max(out=thr[:, 8:16], in_=thr[:, 16:80])
                self.sep(v)
                v.match_replace(out=thr[:, 16:80], in_to_replace=thr[:, 8:16],
                                in_values=thr[:, 16:80], imm_value=-1e30)
                self.sep(v)
                v.max(out=thr[:, 0:8], in_=thr[:, 16:80])
                self.sep(v)
                v.tensor_scalar(out=csA[:, :], in0=sc4[:, :], scalar1=thr[:, 2:3],
                                scalar2=None, op0=ALU.is_ge)
                self.sep(v)
                self.inc("v", v.tensor_copy(out=csB[:, :], in_=csA[:, :]))
            blk.vector(s_thr)

            def s_cum(v):
                a, bb = csB, thr[:, 16:80]
                for sh in [1, 2, 4, 8, 16, 32]:
                    self.sep(v)
                    v.tensor_copy(out=bb[:, 0:sh], in_=a[:, 0:sh])
                    v.tensor_add(bb[:, sh:64], a[:, sh:64], a[:, 0:64 - sh])
                    a, bb = bb, a
                self.sep(v)
                self.inc("v", v.tensor_copy(out=csB[:, :], in_=a[:, :]))
            blk.vector(s_cum)

            def s_tr2(t):
                self.wt(t, "v")
                t.transpose(p_b2[0:64, 4:8], csB[:, :], self.eye[0:4, 0:4])
                self.inc("mm", t.transpose(p_b2b[0:64, 0:4], csA[:, :], self.eye[0:4, 0:4]))
            blk.tensor(s_tr2)

            def s_ev2(sc):
                self.wt(sc, "mm")
                sc.activation(out=selT[:, 0:4], in_=p_b2[0:64, 4:8], func=ACT.Copy)
                self.inc("a", sc.activation(out=selT[:, 4:8], in_=p_b2b[0:64, 0:4],
                                            func=ACT.Copy))
            blk.scalar(s_ev2)

            def s_sel(v):
                self.wt(v, "a")
                for ncl in range(K):
                    v.tensor_tensor(
                        out=self.sel2[0:64, ncl * TK:(ncl + 1) * TK],
                        in0=selT[:, ncl:ncl + 1].to_broadcast([64, TK]),
                        in1=self.iota19[:, :], op=ALU.is_equal)
                    self.sep(v)
                    v.tensor_scalar_mul(self.sel2[0:64, ncl * TK:(ncl + 1) * TK],
                                        self.sel2[0:64, ncl * TK:(ncl + 1) * TK],
                                        selT[:, 4 + ncl:5 + ncl])
                self.inc("v", v.tensor_copy(out=csA[:, :], in_=csA[:, :]))
            blk.vector(s_sel)

            def s_dup(sy):
                self.wt(sy, "v")
                self.inc("ld", sy.dma_start(out=self.sel2[64:128, :],
                                            in_=self.sel2[0:64, :]), 16)
            blk.sync(s_dup)
            self.tap(blk, "gamma16", self.gamma16[:, :])
            self.tap(blk, "beta16", self.beta16[:, :])
            self.tap(blk, "bqkv24", self.bqkv24[:, :])
            self.tap(blk, "sel2", self.sel2[:, :])
            self.tap(blk, "bn_a", self.bn_a[:, :])
            self.tap(blk, "bn_b", self.bn_b[:, :])
            self.tap(blk, "sc4", sc4[:, :])
            self.tap(blk, "csA", csA[:, :])
            self.tap(blk, "csB", csB[:, :])
            self.tap(blk, "selT", selT[:, :])
            self.tap(blk, "allr", self.allr[:, :])
            self.tap(blk, "b2t", b2t[:, :])
            self.tap(blk, "thr", thr[:, :])

    def b3(self, b):
        nc = self.nc
        es = contextlib.ExitStack()
        klocT = [es.enter_context(nc.sbuf_tensor(f"klocT{b}_{i}", [128, NSEL], F32)) for i in range(DCH)]
        vloc = [es.enter_context(nc.sbuf_tensor(f"vloc{b}_{i}", [128, C], F32)) for i in range(5)]
        bvr = es.enter_context(nc.sbuf_tensor(f"bvr{b}", [128, C], F32))
        bpr = es.enter_context(nc.sbuf_tensor(f"bpr{b}", [128, C], F32))
        es2 = contextlib.ExitStack()
        wbuf = [es2.enter_context(nc.sbuf_tensor(f"wb{b}_{i}", [128, C], F32)) for i in range(DCH)]
        xselT = [es2.enter_context(nc.sbuf_tensor(f"xselT{b}_{i}", [128, NSEL], F32)) for i in range(DCH)]
        with nc.Block(f"b3pre{b}") as blk:
            def s_brep(sy):
                self.inc("ld", sy.dma_start(out=bvr[:, :], in_=self.bqkv[2 * C:3 * C].unsqueeze(0).to_broadcast([128, C])), 16)
                self.inc("ld", sy.dma_start(out=bpr[:, :], in_=self.bproj[0:C].unsqueeze(0).to_broadcast([128, C])), 16)
            blk.sync(s_brep)

        with (
            nc.sbuf_tensor(f"xc{b}", [128, 2 * C], F32) as xc,
            nc.sbuf_tensor(f"xsf{b}", [32, 2 * C], F32) as xsf,
            nc.psum_tensor(f"pcmp{b}", [32, C], F32) as pcmp,
            nc.psum_tensor(f"ptrA{b}", [128, 32], F32) as ptrA,
            nc.psum_tensor(f"ptrB{b}", [128, 32], F32) as ptrB,
            nc.Block(f"b3s{b}") as blk,
        ):
            xc_guard = [0, 0]
            xs_guard = [0, 0]
            tr_guard = [0, 0]
            for tt in range(16):
                slot = tt % 2
                xt = xc[:, slot * C:(slot + 1) * C]

                def s_load(sy, tt=tt, xt=xt, g=xc_guard[slot]):
                    self.wt(sy, "mm", g)
                    self.inc("ld", sy.dma_start(out=xt, in_=self.x[b, tt * 128:(tt + 1) * 128, :]), 16)
                blk.sync(s_load)

                def s_norm(v, tt=tt, xt=xt):
                    self.wt(v, "ld")
                    col = b * 16 + tt
                    self.inc("v", v.tensor_scalar(
                        out=xt, in0=xt, scalar1=self.rstd_all[:, col:col + 1],
                        scalar2=self.nmr_all[:, col:col + 1],
                        op0=ALU.mult, op1=ALU.add))
                blk.vector(s_norm)

                for fr in range(2):
                    s = 2 * tt + fr
                    ncl = s // FC
                    fslot = s % 2

                    def s_cmp(t, xt=xt, fr=fr, ncl=ncl, fslot=fslot, g=xs_guard[fslot]):
                        self.wt(t, "v")
                        self.wt(t, "a", g)
                        for hcol in range(2):
                            self.inc("mm", t.matmul(
                                pcmp[0:TK, hcol * 512:(hcol + 1) * 512],
                                self.sel2[fr * 64:(fr + 1) * 64, ncl * TK:(ncl + 1) * TK],
                                xt[fr * 64:(fr + 1) * 64, hcol * 512:(hcol + 1) * 512],
                                start=True, stop=True))
                    blk.tensor(s_cmp)

                    def s_evc(sc, fslot=fslot):
                        self.wt(sc, "mm")
                        self.inc("a", sc.activation(
                            out=xsf[0:TK, fslot * C:(fslot + 1) * C],
                            in_=pcmp[0:TK, :], func=ACT.Copy))
                    blk.scalar(s_evc)

                    for dch in range(DCH):
                        tslot = dch % 2
                        ptr = ptrA if tslot == 0 else ptrB

                        def s_tp(t, dch=dch, fslot=fslot, ptr=ptr, g=tr_guard[tslot]):
                            self.wt(t, "a")
                            self.wt(t, "a", g)
                            self.inc("mm", t.transpose(
                                ptr[:, 0:TK],
                                xsf[0:TK, fslot * C + dch * 128: fslot * C + (dch + 1) * 128],
                                self.eye[0:TK, 0:TK]))
                        blk.tensor(s_tp)

                        def s_evt(sc, s=s, dch=dch, ptr=ptr):
                            self.wt(sc, "mm")
                            self.inc("a", sc.activation(
                                out=xselT[dch][:, s * TK:(s + 1) * TK],
                                in_=ptr[:, 0:TK], func=ACT.Copy))
                        blk.scalar(s_evt)
                        tr_guard[tslot] = self.n("a")
                    xs_guard[fslot] = self.n("a")
                xc_guard[slot] = self.n("mm")
            self.tap(blk, f"xselT0_{b}", xselT[0][:, :])

        with nc.Block(f"b3k{b}") as blk:
            def s_wk(sy):
                for dch in range(DCH):
                    self.inc("ld", sy.dma_start(out=wbuf[dch][:, :],
                                                in_=self.wqkv[dch * 128:(dch + 1) * 128, C:2 * C]), 16)
            blk.sync(s_wk)
        with (
            nc.psum_tensor(f"pk1{b}", [128, 512], F32) as pk1,
            nc.psum_tensor(f"pk2{b}", [128, 512], F32) as pk2,
            nc.Block(f"b3k2{b}") as blk,
        ):
            for hdc in range(DCH):
                for cpart, (c0, cl) in enumerate([(0, 512), (512, NSEL - 512)]):
                    ps = pk1 if cpart == 0 else pk2

                    def s_mm(t, hdc=hdc, c0=c0, cl=cl, ps=ps):
                        self.wt(t, "ld")
                        self.wt(t, "a")
                        for dch in range(DCH):
                            self.inc("mm", t.matmul(
                                ps[:, 0:cl], wbuf[dch][:, hdc * 128:(hdc + 1) * 128],
                                xselT[dch][:, c0:c0 + cl],
                                start=(dch == 0), stop=(dch == DCH - 1)))
                    blk.tensor(s_mm)

                    def s_ev(sc, hdc=hdc, c0=c0, cl=cl, ps=ps):
                        self.wt(sc, "mm")
                        self.inc("a", sc.activation(
                            out=klocT[hdc][:, c0:c0 + cl], in_=ps[:, 0:cl],
                            func=ACT.Identity, bias=self.bqkv24[:, 8 + hdc:9 + hdc]))
                    blk.scalar(s_ev)
        with nc.Block(f"b3v{b}") as blk:
            def s_wv(sy):
                self.wt(sy, "a")
                for dch in range(DCH):
                    self.inc("ld", sy.dma_start(out=wbuf[dch][:, :],
                                                in_=self.wqkv[dch * 128:(dch + 1) * 128, 2 * C:3 * C]), 16)
            blk.sync(s_wv)
        with (
            nc.psum_tensor(f"pv1{b}", [128, 512], F32) as pv1,
            nc.psum_tensor(f"pv2{b}", [128, 512], F32) as pv2,
            nc.Block(f"b3v2{b}") as blk,
        ):
            for st_ in range(5):
                rows = min(128, NSEL - st_ * 128)
                for half in range(2):
                    ps = pv1 if half == 0 else pv2

                    def s_mm(t, st_=st_, rows=rows, half=half, ps=ps):
                        self.wt(t, "ld")
                        self.wt(t, "v")
                        for dch in range(DCH):
                            self.inc("mm", t.matmul(
                                ps[0:rows, :], xselT[dch][:, st_ * 128:st_ * 128 + rows],
                                wbuf[dch][:, half * 512:(half + 1) * 512],
                                start=(dch == 0), stop=(dch == DCH - 1)))
                    blk.tensor(s_mm)

                    def s_ev(v, st_=st_, rows=rows, half=half, ps=ps):
                        self.wt(v, "mm")
                        self.inc("v", v.tensor_add(
                            vloc[st_][0:rows, half * 512:(half + 1) * 512],
                            ps[0:rows, :],
                            bvr[0:rows, half * 512:(half + 1) * 512]))
                    blk.vector(s_ev)
            self.tap(blk, f"kloc0_{b}", klocT[0][:, :])
            self.tap(blk, f"vloc0_{b}", vloc[0][:, :])

        es2.close()
        es3 = contextlib.ExitStack()
        wbuf = [es3.enter_context(nc.sbuf_tensor(f"wc{b}_{i}", [128, C], F32)) for i in range(DCH)]
        for ci in range(K):
            self._b3_ci(b, ci, klocT, vloc, wbuf, bpr)
        es3.close()
        es.close()

    def _b3_ci(self, b, ci, klocT, vloc, wbuf, bpr):
        nc = self.nc
        ranges = _kv_ranges(ci)
        L = sum(l for _, l in ranges)
        rtiles = _range_tiles(ci)
        poff = {}
        acc = 0
        for r, (off, ln) in enumerate(ranges):
            poff[r] = acc
            acc += ln
        with (
            nc.sbuf_tensor(f"qT{b}{ci}", [128, DCH * 512], F32) as qTc,
            nc.sbuf_tensor(f"xns{b}{ci}", [128, DCH * 512], F32) as xns,
            nc.sbuf_tensor(f"vr{b}{ci}", [128, len(rtiles) * C], F32) as vr,
            nc.sbuf_tensor(f"probs{b}{ci}", [128, 2 * 512], F32) as probs,
            nc.sbuf_tensor(f"probsT{b}{ci}", [128, 4 * 128], F32) as probsT,
            nc.sbuf_tensor(f"attn{b}{ci}", [128, 4 * C], F32) as attn,
            nc.sbuf_tensor(f"smax{b}{ci}", [128, 16], F32) as smax,
            nc.sbuf_tensor(f"rs{b}{ci}", [128, 8], F32) as rs,
            nc.sbuf_tensor(f"xres{b}{ci}", [128, 2 * C], F32) as xres,
            nc.psum_tensor(f"plA{b}{ci}", [128, 512], F32) as plA,
            nc.psum_tensor(f"plB{b}{ci}", [128, 512], F32) as plB,
            nc.psum_tensor(f"ptA{b}{ci}", [128, 128], F32) as ptA,
            nc.psum_tensor(f"ptB{b}{ci}", [128, 128], F32) as ptB,
            nc.psum_tensor(f"ptC{b}{ci}", [128, 128], F32) as ptC,
            nc.psum_tensor(f"ptD{b}{ci}", [128, 128], F32) as ptD,
            nc.psum_tensor(f"paA{b}{ci}", [128, 64], F32) as paA,
            nc.psum_tensor(f"paB{b}{ci}", [128, 64], F32) as paB,
            nc.Block(f"b3c{b}{ci}") as blk,
        ):
            def s_ld(sy):
                self.wt(sy, "v")
                self.wt(sy, "a")
                for dch in range(DCH):
                    self.inc("ld", sy.dma_start(
                        out=xns[:, dch * 512:(dch + 1) * 512],
                        in_=self.xnT_s[b, dch, :, ci * 512:(ci + 1) * 512]), 16)
                    self.inc("ld", sy.dma_start(
                        out=wbuf[dch][:, :],
                        in_=self.wqkv[dch * 128:(dch + 1) * 128, 0:C]), 16)
            blk.sync(s_ld)

            def s_vr(sy):
                self.wt(sy, "mm")
                self.wt(sy, "v")
                for ti, (r, bi, soff, nr) in enumerate(rtiles):
                    for (stile, srow, drow, n2) in _asm_pieces(soff, nr):
                        self.inc("ld2", sy.dma_start(
                            out=vr[:, ti * C:(ti + 1) * C][drow:drow + n2, :],
                            in_=vloc[stile][srow:srow + n2, :]), 16)
            blk.scalar(s_vr)

            pg = [pk for pk in (plA, plB)]
            for hdc in range(DCH):
                ps = pg[hdc % 2]

                def s_mm(t, hdc=hdc, ps=ps):
                    self.wt(t, "ld")
                    self.wt(t, "a")
                    for dch in range(DCH):
                        self.inc("mm", t.matmul(
                            ps[:, :], wbuf[dch][:, hdc * 128:(hdc + 1) * 128],
                            xns[:, dch * 512:(dch + 1) * 512],
                            start=(dch == 0), stop=(dch == DCH - 1)))
                blk.tensor(s_mm)

                def s_ev(sc, hdc=hdc, ps=ps):
                    self.wt(sc, "mm")
                    self.inc("a", sc.activation(
                        out=qTc[:, hdc * 512:(hdc + 1) * 512], in_=ps[:, :],
                        func=ACT.Identity, bias=self.bqkv24[:, hdc:hdc + 1]))
                blk.scalar(s_ev)

            a0 = self.n("a")
            gd = dict(pa=[0, 0], pl=[a0, a0], pt=[0, 0, 0, 0],
                      pr=[0, 0], prT=[0, 0, 0, 0], qkmm=[0, 0, 0, 0])
            pls = [plA, plB]
            pts = [ptA, ptB, ptC, ptD]
            units = [(h, qt) for h in range(H) for qt in range(4)]
            self._b3_qk(blk, b, ci, 0, 0, gd, ranges, poff, qTc, klocT, pls[0])
            for i, (h, qt) in enumerate(units):
                if i + 1 < len(units):
                    h2, qt2 = units[i + 1]
                    self._b3_qk(blk, b, ci, h2, qt2, gd, ranges, poff,
                                qTc, klocT, pls[(i + 1) % 2])
                self._b3_sm(blk, b, ci, h, qt, gd, ranges, poff, L, rtiles,
                            vr, probs, probsT, attn, smax, rs,
                            pls[i % 2], pts, paA, paB)
            pt_guard = [0, 0]

            def s_wp(sy):
                self.wt(sy, "mm")
                for dch in range(DCH):
                    self.inc("ld", sy.dma_start(out=wbuf[dch][:, :],
                                                in_=self.wproj[dch * 128:(dch + 1) * 128, :]), 16)
            blk.sync(s_wp)

            attnT = xns
            res_guard = [0, 0]
            pp_guard = [0, 0]
            pt_guard = [0, 0]
            for qt in range(4):
                for dch in range(DCH):
                    pt = ptA if dch % 2 == 0 else ptB

                    def s_tp(t, qt=qt, dch=dch, pt=pt, g=pt_guard[dch % 2]):
                        self.wt(t, "v")
                        self.wt(t, "a", g)
                        self.inc("mm", t.transpose(
                            pt[:, :], attn[:, qt * C + dch * 128: qt * C + (dch + 1) * 128],
                            self.eye[:, :]))
                    blk.tensor(s_tp)

                    def s_evt(sc, dch=dch, pt=pt):
                        self.wt(sc, "mm")
                        self.inc("a", sc.activation(
                            out=attnT[:, dch * 128:(dch + 1) * 128], in_=pt[:, :],
                            func=ACT.Copy))
                    blk.scalar(s_evt)
                    pt_guard[dch % 2] = self.n("a")

                if qt == 0:
                    self.tap(blk, f"attnT0_{b}{ci}", attnT[:, 0:DCH * 128])

                def s_xld(sy, qt=qt, g=res_guard[qt % 2]):
                    self.wt(sy, "st", g)
                    self.inc("ld", sy.dma_start(
                        out=xres[:, (qt % 2) * C:(qt % 2 + 1) * C],
                        in_=self.x[b, ci * 512 + qt * 128: ci * 512 + (qt + 1) * 128, :]), 16)
                blk.sync(s_xld)

                for half in range(2):
                    pp = plA if half == 0 else plB

                    def s_pj(t, qt=qt, half=half, pp=pp, g=pp_guard[half]):
                        self.wt(t, "ld")
                        self.wt(t, "a")
                        self.wt(t, "v", g)
                        for dch in range(DCH):
                            self.inc("mm", t.matmul(
                                pp[:, :], attnT[:, dch * 128:(dch + 1) * 128],
                                wbuf[dch][:, half * 512:(half + 1) * 512],
                                start=(dch == 0), stop=(dch == DCH - 1)))
                    blk.tensor(s_pj)

                    def s_res(v, qt=qt, half=half, pp=pp):
                        self.wt(v, "mm")
                        self.wt(v, "ld")
                        v.tensor_add(
                            xres[:, (qt % 2) * C + half * 512:(qt % 2) * C + (half + 1) * 512],
                            xres[:, (qt % 2) * C + half * 512:(qt % 2) * C + (half + 1) * 512],
                            bpr[:, half * 512:(half + 1) * 512])
                        self.sep(v)
                        self.inc("v", v.tensor_add(
                            xres[:, (qt % 2) * C + half * 512:(qt % 2) * C + (half + 1) * 512],
                            xres[:, (qt % 2) * C + half * 512:(qt % 2) * C + (half + 1) * 512],
                            pp[:, :]))
                    blk.vector(s_res)

                pp_guard[0] = pp_guard[1] = self.n("v")
                if qt == 0:
                    self.tap(blk, f"xres0_{b}{ci}", xres[:, 0:C])

                def s_sp(sy, qt=qt):
                    self.wt(sy, "v")
                    self.inc("st", sy.dma_start(
                        out=self.resid_s[b, ci * 512 + qt * 128: ci * 512 + (qt + 1) * 128, :],
                        in_=xres[:, (qt % 2) * C:(qt % 2 + 1) * C]), 16)
                blk.sync(s_sp)
                res_guard[qt % 2] = self.n("st")
            if ci == 0:
                self.tap(blk, f"qT0_{b}", qTc[:, :])
                self.tap(blk, f"attn0_{b}", attn[:, :])


    def _b3_qk(self, blk, b, ci, h, qt, gd, ranges, poff, qTc, klocT, pl):
        chunk, hof = h // 2, (h % 2) * 64
        pslot = (h * 4 + qt) % 2

        def s_qk(t, g=gd["pl"][pslot]):
            self.wt(t, "a", g)
            for r, (off, ln) in enumerate(ranges):
                self.inc("mm", t.matmul(
                    pl[:, poff[r]:poff[r] + ln],
                    qTc[:, chunk * 512 + qt * 128: chunk * 512 + (qt + 1) * 128][hof:hof + 64, :],
                    klocT[chunk][hof:hof + 64, off:off + ln],
                    start=(r == 0), stop=(r == len(ranges) - 1)))
        blk.tensor(s_qk)
        gd["qkmm"][(h * 4 + qt) % 4] = self.n("mm")

    def _b3_sm(self, blk, b, ci, h, qt, gd, ranges, poff, L, rtiles,
               vr, probs, probsT, attn, smax, rs, pl, pts, paA, paB):
        pa = paA if qt % 2 == 0 else paB
        uslot = (h * 4 + qt) % 2
        npieces = len(rtiles)

        def s_max(v, g=gd["qkmm"][(h * 4 + qt) % 4]):
            self.wt(v, "mm", g)
            v.reduce_max(out=smax[:, qt:qt + 1], in_=pl[:, 0:L],
                         axis=AX.X, negate=True)
            self.sep(v)
            self.inc("v", v.tensor_scalar_mul(smax[:, qt:qt + 1],
                                              smax[:, qt:qt + 1], SCALE))
        blk.vector(s_max)
        smax_v = self.n("v")

        def s_exp(sc, g=gd["pr"][uslot], smax_v=smax_v):
            self.wt(sc, "v", smax_v)
            self.wt(sc, "mm", g)
            self.inc("a", sc.activation(
                out=probs[:, uslot * 512: uslot * 512 + L], in_=pl[:, 0:L],
                func=ACT.Exp, scale=SCALE, bias=smax[:, qt:qt + 1],
                accum_out=rs[:, qt:qt + 1]))
        blk.scalar(s_exp)
        gd["pl"][uslot] = self.n("a")

        evac_cnt = [0] * npieces

        def emit_tp(pi):
            r, bi, soff, nr = rtiles[pi]
            pk0 = poff[r] + bi * 128
            tslot = pi % 4
            pt = pts[tslot]

            def s_tp(t, pk0=pk0, nr=nr, pt=pt, g=gd["pt"][tslot]):
                self.wt(t, "a")
                self.wt(t, "v", g)
                self.inc("mm", t.transpose(
                    pt[0:nr, 0:128],
                    probs[:, uslot * 512 + pk0: uslot * 512 + pk0 + nr],
                    self.eye[:, :]))
            blk.tensor(s_tp)

            def s_evt(v, nr=nr, pt=pt, tslot=tslot, g=gd["prT"][tslot]):
                self.wt(v, "mm")
                self.wt(v, "mm", g)
                self.inc("v", v.tensor_copy(
                    out=probsT[0:nr, tslot * 128:tslot * 128 + 128],
                    in_=pt[0:nr, 0:128]))
            blk.vector(s_evt)
            gd["pt"][tslot] = self.n("v")
            evac_cnt[pi] = self.n("v")

        def emit_pv(pi):
            r, bi, soff, nr = rtiles[pi]
            tslot = pi % 4

            def s_pv(t, pi=pi, nr=nr, ti=pi, tslot=tslot, g=gd["pa"][qt % 2],
                     ev=evac_cnt[pi]):
                self.wt(t, "v", ev)
                if pi == 0:
                    self.wt(t, "v", g)
                    self.wt(t, "ld2")
                self.inc("mm", t.matmul(
                    pa[:, 0:64],
                    probsT[0:nr, tslot * 128:tslot * 128 + 128],
                    vr[:, ti * C:(ti + 1) * C][0:nr, h * 64:(h + 1) * 64],
                    start=(pi == 0), stop=(pi == npieces - 1)))
            blk.tensor(s_pv)
            gd["prT"][tslot] = self.n("mm")

        for pi in range(min(4, npieces)):
            emit_tp(pi)
        for pi in range(npieces):
            emit_pv(pi)
            if pi + 4 < npieces:
                emit_tp(pi + 4)
        gd["pr"][uslot] = self.n("mm")
        pv_end = self.n("mm")
        exp_a = self.n("a")

        def s_attn_ev(v, pv_end=pv_end, exp_a=exp_a):
            self.wt(v, "a", exp_a)
            self.wt(v, "mm", pv_end)
            v.reciprocal(out=rs[:, 4 + qt:5 + qt], in_=rs[:, qt:qt + 1])
            v.tensor_copy(out=attn[:, qt * C + h * 64: qt * C + (h + 1) * 64],
                          in_=pa[:, 0:64])
            self.sep(v)
            self.inc("v", v.tensor_scalar_mul(
                attn[:, qt * C + h * 64: qt * C + (h + 1) * 64],
                attn[:, qt * C + h * 64: qt * C + (h + 1) * 64],
                rs[:, 4 + qt:5 + qt]))
        blk.vector(s_attn_ev)
        gd["pa"][qt % 2] = self.n("v")

    def b4(self):
        nc = self.nc
        with (
            nc.sbuf_tensor("qf", [128, 2 * 2048], F32) as qf,
            nc.sbuf_tensor("simb", [128, 2 * 4096], F32) as simb,
            nc.sbuf_tensor("tkb", [128, 512], F32) as tkb,
            nc.sbuf_tensor("tku", [128, 128], U32) as tku,
            nc.psum_tensor("psA", [128, 256], F32) as psA,
            nc.psum_tensor("psB", [128, 256], F32) as psB,
            nc.Block("b4") as blk,
        ):
            qf_guard = [0, 0]
            sim_guard = [0, 0]
            ps_guard = [0, 0]
            for tg in range(NTILE):
                b, tt = tg // 16, tg % 16
                qslot = tg % 2
                sslot = tg % 2

                def s_ld(sy, tg=tg, qslot=qslot, g=qf_guard[qslot]):
                    self.wt(sy, "v", g)
                    for jc in range(16):
                        self.inc("ld", sy.dma_start(
                            out=qf[:, qslot * 2048 + jc * 128: qslot * 2048 + (jc + 1) * 128],
                            in_=self.q_s[jc, :, tg * 128:(tg + 1) * 128]), 16)
                blk.sync(s_ld)

                def s_bn(v, qslot=qslot):
                    self.wt(v, "ld")
                    for jc in range(16):
                        sl = qf[:, qslot * 2048 + jc * 128: qslot * 2048 + (jc + 1) * 128]
                        ins = v.tensor_scalar(out=sl, in0=sl,
                                              scalar1=self.bn_a[:, jc:jc + 1],
                                              scalar2=self.bn_b[:, jc:jc + 1],
                                              op0=ALU.mult, op1=ALU.add)
                    self.inc("v", ins)
                blk.vector(s_bn)
                bn_v = self.n("v")

                for jc in range(16):
                    ps = psA if jc % 2 == 0 else psB

                    def s_mm(t, jc=jc, qslot=qslot, ps=ps, g=ps_guard[jc % 2],
                             bn_v=bn_v):
                        self.wt(t, "v", bn_v)
                        self.wt(t, "v", g)
                        self.inc("mm", t.matmul(
                            ps[:, :],
                            qf[:, qslot * 2048 + jc * 128: qslot * 2048 + (jc + 1) * 128],
                            self.keys_sb[:, jc * NKEY:(jc + 1) * NKEY],
                            start=True, stop=True))
                    blk.tensor(s_mm)

                    def s_ev(v, jc=jc, sslot=sslot, ps=ps,
                             g=sim_guard[sslot] if jc == 0 else None):
                        self.wt(v, "mm")
                        if g is not None:
                            self.wt(v, "v", g)
                        self.inc("v", v.tensor_copy(
                            out=simb[:, sslot * 4096 + jc * NKEY: sslot * 4096 + (jc + 1) * NKEY],
                            in_=ps[:, :]))
                    blk.vector(s_ev)
                    ps_guard[jc % 2] = self.n("v")
                qf_guard[qslot] = self.n("v")

                for hp in range(4):
                    chains = [self._b4_chain(tg, 2 * hp + par, sslot, par, simb, tkb, tku)
                              for par in range(2)]

                    def s_tk1(v, chains=chains):
                        self.wt(v, "v")
                        for opa, opb in zip(chains[0][0], chains[1][0]):
                            opa(v)
                            opb(v)
                        self.inc("v", v.tensor_copy(out=tkb[:, 248:249], in_=tkb[:, 152:153]))
                    blk.vector(s_tk1)

                    def s_tk2(sc, chains=chains):
                        self.wt(sc, "v")
                        for par in range(2):
                            o = 256 * par
                            ins = sc.activation(out=tkb[:, o + 160:o + 168],
                                                in_=tkb[:, o + 96:o + 104],
                                                func=ACT.Exp, bias=tkb[:, o + 152:o + 153],
                                                accum_out=tkb[:, o + 153:o + 154])
                        self.inc("a", ins)
                    blk.scalar(s_tk2)

                    def s_tk3(v, chains=chains, tg=tg, hp=hp):
                        self.wt(v, "a")
                        for par in range(2):
                            o = 256 * par
                            v.reciprocal(out=tkb[:, o + 154:o + 155],
                                         in_=tkb[:, o + 153:o + 154])
                        for par in range(2):
                            o = 256 * par
                            hf = 2 * hp + par
                            self.inc("v", v.tensor_scalar_mul(
                                self.g_all[:, tg * 64 + hf * 8: tg * 64 + hf * 8 + 8],
                                tkb[:, o + 160:o + 168], tkb[:, o + 154:o + 155]))
                    blk.vector(s_tk3)
                sim_guard[sslot] = self.n("v")
            self.tap(blk, "idx_all", self.idx_all[:, :])
            self.tap(blk, "g_all", self.g_all[:, :])

    def _b4_chain(self, tg, hf, sslot, par, simb, tkb, tku):
        o = 256 * par
        u = 64 * par
        sx = simb[:, sslot * 4096 + hf * NKEY: sslot * 4096 + (hf + 1) * NKEY]
        sy_ = simb[:, sslot * 4096 + (8 + hf) * NKEY: sslot * 4096 + (9 + hf) * NKEY]
        T = tkb
        U = tku
        ops = [
            lambda v: v.max(out=T[:, o:o + 8], in_=sx),
            lambda v: v.max(out=T[:, o + 8:o + 16], in_=sy_),
            lambda v: v.max_index(out=U[:, u:u + 8], in_max=T[:, o:o + 8], in_values=sx),
            lambda v: v.max_index(out=U[:, u + 8:u + 16], in_max=T[:, o + 8:o + 16], in_values=sy_),
            lambda v: v.tensor_copy(out=T[:, o + 16:o + 24], in_=U[:, u:u + 8]),
            lambda v: v.tensor_copy(out=T[:, o + 24:o + 32], in_=U[:, u + 8:u + 16]),
            lambda v: v.tensor_tensor(
                out=T[:, o + 32:o + 96].rearrange("p (i j) -> p i j", i=8),
                in0=T[:, o:o + 8].unsqueeze(2).to_broadcast([128, 8, 8]),
                in1=T[:, o + 8:o + 16].unsqueeze(1).to_broadcast([128, 8, 8]),
                op=ALU.add),
            lambda v: v.max(out=T[:, o + 96:o + 104], in_=T[:, o + 32:o + 96]),
            lambda v: v.max_index(out=U[:, u + 16:u + 24], in_max=T[:, o + 96:o + 104],
                                  in_values=T[:, o + 32:o + 96]),
            lambda v: v.tensor_scalar(out=U[:, u + 24:u + 32], in0=U[:, u + 16:u + 24],
                                      scalar1=3, scalar2=None,
                                      op0=ALU.logical_shift_right),
            lambda v: v.tensor_scalar(out=U[:, u + 32:u + 40], in0=U[:, u + 16:u + 24],
                                      scalar1=7, scalar2=None, op0=ALU.bitwise_and),
            lambda v: v.tensor_copy(out=T[:, o + 104:o + 112], in_=U[:, u + 24:u + 32]),
            lambda v: v.tensor_copy(out=T[:, o + 112:o + 120], in_=U[:, u + 32:u + 40]),
            lambda v: v.tensor_tensor(
                out=T[:, o + 128:o + 192].rearrange("p (s j) -> p s j", s=8),
                in0=T[:, o + 104:o + 112].unsqueeze(2).to_broadcast([128, 8, 8]),
                in1=self.iota8[:, :].unsqueeze(1).to_broadcast([128, 8, 8]),
                op=ALU.is_equal),
            lambda v: v.tensor_tensor(
                out=T[:, o + 192:o + 256].rearrange("p (s j) -> p s j", s=8),
                in0=T[:, o + 128:o + 192].rearrange("p (s j) -> p s j", s=8),
                in1=T[:, o + 16:o + 24].unsqueeze(1).to_broadcast([128, 8, 8]),
                op=ALU.mult),
            lambda v: v.reduce_sum(out=T[:, o + 120:o + 128],
                                   in_=T[:, o + 192:o + 256].rearrange("p (s j) -> p s j", s=8),
                                   axis=AX.X),
            lambda v: v.tensor_tensor(
                out=T[:, o + 128:o + 192].rearrange("p (s j) -> p s j", s=8),
                in0=T[:, o + 112:o + 120].unsqueeze(2).to_broadcast([128, 8, 8]),
                in1=self.iota8[:, :].unsqueeze(1).to_broadcast([128, 8, 8]),
                op=ALU.is_equal),
            lambda v: v.tensor_tensor(
                out=T[:, o + 192:o + 256].rearrange("p (s j) -> p s j", s=8),
                in0=T[:, o + 128:o + 192].rearrange("p (s j) -> p s j", s=8),
                in1=T[:, o + 24:o + 32].unsqueeze(1).to_broadcast([128, 8, 8]),
                op=ALU.mult),
            lambda v: v.reduce_sum(out=T[:, o + 136:o + 144],
                                   in_=T[:, o + 192:o + 256].rearrange("p (s j) -> p s j", s=8),
                                   axis=AX.X),
            lambda v: v.scalar_tensor_tensor(out=T[:, o + 144:o + 152], in0=T[:, o + 120:o + 128],
                                             scalar=256.0, in1=T[:, o + 136:o + 144],
                                             op0=ALU.mult, op1=ALU.add),
            lambda v: v.tensor_copy(
                out=self.idx_all[:, tg * 64 + hf * 8: tg * 64 + hf * 8 + 8],
                in_=T[:, o + 144:o + 152]),
            lambda v: v.tensor_scalar_mul(T[:, o + 152:o + 153], T[:, o + 96:o + 97], -1.0),
        ]
        return (ops,)

    def b5(self):
        nc = self.nc
        acc_vcount = {}
        gcnt = {}
        with (
            nc.sbuf_tensor("gring", [128, NBUF * 2048], F32) as gring,
            nc.sbuf_tensor("xn5", [128, 2 * C], F32) as xn5,
            nc.sbuf_tensor("acc5", [128, 2 * C], F32) as acc5,
            nc.sbuf_tensor("scr5", [128, C], F32) as scr5,
            nc.sbuf_tensor("hb5", [128, 32], F32) as hb5,
            nc.Block("b5") as blk,
        ):
            xn_guard = [0, 0]
            acc_guard = [0, 0]
            for tg in range(NTILE):
                b, tt = tg // 16, tg % 16
                slot = tg % 2

                def s_ld(sy, tg=tg, b=b, tt=tt, slot=slot, gx=xn_guard[slot],
                         ga=acc_guard[slot]):
                    self.wt(sy, "v", gx)
                    self.inc("ld", sy.dma_start(
                        out=xn5[:, slot * C:(slot + 1) * C],
                        in_=self.x[b, tt * 128:(tt + 1) * 128, :]), 16)
                    self.wt(sy, "st", ga)
                    self.inc("ld", sy.dma_start(
                        out=acc5[:, slot * C:(slot + 1) * C],
                        in_=self.resid_s[b, tt * 128:(tt + 1) * 128, :]), 16)
                blk.sync(s_ld)

                def s_norm(v, tg=tg, slot=slot):
                    self.wt(v, "ld")
                    self.inc("v", v.tensor_scalar(
                        out=xn5[:, slot * C:(slot + 1) * C],
                        in0=xn5[:, slot * C:(slot + 1) * C],
                        scalar1=self.rstd_all[:, tg:tg + 1],
                        scalar2=self.nmr_all[:, tg:tg + 1],
                        op0=ALU.mult, op1=ALU.add))
                blk.vector(s_norm)

                def emit_gathers(grp, tg=tg):
                    for i in range(4):
                        gk = tg * 64 + grp * 4 + i
                        buf = gk % NBUF

                        def s_g(g, gk=gk, buf=buf,
                                gv=acc_vcount.get(gk - NBUF, 0)):
                            if gv:
                                self.wt(g, "v", gv)
                            self.inc("g", g.indirect_dma_start(
                                gring[:, buf * 2048:(buf + 1) * 2048], None,
                                self.tbl[:, :],
                                bass.IndirectOffsetOnAxis(
                                    self.idx_all[:, gk:gk + 1], 0)), 16)
                        blk.gpsimd(s_g)
                        gcnt[gk] = self.n("g")

                def dot(v, grp, i, slot=slot, tg=tg):
                    gk = tg * 64 + grp * 4 + i
                    buf = gk % NBUF
                    hcol = (grp % 4) * 4 + i
                    self.wt(v, "g", gcnt[gk])
                    self.inc("v", v.scalar_tensor_tensor(
                        out=scr5[:, :],
                        in0=gring[:, buf * 2048: buf * 2048 + 1024],
                        scalar=1.0,
                        in1=xn5[:, slot * C:(slot + 1) * C],
                        op0=ALU.bypass, op1=ALU.mult,
                        accum_out=hb5[:, hcol:hcol + 1]))

                def acc(v, grp, i, slot=slot, tg=tg):
                    gk = tg * 64 + grp * 4 + i
                    buf = gk % NBUF
                    wcol = 16 + (grp % 4) * 4 + i
                    self.inc("v", v.scalar_tensor_tensor(
                        out=acc5[:, slot * C:(slot + 1) * C],
                        in0=gring[:, buf * 2048 + 1024: (buf + 1) * 2048],
                        scalar=hb5[:, wcol:wcol + 1],
                        in1=acc5[:, slot * C:(slot + 1) * C],
                        op0=ALU.mult, op1=ALU.add))
                    acc_vcount[gk] = self.n("v")

                def gelu(sc, grp):
                    hc = (grp % 4) * 4
                    self.wt(sc, "v")
                    self.inc("a", sc.activation(out=hb5[:, 16 + hc:20 + hc],
                                                in_=hb5[:, hc:hc + 4],
                                                func=ACT.Gelu_apprx_tanh))

                def wmul(v, grp, tg=tg):
                    hc = (grp % 4) * 4
                    self.wt(v, "a")
                    self.inc("v", v.tensor_mul(
                        hb5[:, 16 + hc:20 + hc], hb5[:, 16 + hc:20 + hc],
                        self.g_all[:, tg * 64 + grp * 4: tg * 64 + grp * 4 + 4]))

                emit_gathers(0)
                emit_gathers(1)

                def s_d01(v):
                    for i in range(4):
                        dot(v, 0, i)
                blk.vector(s_d01)
                blk.scalar(lambda sc: gelu(sc, 0))

                def s_d1(v):
                    for i in range(4):
                        dot(v, 1, i)
                blk.vector(s_d1)
                blk.scalar(lambda sc: gelu(sc, 1))

                for grp in range(2, 16):
                    emit_gathers(grp)

                    def s_body(v, grp=grp):
                        wmul(v, grp - 2)
                        for i in range(4):
                            dot(v, grp, i)
                            acc(v, grp - 2, i)
                    blk.vector(s_body)
                    blk.scalar(lambda sc, grp=grp: gelu(sc, grp))

                def s_tail(v):
                    wmul(v, 14)
                    for i in range(4):
                        self.sep(v)
                        acc(v, 14, i)
                    wmul(v, 15)
                    for i in range(4):
                        self.sep(v)
                        acc(v, 15, i)
                blk.vector(s_tail)

                def s_out(sy, tg=tg, b=b, tt=tt, slot=slot):
                    self.wt(sy, "v")
                    self.inc("st", sy.dma_start(
                        out=self.out[b, tt * 128:(tt + 1) * 128, :],
                        in_=acc5[:, slot * C:(slot + 1) * C]), 16)
                blk.sync(s_out)
                xn_guard[slot] = self.n("v")
                acc_guard[slot] = self.n("st")

            def s_fin(sy):
                self.wt(sy, "st")
                self.wt(sy, "st2")
            blk.sync(s_fin)

    def build(self):
        nc = self.nc
        self.b0()
        if self.stop_after == "b0":
            return self._fin()
        es1 = contextlib.ExitStack()
        xnT = [es1.enter_context(nc.sbuf_tensor(f"xnTc{i}", [128, N], F32)) for i in range(DCH)]
        wqk = [es1.enter_context(nc.sbuf_tensor(f"wqk{i}", [128, 2 * C], F32)) for i in range(DCH)]
        with nc.Block("b1w") as blk:
            def s_w(sy):
                for dch in range(DCH):
                    self.inc("ld", sy.dma_start(out=wqk[dch][:, :],
                                                in_=self.wqkv[dch * 128:(dch + 1) * 128, 0:2 * C]), 16)
            blk.sync(s_w)
        for b in range(NB):
            self.b1(b, wqk, xnT)
        es1.close()
        if self.stop_after == "b1":
            return self._fin()
        self.b2()
        if self.stop_after == "b2":
            return self._fin()
        for b in range(NB):
            self.b3(b)
        if self.stop_after == "b3":
            return self._fin()
        self.b4()
        if self.stop_after == "b4":
            return self._fin()
        self.b5()
        return self._fin()

    def _fin(self):
        nc = self.nc
        if self.taps:
            with nc.Block("tapw") as blk:
                def f(sy):
                    self.wt(sy, "st2")
                blk.sync(f)
        return nc


def build_nc(n_cores=NCORES, taps=(), stop_after=None):
    mk = MK(n_cores=n_cores, taps=taps, stop_after=stop_after)
    mk.build()
    return mk


def make_in_map(x_shard, inputs):
    keys_r = np.ascontiguousarray(
        np.transpose(np.asarray(inputs["L0_ffn_keys"], dtype=np.float32), (3, 2, 0, 1))
    ).reshape(128, 16, NKEY)
    tbl = np.concatenate([
        np.asarray(inputs["L0_ffn_down_embed"], dtype=np.float32),
        np.asarray(inputs["L0_ffn_up_embed"], dtype=np.float32)], axis=1)
    return {
        "x": np.ascontiguousarray(x_shard),
        "wqkv": np.ascontiguousarray(np.asarray(inputs["L0_attn_W_qkv"], np.float32)),
        "bqkv": np.ascontiguousarray(np.asarray(inputs["L0_attn_b_qkv"], np.float32)),
        "wproj": np.ascontiguousarray(np.asarray(inputs["L0_attn_W_proj"], np.float32)),
        "bproj": np.ascontiguousarray(np.asarray(inputs["L0_attn_b_proj"], np.float32)),
        "wq_in": np.ascontiguousarray(np.asarray(inputs["L0_ffn_w_q"], np.float32)),
        "gamma": np.ascontiguousarray(np.asarray(inputs["L0_ffn_bn_gamma"], np.float32)),
        "beta": np.ascontiguousarray(np.asarray(inputs["L0_ffn_bn_beta"], np.float32)),
        "keys": keys_r,
        "tbl": np.ascontiguousarray(tbl),
    }


def kernel(**inputs):
    x = np.asarray(inputs["L0_attn_x"], dtype=np.float32)
    mk = MK(n_cores=NCORES)
    mk.build()
    base = make_in_map(x[0:2], inputs)
    in_maps = []
    for c in range(NCORES):
        m = dict(base)
        m["x"] = np.ascontiguousarray(x[2 * c:2 * c + 2])
        in_maps.append(m)
    res = run_bass_kernel_spmd(mk.nc, in_maps, core_ids=list(range(NCORES)))
    out = np.concatenate([r["out"] for r in res.results], axis=0)
    return out.astype(np.float32)


# version 08 (final)
# speedup vs baseline: 1.5795x; 1.1099x over previous
import contextlib
import numpy as np
import concourse.bass as bass
import concourse.mybir as mybir
from concourse.bass_utils import run_bass_kernel_spmd

F32 = mybir.dt.float32
I32 = mybir.dt.int32
U32 = mybir.dt.uint32
AX = mybir.AxisListType
ALU = mybir.AluOpType
ACT = mybir.ActivationFunctionType

NCORES = 8
NB = 2
N = 2048
C = 1024
H = 16
D = 64
S = 32
P = 64
K = 4
FC = 8
TK = 19
NSUB = 5
SCALE = D ** -0.5
LN_EPS = 1e-5
NKEY = 256
NEXP = 65536
BN_EPS = 1e-5
NTOK = NB * N
NTILE = NTOK // 128
DCH = C // 128
NSEL = S * TK
NBUF = 16


def _kv_ranges(ci):
    out = []
    for ncl in range(K):
        nf = FC if (ncl == 0 or ncl == ci) else NSUB
        out.append((ncl * FC * TK, nf * TK))
    return out


def _range_tiles(ci):
    out = []
    for r, (off, ln) in enumerate(_kv_ranges(ci)):
        c0 = 0
        bi = 0
        while c0 < ln:
            nr = min(128, ln - c0)
            out.append((r, bi, off + c0, nr))
            c0 += nr
            bi += 1
    return out


def _asm_pieces(sel_off, nrows):
    out = []
    done = 0
    while done < nrows:
        g = sel_off + done
        nr = min(nrows - done, 128 - (g % 128))
        out.append((g // 128, g % 128, done, nr))
        done += nr
    return out


class MK:
    def __init__(self, n_cores=NCORES, taps=(), stop_after=None, dbg_dram=False):
        self.nc = bass.Bass("TRN2", target_bir_lowering=False,
                            detect_race_conditions=False)
        self.n_cores = n_cores
        self.tap_names = set(taps)
        self.taps = {}
        self.stop_after = stop_after
        nc = self.nc
        self.x = nc.dram_tensor("x", [NB, N, C], F32, kind="ExternalInput").ap()
        self.wqkv = nc.dram_tensor("wqkv", [C, 3 * C], F32, kind="ExternalInput").ap()
        self.bqkv = nc.dram_tensor("bqkv", [3 * C], F32, kind="ExternalInput").ap()
        self.wproj = nc.dram_tensor("wproj", [C, C], F32, kind="ExternalInput").ap()
        self.bproj = nc.dram_tensor("bproj", [C], F32, kind="ExternalInput").ap()
        self.wq_in = nc.dram_tensor("wq_in", [C, 2 * C], F32, kind="ExternalInput").ap()
        self.gamma = nc.dram_tensor("gamma", [2 * C], F32, kind="ExternalInput").ap()
        self.beta = nc.dram_tensor("beta", [2 * C], F32, kind="ExternalInput").ap()
        self.keys = nc.dram_tensor("keys", [128, 16, NKEY], F32, kind="ExternalInput").ap()
        self.tbl = nc.dram_tensor("tbl", [NEXP, 2048], F32, kind="ExternalInput").ap()
        self.out = nc.dram_tensor("out", [NB, N, C], F32, kind="ExternalOutput").ap()
        dbk = dict(kind="ExternalOutput") if dbg_dram else {}
        self.xnT_s = nc.dram_tensor("xnT_s", [NB, DCH, 128, N], F32, **dbk).ap()
        self.q_s = nc.dram_tensor("q_s", [16, 128, NTOK], F32, **dbk).ap()
        self.resid_s = nc.dram_tensor("resid_s", [NB, N, C], F32, **dbk).ap()
        self.cc_in = nc.dram_tensor("cc_in", [128, 36], F32).ap()
        self.cc_out = nc.dram_tensor("cc_out", [128, 36], F32).ap()

        A = nc.alloc_sbuf_tensor
        self.eye = A("eye", [128, 128], F32)
        self.iota19 = A("iota19", [64, TK], F32)
        self.iota8 = A("iota8", [128, 8], F32)
        self.bqkv24 = A("bqkv24", [128, 24], F32)
        self.bq_rep = None
        self.bk_rep = None
        self.bv_rep = None
        self.bproj_rep = None
        self.gamma16 = A("gamma16", [128, 16], F32)
        self.beta16 = A("beta16", [128, 16], F32)
        self.keys_sb = A("keys_sb", [128, 16 * NKEY], F32)
        self.rstd_all = A("rstd_all", [128, NTILE], F32)
        self.nmr_all = A("nmr_all", [128, NTILE], F32)
        self.bnacc = A("bnacc", [128, 128], F32)
        self.bnacc2 = A("bnacc2", [128, 128], F32)
        self.bnstat = A("bnstat", [128, 36], F32)
        self.allr = A("allr", [128, 36], F32)
        self.bn_a = A("bn_a", [128, 16], F32)
        self.bn_b = A("bn_b", [128, 16], F32)
        self.sel2 = A("sel2", [128, K * TK], F32)
        self.idx_all = A("idx_all", [128, NTILE * 64], I32)
        self.g_all = A("g_all", [128, NTILE * 64], F32)

        self.sems = {}
        for s in ["ld", "ld2", "g", "st", "st2", "mm", "v", "a", "gp", "cc", "fr"]:
            self.sems[s] = [nc.alloc_semaphore(s), 0]

    def inc(self, name, instr, k=1):
        st = self.sems[name]
        instr.then_inc(st[0], k)
        st[1] += k
        return st[1]

    def n(self, name):
        return self.sems[name][1]

    def wt(self, eng, name, val=None):
        st = self.sems[name]
        eng.wait_ge(st[0], st[1] if val is None else val)

    def sep(self, eng):
        eng.drain()

    def tap(self, blk, name, src_ap):
        if name in self.tap_names and name not in self.taps:
            t = self.nc.dram_tensor("tap_" + name, list(src_ap.shape),
                                    src_ap.dtype, kind="ExternalOutput").ap()
            self.taps[name] = t

            def f(sy, t=t, src=src_ap):
                for s in ("ld", "ld2", "v", "a", "mm", "g"):
                    self.wt(sy, s)
                self.inc("st2", sy.dma_start(
                    out=t[tuple([slice(None)] * len(t.shape))], in_=src), 16)
            blk.scalar(f)

    def b0(self):
        nc = self.nc
        with (
            nc.sbuf_tensor("c_i32", [128, 128], I32) as c_i32,
            nc.sbuf_tensor("c_m24", [24, 128], F32) as c_m24,
            nc.sbuf_tensor("c_g16", [16, 128], F32) as c_g16,
            nc.sbuf_tensor("c_b16", [16, 128], F32) as c_b16,
            nc.psum_tensor("p_c", [128, 128], F32) as p_c,
            nc.psum_tensor("p_c2", [128, 16], F32) as p_c2,
            nc.psum_tensor("p_c3", [128, 16], F32) as p_c3,
            nc.Block("b0") as blk,
        ):
            def s_load(sy):
                self.inc("ld", sy.dma_start(out=c_m24[:, :], in_=self.bqkv.rearrange("(a b) -> a b", a=24)), 16)
                self.inc("ld", sy.dma_start(out=c_g16[:, :], in_=self.gamma.rearrange("(a b) -> a b", a=16)), 16)
                self.inc("ld", sy.dma_start(out=c_b16[:, :], in_=self.beta.rearrange("(a b) -> a b", a=16)), 16)
                self.inc("ld", sy.dma_start(out=self.keys_sb[:, :], in_=self.keys.rearrange("p a k -> p (a k)")), 16)
            blk.sync(s_load)

            def s_iota(g):
                g.iota(c_i32[:, :], pattern=[[1, 128]], base=0, channel_multiplier=-1)
                g.iota(self.iota19[:, :], pattern=[[1, TK]], base=1,
                       channel_multiplier=0, allow_small_or_imprecise_dtypes=True)
                self.inc("gp", g.iota(self.iota8[:, :], pattern=[[1, 8]], base=0,
                                      channel_multiplier=0,
                                      allow_small_or_imprecise_dtypes=True))
            blk.gpsimd(s_iota)

            def s_eye(v):
                self.wt(v, "gp")
                v.tensor_scalar(out=self.eye[:, :], in0=c_i32[:, :], scalar1=0,
                                scalar2=None, op0=ALU.is_equal)
                v.memset(self.bnstat[:, :], 0.0)
                v.memset(self.bnacc[:, :], 0.0)
                self.inc("v", v.memset(self.bnacc2[:, :], 0.0))
            blk.vector(s_eye)

            def s_tr(t):
                self.wt(t, "ld")
                self.wt(t, "v")
                t.transpose(p_c[:, 0:24], c_m24[:, :], self.eye[0:24, 0:24])
                t.transpose(p_c2[:, 0:16], c_g16[:, :], self.eye[0:16, 0:16])
                self.inc("mm", t.transpose(p_c3[:, 0:16], c_b16[:, :], self.eye[0:16, 0:16]))
            blk.tensor(s_tr)

            def s_ev(sc):
                self.wt(sc, "mm")
                sc.activation(out=self.bqkv24[:, :], in_=p_c[:, 0:24], func=ACT.Copy)
                sc.activation(out=self.gamma16[:, :], in_=p_c2[:, 0:16], func=ACT.Copy)
                self.inc("a", sc.activation(out=self.beta16[:, :], in_=p_c3[:, 0:16], func=ACT.Copy))
            blk.scalar(s_ev)

    def b1(self, b, wqk, xnT):
        nc = self.nc
        with (
            nc.sbuf_tensor(f"xnr{b}", [128, 2 * C], F32) as xnr,
            nc.sbuf_tensor(f"lnt{b}", [128, 8], F32) as lnt,
            nc.sbuf_tensor(f"sq_scr{b}", [128, C], F32) as sq_scr,
            nc.psum_tensor(f"ptA{b}", [128, 128], F32) as ptA,
            nc.psum_tensor(f"ptB{b}", [128, 128], F32) as ptB,
            nc.Block(f"b1_ln{b}") as blk,
        ):
            slot_guard = [0, 0]
            bank_guard = [0, 0]
            for tt in range(16):
                col = b * 16 + tt
                slot = tt % 2
                xt = xnr[:, slot * C:(slot + 1) * C]

                def s_load(sy, xt=xt, tt=tt, g=slot_guard[slot]):
                    self.wt(sy, "mm", g)
                    self.inc("ld", sy.dma_start(out=xt, in_=self.x[b, tt * 128:(tt + 1) * 128, :]), 16)
                blk.sync(s_load)

                def s_sums(sc, xt=xt):
                    self.wt(sc, "ld")
                    sc.activation(out=sq_scr[:, :], in_=xt, func=ACT.Copy,
                                  accum_out=lnt[:, 0:1])
                    self.inc("a", sc.activation(out=sq_scr[:, :], in_=xt, func=ACT.Square,
                                                accum_out=lnt[:, 1:2]))
                blk.scalar(s_sums)

                def s_stats(v):
                    self.wt(v, "a")
                    v.tensor_scalar_mul(lnt[:, 2:3], lnt[:, 0:1], 1.0 / C)
                    v.tensor_scalar_mul(lnt[:, 3:4], lnt[:, 1:2], 1.0 / C)
                    v.tensor_mul(lnt[:, 4:5], lnt[:, 2:3], lnt[:, 2:3])
                    self.sep(v)
                    v.tensor_sub(lnt[:, 5:6], lnt[:, 3:4], lnt[:, 4:5])
                    self.sep(v)
                    self.inc("v", v.tensor_scalar_add(lnt[:, 5:6], lnt[:, 5:6], LN_EPS))
                blk.vector(s_stats)

                def s_rstd(sc, col=col):
                    self.wt(sc, "v")
                    self.inc("a", sc.activation(out=lnt[:, 6:7],
                                                in_=lnt[:, 5:6], func=ACT.Sqrt))
                blk.scalar(s_rstd)

                def s_nmr(v, col=col):
                    self.wt(v, "a")
                    v.reciprocal(out=self.rstd_all[:, col:col + 1], in_=lnt[:, 6:7])
                    self.sep(v)
                    self.inc("v", v.scalar_tensor_tensor(
                        out=self.nmr_all[:, col:col + 1], in0=lnt[:, 2:3], scalar=-1.0,
                        in1=self.rstd_all[:, col:col + 1], op0=ALU.mult, op1=ALU.mult))
                blk.vector(s_nmr)

                def s_norm(v, xt=xt, col=col):
                    self.sep(v)
                    self.inc("v", v.tensor_scalar(
                        out=xt, in0=xt, scalar1=self.rstd_all[:, col:col + 1],
                        scalar2=self.nmr_all[:, col:col + 1],
                        op0=ALU.mult, op1=ALU.add))
                blk.vector(s_norm)

                for dch in range(DCH):
                    pbank = (tt * DCH + dch) % 2
                    ps = ptA if pbank == 0 else ptB

                    def s_tp(t, xt=xt, dch=dch, ps=ps, g=bank_guard[pbank]):
                        self.wt(t, "v")
                        self.wt(t, "a", g)
                        self.inc("mm", t.transpose(
                            ps[:, :], xt[:, dch * 128:(dch + 1) * 128],
                            self.eye[:, :]))
                    blk.tensor(s_tp)

                    def s_ev(sc, tt=tt, dch=dch, ps=ps):
                        self.wt(sc, "mm")
                        self.inc("a", sc.activation(
                            out=xnT[dch][:, tt * 128:(tt + 1) * 128],
                            in_=ps[:, :], func=ACT.Copy))
                    blk.scalar(s_ev)
                    bank_guard[pbank] = self.n("a")
                slot_guard[slot] = self.n("mm")

            def s_spill(sy):
                self.wt(sy, "a")
                for dch in range(DCH):
                    self.inc("st", sy.dma_start(out=self.xnT_s[b, dch, :, :],
                                                in_=xnT[dch][:, :]), 16)
            blk.sync(s_spill)

        with (
            nc.sbuf_tensor(f"kfl{b}", [128, 16 * 128], F32) as kfl,
            nc.sbuf_tensor(f"bqr{b}", [128, C], F32) as bqr,
            nc.sbuf_tensor(f"bkr{b}", [128, C], F32) as bkr,
            nc.sbuf_tensor(f"qkf{b}", [128, C], F32) as qkf,
            nc.sbuf_tensor(f"kkf{b}", [128, C], F32) as kkf,
            nc.sbuf_tensor(f"dotb{b}", [128, C + 32], F32) as dotb,
            nc.psum_tensor(f"pqA{b}", [128, 512], F32) as pqA,
            nc.psum_tensor(f"pqB{b}", [128, 512], F32) as pqB,
            nc.Block(f"b1_kf{b}") as blk,
        ):
            def s_brep(sy):
                self.inc("ld", sy.dma_start(out=bqr[:, :], in_=self.bqkv[0:C].unsqueeze(0).to_broadcast([128, C])), 16)
                self.inc("ld", sy.dma_start(out=bkr[:, :], in_=self.bqkv[C:2 * C].unsqueeze(0).to_broadcast([128, C])), 16)
            blk.sync(s_brep)

            def s_kfl(v):
                for kft in range(2):
                    for dch in range(DCH):
                        lhs = xnT[dch][:, :].rearrange(
                            "p (g r) -> p g r", g=4)[:, kft * 2:(kft + 1) * 2, 0:64]
                        v.tensor_copy(
                            out=kfl[:, (kft * DCH + dch) * 128:(kft * DCH + dch + 1) * 128],
                            in_=lhs)
                self.inc("v", v.tensor_copy(out=kfl[:, 0:1], in_=kfl[:, 0:1]))
            blk.vector(s_kfl)
            for kft in range(2):
                for qk in range(2):
                    dst = qkf if qk == 0 else kkf
                    rep = bqr if qk == 0 else bkr
                    for half in range(2):
                        ps = pqA if half == 0 else pqB

                        def s_mm(t, kft=kft, qk=qk, half=half, ps=ps):
                            self.wt(t, "ld")
                            self.wt(t, "v")
                            for dch in range(DCH):
                                lhs = kfl[:, (kft * DCH + dch) * 128:(kft * DCH + dch + 1) * 128]
                                self.inc("mm", t.matmul(
                                    ps[:, :], lhs,
                                    wqk[dch][:, qk * C + half * 512: qk * C + (half + 1) * 512],
                                    start=(dch == 0), stop=(dch == DCH - 1)))
                        blk.tensor(s_mm)

                        def s_ev(v, half=half, ps=ps, dst=dst, rep=rep):
                            self.wt(v, "mm")
                            self.inc("v", v.tensor_add(
                                dst[:, half * 512:(half + 1) * 512], ps[:, :],
                                rep[:, half * 512:(half + 1) * 512]))
                        blk.vector(s_ev)

                def s_dot(v, kft=kft):
                    self.sep(v)
                    v.tensor_mul(dotb[:, 0:C], qkf[:, :], kkf[:, :])
                    self.sep(v)
                    v.reduce_sum(out=dotb[:, C:C + 16],
                                 in_=dotb[:, 0:C].rearrange("p (h d) -> p h d", h=H),
                                 axis=AX.X)
                    self.sep(v)
                    v.reduce_max(out=dotb[:, C + 16:C + 17], in_=dotb[:, C:C + 16],
                                 axis=AX.X)
                    self.sep(v)
                    self.inc("v", v.tensor_add(self.bnstat[:, 32 + kft:33 + kft],
                                               self.bnstat[:, 32 + kft:33 + kft],
                                               dotb[:, C + 16:C + 17]))
                blk.vector(s_dot)
            self.tap(blk, f"qkf_{b}", qkf[:, :])
            self.tap(blk, f"kkf_{b}", kkf[:, :])
            self.tap(blk, f"dotb_{b}", dotb[:, :])

        with (
            nc.sbuf_tensor(f"wqjc{b}", [128, 2 * DCH * 128], F32) as wqjc,
            nc.sbuf_tensor(f"qev{b}", [128, 2 * 512], F32) as qev,
            nc.sbuf_tensor(f"qsq{b}", [128, 512], F32) as qsq,
            nc.psum_tensor(f"pfA{b}", [128, 512], F32) as pfA,
            nc.psum_tensor(f"pfB{b}", [128, 512], F32) as pfB,
            nc.Block(f"b1_q{b}") as blk,
        ):
            ev_guard = [0, 0]
            ev_guard2 = [0, 0]
            w_guard = [0, 0]
            for jc in range(16):
                wslot = jc % 2

                def s_wload(sy, jc=jc, wslot=wslot, g=w_guard[wslot]):
                    self.wt(sy, "mm", g)
                    src = self.wq_in.rearrange("(dc p) (jcc j) -> p dc jcc j",
                                               dc=DCH, jcc=16)[:, :, jc, :]
                    self.inc("ld", sy.dma_start(
                        out=wqjc[:, wslot * DCH * 128:(wslot + 1) * DCH * 128],
                        in_=src), 16)
                blk.sync(s_wload)

                for t4 in range(4):
                    ps = pfA if t4 % 2 == 0 else pfB
                    eslot = t4 % 2

                    def s_mm(t, jc=jc, t4=t4, ps=ps, wslot=wslot,
                             g=ev_guard2[t4 % 2]):
                        self.wt(t, "ld")
                        self.wt(t, "a", g)
                        for dch in range(DCH):
                            self.inc("mm", t.matmul(
                                ps[:, :],
                                wqjc[:, wslot * DCH * 128 + dch * 128:
                                     wslot * DCH * 128 + (dch + 1) * 128],
                                xnT[dch][:, t4 * 512:(t4 + 1) * 512],
                                start=(dch == 0), stop=(dch == DCH - 1)))
                    blk.tensor(s_mm)

                    def s_ev(sc, jc=jc, t4=t4, ps=ps, eslot=eslot, g=ev_guard[eslot]):
                        self.wt(sc, "mm")
                        self.wt(sc, "st", g)
                        acol = jc * 8 + b * 4 + t4
                        sc.activation(out=qev[:, eslot * 512:(eslot + 1) * 512],
                                      in_=ps[:, :], func=ACT.Copy,
                                      accum_out=self.bnacc[:, acol:acol + 1])
                        self.inc("a", sc.activation(
                            out=qsq[:, :], in_=qev[:, eslot * 512:(eslot + 1) * 512],
                            func=ACT.Square, accum_out=self.bnacc2[:, acol:acol + 1]))
                    blk.scalar(s_ev)

                    def s_spill(sy, jc=jc, t4=t4, eslot=eslot):
                        self.wt(sy, "a")
                        self.inc("st", sy.dma_start(
                            out=self.q_s[jc, :, b * N + t4 * 512: b * N + (t4 + 1) * 512],
                            in_=qev[:, eslot * 512:(eslot + 1) * 512]), 16)
                    blk.sync(s_spill)
                    ev_guard[eslot] = self.n("st")
                    ev_guard2[t4 % 2] = self.n("a")
                w_guard[wslot] = self.n("mm")

    def b2(self):
        nc = self.nc
        with (
            nc.sbuf_tensor("sc2", [2, 128], F32) as sc2,
            nc.sbuf_tensor("sc4", [4, 64], F32) as sc4,
            nc.sbuf_tensor("thr", [4, 80], F32) as thr,
            nc.sbuf_tensor("csA", [4, 64], F32) as csA,
            nc.sbuf_tensor("csB", [4, 64], F32) as csB,
            nc.sbuf_tensor("selT", [64, 8], F32) as selT,
            nc.sbuf_tensor("b2t", [128, 16], F32) as b2t,
            nc.psum_tensor("p_b2", [128, 128], F32) as p_b2,
            nc.psum_tensor("p_b2b", [128, 16], F32) as p_b2b,
            nc.Block("b2") as blk,
        ):
            def s_red(v):
                v.reduce_sum(out=self.bnstat[:, 0:16],
                             in_=self.bnacc[:, :].rearrange("p (jc t) -> p jc t", jc=16),
                             axis=AX.X)
                self.inc("v", v.reduce_sum(
                    out=self.bnstat[:, 16:32],
                    in_=self.bnacc2[:, :].rearrange("p (jc t) -> p jc t", jc=16),
                    axis=AX.X))
            blk.vector(s_red)

            def s_cc0(sy):
                self.wt(sy, "v")
                self.inc("st", sy.dma_start(out=self.cc_in[:, :], in_=self.bnstat[:, :]), 16)
            blk.sync(s_cc0)

            def s_cc(g):
                self.wt(g, "st")
                self.inc("cc", g.collective_compute(
                    "AllReduce", ALU.add,
                    replica_groups=[list(range(self.n_cores))],
                    ins=[self.cc_in[:, :]], outs=[self.cc_out[:, :]]))
            blk.gpsimd(s_cc)

            def s_cc1(sy):
                self.wt(sy, "cc")
                self.inc("ld", sy.dma_start(out=self.allr[:, :], in_=self.cc_out[:, :]), 16)
            blk.sync(s_cc1)

            def s_bn(v):
                self.wt(v, "ld")
                mean = b2t[:, 0:16]
                v.tensor_scalar_mul(mean, self.allr[:, 0:16], 1.0 / (NTOK * self.n_cores))
                v.tensor_scalar_mul(self.bn_b[:, :], self.allr[:, 16:32], 1.0 / (NTOK * self.n_cores))
                v.tensor_mul(self.bn_a[:, :], mean, mean)
                self.sep(v)
                v.tensor_sub(self.bn_b[:, :], self.bn_b[:, :], self.bn_a[:, :])
                self.sep(v)
                self.inc("v", v.tensor_scalar_add(self.bn_b[:, :], self.bn_b[:, :], BN_EPS))
            blk.vector(s_bn)

            def s_bn2(sc):
                self.wt(sc, "v")
                self.inc("a", sc.activation(out=self.bn_a[:, :], in_=self.bn_b[:, :],
                                            func=ACT.Sqrt))
            blk.scalar(s_bn2)

            def s_bn3(v):
                self.wt(v, "a")
                v.reciprocal(out=self.bn_a[:, :], in_=self.bn_a[:, :])
                self.sep(v)
                v.tensor_mul(self.bn_a[:, :], self.bn_a[:, :], self.gamma16[:, :])
                self.sep(v)
                v.tensor_mul(self.bn_b[:, :], b2t[:, 0:16], self.bn_a[:, :])
                self.sep(v)
                self.inc("v", v.tensor_sub(self.bn_b[:, :], self.beta16[:, :], self.bn_b[:, :]))
            blk.vector(s_bn3)

            def s_tr(t):
                self.wt(t, "v")
                self.inc("mm", t.transpose(p_b2[0:2, 0:128],
                                           self.allr[:, 32:34], self.eye[:, :]))
            blk.tensor(s_tr)

            def s_ev(sc):
                self.wt(sc, "mm")
                self.inc("a", sc.activation(out=sc2[:, :], in_=p_b2[0:2, 0:128],
                                            func=ACT.Copy))
            blk.scalar(s_ev)

            def s_mv(sy):
                self.wt(sy, "a")
                for kf in range(4):
                    self.inc("ld", sy.dma_start(
                        out=sc4[kf:kf + 1, :],
                        in_=sc2[kf // 2:kf // 2 + 1, (kf % 2) * 64:(kf % 2) * 64 + 64]), 16)
            blk.sync(s_mv)

            def s_thr(v):
                self.wt(v, "ld")
                v.max(out=thr[:, 0:8], in_=sc4[:, :])
                self.sep(v)
                v.match_replace(out=thr[:, 16:80], in_to_replace=thr[:, 0:8],
                                in_values=sc4[:, :], imm_value=-1e30)
                self.sep(v)
                v.max(out=thr[:, 8:16], in_=thr[:, 16:80])
                self.sep(v)
                v.match_replace(out=thr[:, 16:80], in_to_replace=thr[:, 8:16],
                                in_values=thr[:, 16:80], imm_value=-1e30)
                self.sep(v)
                v.max(out=thr[:, 0:8], in_=thr[:, 16:80])
                self.sep(v)
                v.tensor_scalar(out=csA[:, :], in0=sc4[:, :], scalar1=thr[:, 2:3],
                                scalar2=None, op0=ALU.is_ge)
                self.sep(v)
                self.inc("v", v.tensor_copy(out=csB[:, :], in_=csA[:, :]))
            blk.vector(s_thr)

            def s_cum(v):
                a, bb = csB, thr[:, 16:80]
                for sh in [1, 2, 4, 8, 16, 32]:
                    self.sep(v)
                    v.tensor_copy(out=bb[:, 0:sh], in_=a[:, 0:sh])
                    v.tensor_add(bb[:, sh:64], a[:, sh:64], a[:, 0:64 - sh])
                    a, bb = bb, a
                self.sep(v)
                self.inc("v", v.tensor_copy(out=csB[:, :], in_=a[:, :]))
            blk.vector(s_cum)

            def s_tr2(t):
                self.wt(t, "v")
                t.transpose(p_b2[0:64, 4:8], csB[:, :], self.eye[0:4, 0:4])
                self.inc("mm", t.transpose(p_b2b[0:64, 0:4], csA[:, :], self.eye[0:4, 0:4]))
            blk.tensor(s_tr2)

            def s_ev2(sc):
                self.wt(sc, "mm")
                sc.activation(out=selT[:, 0:4], in_=p_b2[0:64, 4:8], func=ACT.Copy)
                self.inc("a", sc.activation(out=selT[:, 4:8], in_=p_b2b[0:64, 0:4],
                                            func=ACT.Copy))
            blk.scalar(s_ev2)

            def s_sel(v):
                self.wt(v, "a")
                for ncl in range(K):
                    v.tensor_tensor(
                        out=self.sel2[0:64, ncl * TK:(ncl + 1) * TK],
                        in0=selT[:, ncl:ncl + 1].to_broadcast([64, TK]),
                        in1=self.iota19[:, :], op=ALU.is_equal)
                    self.sep(v)
                    v.tensor_scalar_mul(self.sel2[0:64, ncl * TK:(ncl + 1) * TK],
                                        self.sel2[0:64, ncl * TK:(ncl + 1) * TK],
                                        selT[:, 4 + ncl:5 + ncl])
                self.inc("v", v.tensor_copy(out=csA[:, :], in_=csA[:, :]))
            blk.vector(s_sel)

            def s_dup(sy):
                self.wt(sy, "v")
                self.inc("ld", sy.dma_start(out=self.sel2[64:128, :],
                                            in_=self.sel2[0:64, :]), 16)
            blk.sync(s_dup)
            self.tap(blk, "gamma16", self.gamma16[:, :])
            self.tap(blk, "beta16", self.beta16[:, :])
            self.tap(blk, "bqkv24", self.bqkv24[:, :])
            self.tap(blk, "sel2", self.sel2[:, :])
            self.tap(blk, "bn_a", self.bn_a[:, :])
            self.tap(blk, "bn_b", self.bn_b[:, :])
            self.tap(blk, "sc4", sc4[:, :])
            self.tap(blk, "csA", csA[:, :])
            self.tap(blk, "csB", csB[:, :])
            self.tap(blk, "selT", selT[:, :])
            self.tap(blk, "allr", self.allr[:, :])
            self.tap(blk, "b2t", b2t[:, :])
            self.tap(blk, "thr", thr[:, :])

    def b3(self, b):
        nc = self.nc
        es = contextlib.ExitStack()
        klocT = [es.enter_context(nc.sbuf_tensor(f"klocT{b}_{i}", [128, NSEL], F32)) for i in range(DCH)]
        vloc = [es.enter_context(nc.sbuf_tensor(f"vloc{b}_{i}", [128, C], F32)) for i in range(5)]
        bvr = es.enter_context(nc.sbuf_tensor(f"bvr{b}", [128, C], F32))
        bpr = es.enter_context(nc.sbuf_tensor(f"bpr{b}", [128, C], F32))
        es2 = contextlib.ExitStack()
        wbuf = [es2.enter_context(nc.sbuf_tensor(f"wb{b}_{i}", [128, C], F32)) for i in range(DCH)]
        xselT = [es2.enter_context(nc.sbuf_tensor(f"xselT{b}_{i}", [128, NSEL], F32)) for i in range(DCH)]
        with nc.Block(f"b3pre{b}") as blk:
            def s_brep(sy):
                self.inc("ld", sy.dma_start(out=bvr[:, :], in_=self.bqkv[2 * C:3 * C].unsqueeze(0).to_broadcast([128, C])), 16)
                self.inc("ld", sy.dma_start(out=bpr[:, :], in_=self.bproj[0:C].unsqueeze(0).to_broadcast([128, C])), 16)
            blk.sync(s_brep)

        with (
            nc.sbuf_tensor(f"xc{b}", [128, 2 * C], F32) as xc,
            nc.sbuf_tensor(f"xsf{b}", [32, 2 * C], F32) as xsf,
            nc.psum_tensor(f"pcmp{b}", [32, C], F32) as pcmp,
            nc.psum_tensor(f"ptrA{b}", [128, 32], F32) as ptrA,
            nc.psum_tensor(f"ptrB{b}", [128, 32], F32) as ptrB,
            nc.Block(f"b3s{b}") as blk,
        ):
            xc_guard = [0, 0]
            xs_guard = [0, 0]
            tr_guard = [0, 0]
            for tt in range(16):
                slot = tt % 2
                xt = xc[:, slot * C:(slot + 1) * C]

                def s_load(sy, tt=tt, xt=xt, g=xc_guard[slot]):
                    self.wt(sy, "mm", g)
                    self.inc("ld", sy.dma_start(out=xt, in_=self.x[b, tt * 128:(tt + 1) * 128, :]), 16)
                blk.sync(s_load)

                def s_norm(v, tt=tt, xt=xt):
                    self.wt(v, "ld")
                    col = b * 16 + tt
                    self.inc("v", v.tensor_scalar(
                        out=xt, in0=xt, scalar1=self.rstd_all[:, col:col + 1],
                        scalar2=self.nmr_all[:, col:col + 1],
                        op0=ALU.mult, op1=ALU.add))
                blk.vector(s_norm)

                for fr in range(2):
                    s = 2 * tt + fr
                    ncl = s // FC
                    fslot = s % 2

                    def s_cmp(t, xt=xt, fr=fr, ncl=ncl, fslot=fslot, g=xs_guard[fslot]):
                        self.wt(t, "v")
                        self.wt(t, "a", g)
                        for hcol in range(2):
                            self.inc("mm", t.matmul(
                                pcmp[0:TK, hcol * 512:(hcol + 1) * 512],
                                self.sel2[fr * 64:(fr + 1) * 64, ncl * TK:(ncl + 1) * TK],
                                xt[fr * 64:(fr + 1) * 64, hcol * 512:(hcol + 1) * 512],
                                start=True, stop=True))
                    blk.tensor(s_cmp)

                    def s_evc(sc, fslot=fslot):
                        self.wt(sc, "mm")
                        self.inc("a", sc.activation(
                            out=xsf[0:TK, fslot * C:(fslot + 1) * C],
                            in_=pcmp[0:TK, :], func=ACT.Copy))
                    blk.scalar(s_evc)

                    for dch in range(DCH):
                        tslot = dch % 2
                        ptr = ptrA if tslot == 0 else ptrB

                        def s_tp(t, dch=dch, fslot=fslot, ptr=ptr, g=tr_guard[tslot]):
                            self.wt(t, "a")
                            self.wt(t, "a", g)
                            self.inc("mm", t.transpose(
                                ptr[:, 0:TK],
                                xsf[0:TK, fslot * C + dch * 128: fslot * C + (dch + 1) * 128],
                                self.eye[0:TK, 0:TK]))
                        blk.tensor(s_tp)

                        def s_evt(sc, s=s, dch=dch, ptr=ptr):
                            self.wt(sc, "mm")
                            self.inc("a", sc.activation(
                                out=xselT[dch][:, s * TK:(s + 1) * TK],
                                in_=ptr[:, 0:TK], func=ACT.Copy))
                        blk.scalar(s_evt)
                        tr_guard[tslot] = self.n("a")
                    xs_guard[fslot] = self.n("a")
                xc_guard[slot] = self.n("mm")
            self.tap(blk, f"xselT0_{b}", xselT[0][:, :])

        with nc.Block(f"b3k{b}") as blk:
            def s_wk(sy):
                for dch in range(DCH):
                    self.inc("ld", sy.dma_start(out=wbuf[dch][:, :],
                                                in_=self.wqkv[dch * 128:(dch + 1) * 128, C:2 * C]), 16)
            blk.sync(s_wk)
        with (
            nc.psum_tensor(f"pk1{b}", [128, 512], F32) as pk1,
            nc.psum_tensor(f"pk2{b}", [128, 512], F32) as pk2,
            nc.Block(f"b3k2{b}") as blk,
        ):
            for hdc in range(DCH):
                for cpart, (c0, cl) in enumerate([(0, 512), (512, NSEL - 512)]):
                    ps = pk1 if cpart == 0 else pk2

                    def s_mm(t, hdc=hdc, c0=c0, cl=cl, ps=ps):
                        self.wt(t, "ld")
                        self.wt(t, "a")
                        for dch in range(DCH):
                            self.inc("mm", t.matmul(
                                ps[:, 0:cl], wbuf[dch][:, hdc * 128:(hdc + 1) * 128],
                                xselT[dch][:, c0:c0 + cl],
                                start=(dch == 0), stop=(dch == DCH - 1)))
                    blk.tensor(s_mm)

                    def s_ev(sc, hdc=hdc, c0=c0, cl=cl, ps=ps):
                        self.wt(sc, "mm")
                        self.inc("a", sc.activation(
                            out=klocT[hdc][:, c0:c0 + cl], in_=ps[:, 0:cl],
                            func=ACT.Identity, bias=self.bqkv24[:, 8 + hdc:9 + hdc]))
                    blk.scalar(s_ev)
        with nc.Block(f"b3v{b}") as blk:
            def s_wv(sy):
                self.wt(sy, "a")
                for dch in range(DCH):
                    self.inc("ld", sy.dma_start(out=wbuf[dch][:, :],
                                                in_=self.wqkv[dch * 128:(dch + 1) * 128, 2 * C:3 * C]), 16)
            blk.sync(s_wv)
        with (
            nc.psum_tensor(f"pv1{b}", [128, 512], F32) as pv1,
            nc.psum_tensor(f"pv2{b}", [128, 512], F32) as pv2,
            nc.Block(f"b3v2{b}") as blk,
        ):
            for st_ in range(5):
                rows = min(128, NSEL - st_ * 128)
                for half in range(2):
                    ps = pv1 if half == 0 else pv2

                    def s_mm(t, st_=st_, rows=rows, half=half, ps=ps):
                        self.wt(t, "ld")
                        self.wt(t, "v")
                        for dch in range(DCH):
                            self.inc("mm", t.matmul(
                                ps[0:rows, :], xselT[dch][:, st_ * 128:st_ * 128 + rows],
                                wbuf[dch][:, half * 512:(half + 1) * 512],
                                start=(dch == 0), stop=(dch == DCH - 1)))
                    blk.tensor(s_mm)

                    def s_ev(v, st_=st_, rows=rows, half=half, ps=ps):
                        self.wt(v, "mm")
                        self.inc("v", v.tensor_add(
                            vloc[st_][0:rows, half * 512:(half + 1) * 512],
                            ps[0:rows, :],
                            bvr[0:rows, half * 512:(half + 1) * 512]))
                    blk.vector(s_ev)
            self.tap(blk, f"kloc0_{b}", klocT[0][:, :])
            self.tap(blk, f"vloc0_{b}", vloc[0][:, :])

        es2.close()
        es3 = contextlib.ExitStack()
        wbuf = [es3.enter_context(nc.sbuf_tensor(f"wc{b}_{i}", [128, C], F32)) for i in range(DCH)]
        for ci in range(K):
            self._b3_ci(b, ci, klocT, vloc, wbuf, bpr)
        es3.close()
        es.close()

    def _b3_ci(self, b, ci, klocT, vloc, wbuf, bpr):
        nc = self.nc
        ranges = _kv_ranges(ci)
        L = sum(l for _, l in ranges)
        rtiles = _range_tiles(ci)
        poff = {}
        acc = 0
        for r, (off, ln) in enumerate(ranges):
            poff[r] = acc
            acc += ln
        with (
            nc.sbuf_tensor(f"qT{b}{ci}", [128, DCH * 512], F32) as qTc,
            nc.sbuf_tensor(f"xns{b}{ci}", [128, DCH * 512], F32) as xns,
            nc.sbuf_tensor(f"vr{b}{ci}", [128, len(rtiles) * C], F32) as vr,
            nc.sbuf_tensor(f"probs{b}{ci}", [128, 3 * 512], F32) as probs,
            nc.sbuf_tensor(f"probsT{b}{ci}", [128, 4 * 128], F32) as probsT,
            nc.sbuf_tensor(f"attn{b}{ci}", [128, 4 * C], F32) as attn,
            nc.sbuf_tensor(f"smax{b}{ci}", [128, 16], F32) as smax,
            nc.sbuf_tensor(f"rs{b}{ci}", [128, 8], F32) as rs,
            nc.sbuf_tensor(f"xres{b}{ci}", [128, 2 * C], F32) as xres,
            nc.psum_tensor(f"plA{b}{ci}", [128, 512], F32) as plA,
            nc.psum_tensor(f"plB{b}{ci}", [128, 512], F32) as plB,
            nc.psum_tensor(f"ptA{b}{ci}", [128, 128], F32) as ptA,
            nc.psum_tensor(f"ptB{b}{ci}", [128, 128], F32) as ptB,
            nc.psum_tensor(f"ptC{b}{ci}", [128, 128], F32) as ptC,
            nc.psum_tensor(f"ptD{b}{ci}", [128, 128], F32) as ptD,
            nc.psum_tensor(f"paA{b}{ci}", [128, 64], F32) as paA,
            nc.psum_tensor(f"paB{b}{ci}", [128, 64], F32) as paB,
            nc.Block(f"b3c{b}{ci}") as blk,
        ):
            def s_ld(sy):
                self.wt(sy, "v")
                self.wt(sy, "a")
                for dch in range(DCH):
                    self.inc("ld", sy.dma_start(
                        out=xns[:, dch * 512:(dch + 1) * 512],
                        in_=self.xnT_s[b, dch, :, ci * 512:(ci + 1) * 512]), 16)
                    self.inc("ld", sy.dma_start(
                        out=wbuf[dch][:, :],
                        in_=self.wqkv[dch * 128:(dch + 1) * 128, 0:C]), 16)
            blk.sync(s_ld)

            def s_vr(sy):
                self.wt(sy, "mm")
                self.wt(sy, "v")
                for ti, (r, bi, soff, nr) in enumerate(rtiles):
                    for (stile, srow, drow, n2) in _asm_pieces(soff, nr):
                        self.inc("ld2", sy.dma_start(
                            out=vr[:, ti * C:(ti + 1) * C][drow:drow + n2, :],
                            in_=vloc[stile][srow:srow + n2, :]), 16)
            blk.scalar(s_vr)

            pg = [pk for pk in (plA, plB)]
            for hdc in range(DCH):
                ps = pg[hdc % 2]

                def s_mm(t, hdc=hdc, ps=ps):
                    self.wt(t, "ld")
                    self.wt(t, "a")
                    for dch in range(DCH):
                        self.inc("mm", t.matmul(
                            ps[:, :], wbuf[dch][:, hdc * 128:(hdc + 1) * 128],
                            xns[:, dch * 512:(dch + 1) * 512],
                            start=(dch == 0), stop=(dch == DCH - 1)))
                blk.tensor(s_mm)

                def s_ev(sc, hdc=hdc, ps=ps):
                    self.wt(sc, "mm")
                    self.inc("a", sc.activation(
                        out=qTc[:, hdc * 512:(hdc + 1) * 512], in_=ps[:, :],
                        func=ACT.Identity, bias=self.bqkv24[:, hdc:hdc + 1]))
                blk.scalar(s_ev)

            a0 = self.n("a")
            gd = dict(pa=[0, 0], pl=[a0, a0], pt=[0, 0, 0, 0],
                      pr=[0, 0, 0], prT=[0, 0, 0, 0], qkmm=[0, 0, 0, 0],
                      expa={}, smv={})
            pls = [plA, plB]
            pts = [ptA, ptB, ptC, ptD]
            units = [(h, qt) for h in range(H) for qt in range(4)]
            nu = len(units)
            self._b3_qk(blk, b, ci, 0, 0, gd, ranges, poff, qTc, klocT, pls[0])
            for i in range(nu + 1):
                if i + 1 <= nu - 1:
                    h2, qt2 = units[i + 1]
                    self._b3_qk(blk, b, ci, h2, qt2, gd, ranges, poff,
                                qTc, klocT, pls[(i + 1) % 2])
                if i <= nu - 1:
                    h, qt = units[i]
                    self._b3_soft(blk, b, ci, h, qt, i, gd, L,
                                  probs, smax, rs, pls[i % 2])
                if i >= 1:
                    h, qt = units[i - 1]
                    self._b3_tpv(blk, b, ci, h, qt, i - 1, gd, poff, rtiles,
                                 vr, probs, probsT, attn, rs, pts, paA, paB)
            pt_guard = [0, 0]

            def s_wp(sy):
                self.wt(sy, "mm")
                for dch in range(DCH):
                    self.inc("ld", sy.dma_start(out=wbuf[dch][:, :],
                                                in_=self.wproj[dch * 128:(dch + 1) * 128, :]), 16)
            blk.sync(s_wp)

            attnT = xns
            res_guard = [0, 0]
            pp_guard = [0, 0]
            pt_guard = [0, 0]
            for qt in range(4):
                for dch in range(DCH):
                    pt = ptA if dch % 2 == 0 else ptB

                    def s_tp(t, qt=qt, dch=dch, pt=pt, g=pt_guard[dch % 2]):
                        self.wt(t, "v")
                        self.wt(t, "a", g)
                        self.inc("mm", t.transpose(
                            pt[:, :], attn[:, qt * C + dch * 128: qt * C + (dch + 1) * 128],
                            self.eye[:, :]))
                    blk.tensor(s_tp)

                    def s_evt(sc, dch=dch, pt=pt):
                        self.wt(sc, "mm")
                        self.inc("a", sc.activation(
                            out=attnT[:, dch * 128:(dch + 1) * 128], in_=pt[:, :],
                            func=ACT.Copy))
                    blk.scalar(s_evt)
                    pt_guard[dch % 2] = self.n("a")

                if qt == 0:
                    self.tap(blk, f"attnT0_{b}{ci}", attnT[:, 0:DCH * 128])

                def s_xld(sy, qt=qt, g=res_guard[qt % 2]):
                    self.wt(sy, "st", g)
                    self.inc("ld", sy.dma_start(
                        out=xres[:, (qt % 2) * C:(qt % 2 + 1) * C],
                        in_=self.x[b, ci * 512 + qt * 128: ci * 512 + (qt + 1) * 128, :]), 16)
                blk.sync(s_xld)

                for half in range(2):
                    pp = plA if half == 0 else plB

                    def s_pj(t, qt=qt, half=half, pp=pp, g=pp_guard[half]):
                        self.wt(t, "ld")
                        self.wt(t, "a")
                        self.wt(t, "v", g)
                        for dch in range(DCH):
                            self.inc("mm", t.matmul(
                                pp[:, :], attnT[:, dch * 128:(dch + 1) * 128],
                                wbuf[dch][:, half * 512:(half + 1) * 512],
                                start=(dch == 0), stop=(dch == DCH - 1)))
                    blk.tensor(s_pj)

                    def s_res(v, qt=qt, half=half, pp=pp):
                        self.wt(v, "mm")
                        self.wt(v, "ld")
                        v.tensor_add(
                            xres[:, (qt % 2) * C + half * 512:(qt % 2) * C + (half + 1) * 512],
                            xres[:, (qt % 2) * C + half * 512:(qt % 2) * C + (half + 1) * 512],
                            bpr[:, half * 512:(half + 1) * 512])
                        self.sep(v)
                        self.inc("v", v.tensor_add(
                            xres[:, (qt % 2) * C + half * 512:(qt % 2) * C + (half + 1) * 512],
                            xres[:, (qt % 2) * C + half * 512:(qt % 2) * C + (half + 1) * 512],
                            pp[:, :]))
                    blk.vector(s_res)

                pp_guard[0] = pp_guard[1] = self.n("v")
                if qt == 0:
                    self.tap(blk, f"xres0_{b}{ci}", xres[:, 0:C])

                def s_sp(sy, qt=qt):
                    self.wt(sy, "v")
                    self.inc("st", sy.dma_start(
                        out=self.resid_s[b, ci * 512 + qt * 128: ci * 512 + (qt + 1) * 128, :],
                        in_=xres[:, (qt % 2) * C:(qt % 2 + 1) * C]), 16)
                blk.sync(s_sp)
                res_guard[qt % 2] = self.n("st")
            if ci == 0:
                self.tap(blk, f"qT0_{b}", qTc[:, :])
                self.tap(blk, f"attn0_{b}", attn[:, :])


    def _b3_qk(self, blk, b, ci, h, qt, gd, ranges, poff, qTc, klocT, pl):
        chunk, hof = h // 2, (h % 2) * 64
        pslot = (h * 4 + qt) % 2

        def s_qk(t, g=gd["pl"][pslot]):
            self.wt(t, "a", g)
            for r, (off, ln) in enumerate(ranges):
                self.inc("mm", t.matmul(
                    pl[:, poff[r]:poff[r] + ln],
                    qTc[:, chunk * 512 + qt * 128: chunk * 512 + (qt + 1) * 128][hof:hof + 64, :],
                    klocT[chunk][hof:hof + 64, off:off + ln],
                    start=(r == 0), stop=(r == len(ranges) - 1)))
        blk.tensor(s_qk)
        gd["qkmm"][(h * 4 + qt) % 4] = self.n("mm")

    def _b3_soft(self, blk, b, ci, h, qt, i, gd, L, probs, smax, rs, pl):
        uslot = i % 3

        def s_max(v, g=gd["qkmm"][i % 4]):
            self.wt(v, "mm", g)
            v.reduce_max(out=smax[:, qt:qt + 1], in_=pl[:, 0:L],
                         axis=AX.X, negate=True)
            self.sep(v)
            self.inc("v", v.tensor_scalar_mul(smax[:, qt:qt + 1],
                                              smax[:, qt:qt + 1], SCALE))
        blk.vector(s_max)
        gd["smv"][i] = self.n("v")

        def s_exp(sc, g=gd["pr"][uslot], smv=gd["smv"][i]):
            self.wt(sc, "v", smv)
            self.wt(sc, "mm", g)
            self.inc("a", sc.activation(
                out=probs[:, uslot * 512: uslot * 512 + L], in_=pl[:, 0:L],
                func=ACT.Exp, scale=SCALE, bias=smax[:, qt:qt + 1],
                accum_out=rs[:, qt:qt + 1]))
        blk.scalar(s_exp)
        gd["pl"][i % 2] = self.n("a")
        gd["expa"][i] = self.n("a")

    def _b3_tpv(self, blk, b, ci, h, qt, i, gd, poff, rtiles,
                vr, probs, probsT, attn, rs, pts, paA, paB):
        pa = paA if i % 2 == 0 else paB
        uslot = i % 3
        npieces = len(rtiles)
        evac_cnt = [0] * npieces

        def emit_tp(pi):
            r, bi, soff, nr = rtiles[pi]
            pk0 = poff[r] + bi * 128
            tslot = pi % 4
            pt = pts[tslot]

            def s_tp(t, pk0=pk0, nr=nr, pt=pt, g=gd["pt"][tslot],
                     ea=gd["expa"][i]):
                self.wt(t, "a", ea)
                self.wt(t, "v", g)
                self.inc("mm", t.transpose(
                    pt[0:nr, 0:128],
                    probs[:, uslot * 512 + pk0: uslot * 512 + pk0 + nr],
                    self.eye[:, :]))
            blk.tensor(s_tp)

            def s_evt(v, nr=nr, pt=pt, tslot=tslot, g=gd["prT"][tslot]):
                self.wt(v, "mm")
                self.wt(v, "mm", g)
                self.inc("v", v.tensor_copy(
                    out=probsT[0:nr, tslot * 128:tslot * 128 + 128],
                    in_=pt[0:nr, 0:128]))
            blk.vector(s_evt)
            gd["pt"][tslot] = self.n("v")
            evac_cnt[pi] = self.n("v")

        def emit_pv(pi):
            r, bi, soff, nr = rtiles[pi]
            tslot = pi % 4

            def s_pv(t, pi=pi, nr=nr, ti=pi, tslot=tslot, g=gd["pa"][i % 2],
                     ev=evac_cnt[pi]):
                self.wt(t, "v", ev)
                if pi == 0:
                    self.wt(t, "v", g)
                    self.wt(t, "ld2")
                self.inc("mm", t.matmul(
                    pa[:, 0:64],
                    probsT[0:nr, tslot * 128:tslot * 128 + 128],
                    vr[:, ti * C:(ti + 1) * C][0:nr, h * 64:(h + 1) * 64],
                    start=(pi == 0), stop=(pi == npieces - 1)))
            blk.tensor(s_pv)
            gd["prT"][tslot] = self.n("mm")

        for pi in range(min(4, npieces)):
            emit_tp(pi)
        for pi in range(npieces):
            emit_pv(pi)
            if pi + 4 < npieces:
                emit_tp(pi + 4)
        gd["pr"][uslot] = self.n("mm")
        pv_end = self.n("mm")

        def s_attn_ev(v, pv_end=pv_end, exp_a=gd["expa"][i]):
            self.wt(v, "a", exp_a)
            self.wt(v, "mm", pv_end)
            v.reciprocal(out=rs[:, 4 + qt:5 + qt], in_=rs[:, qt:qt + 1])
            v.tensor_copy(out=attn[:, qt * C + h * 64: qt * C + (h + 1) * 64],
                          in_=pa[:, 0:64])
            self.sep(v)
            self.inc("v", v.tensor_scalar_mul(
                attn[:, qt * C + h * 64: qt * C + (h + 1) * 64],
                attn[:, qt * C + h * 64: qt * C + (h + 1) * 64],
                rs[:, 4 + qt:5 + qt]))
        blk.vector(s_attn_ev)
        gd["pa"][i % 2] = self.n("v")

    def b4(self):
        nc = self.nc
        with (
            nc.sbuf_tensor("qf", [128, 2 * 2048], F32) as qf,
            nc.sbuf_tensor("simb", [128, 2 * 4096], F32) as simb,
            nc.sbuf_tensor("tkb", [128, 512], F32) as tkb,
            nc.sbuf_tensor("tku", [128, 128], U32) as tku,
            nc.psum_tensor("psA", [128, 256], F32) as psA,
            nc.psum_tensor("psB", [128, 256], F32) as psB,
            nc.Block("b4") as blk,
        ):
            qf_guard = [0, 0]
            sim_guard = [0, 0]
            ps_guard = [0, 0]
            for tg in range(NTILE):
                b, tt = tg // 16, tg % 16
                qslot = tg % 2
                sslot = tg % 2

                def s_ld(sy, tg=tg, qslot=qslot, g=qf_guard[qslot]):
                    self.wt(sy, "v", g)
                    for jc in range(16):
                        self.inc("ld", sy.dma_start(
                            out=qf[:, qslot * 2048 + jc * 128: qslot * 2048 + (jc + 1) * 128],
                            in_=self.q_s[jc, :, tg * 128:(tg + 1) * 128]), 16)
                blk.sync(s_ld)

                def s_bn(v, qslot=qslot):
                    self.wt(v, "ld")
                    for jc in range(16):
                        sl = qf[:, qslot * 2048 + jc * 128: qslot * 2048 + (jc + 1) * 128]
                        ins = v.tensor_scalar(out=sl, in0=sl,
                                              scalar1=self.bn_a[:, jc:jc + 1],
                                              scalar2=self.bn_b[:, jc:jc + 1],
                                              op0=ALU.mult, op1=ALU.add)
                    self.inc("v", ins)
                blk.vector(s_bn)
                bn_v = self.n("v")

                for jc in range(16):
                    ps = psA if jc % 2 == 0 else psB

                    def s_mm(t, jc=jc, qslot=qslot, ps=ps, g=ps_guard[jc % 2],
                             bn_v=bn_v):
                        self.wt(t, "v", bn_v)
                        self.wt(t, "v", g)
                        self.inc("mm", t.matmul(
                            ps[:, :],
                            qf[:, qslot * 2048 + jc * 128: qslot * 2048 + (jc + 1) * 128],
                            self.keys_sb[:, jc * NKEY:(jc + 1) * NKEY],
                            start=True, stop=True))
                    blk.tensor(s_mm)

                    def s_ev(v, jc=jc, sslot=sslot, ps=ps,
                             g=sim_guard[sslot] if jc == 0 else None):
                        self.wt(v, "mm")
                        if g is not None:
                            self.wt(v, "v", g)
                        self.inc("v", v.tensor_copy(
                            out=simb[:, sslot * 4096 + jc * NKEY: sslot * 4096 + (jc + 1) * NKEY],
                            in_=ps[:, :]))
                    blk.vector(s_ev)
                    ps_guard[jc % 2] = self.n("v")
                qf_guard[qslot] = self.n("v")

                for hp in range(4):
                    chains = [self._b4_chain(tg, 2 * hp + par, sslot, par, simb, tkb, tku)
                              for par in range(2)]

                    def s_tk1(v, chains=chains):
                        self.wt(v, "v")
                        for opa, opb in zip(chains[0][0], chains[1][0]):
                            opa(v)
                            opb(v)
                        self.inc("v", v.tensor_copy(out=tkb[:, 248:249], in_=tkb[:, 152:153]))
                    blk.vector(s_tk1)

                    def s_tk2(sc, chains=chains):
                        self.wt(sc, "v")
                        for par in range(2):
                            o = 256 * par
                            ins = sc.activation(out=tkb[:, o + 160:o + 168],
                                                in_=tkb[:, o + 96:o + 104],
                                                func=ACT.Exp, bias=tkb[:, o + 152:o + 153],
                                                accum_out=tkb[:, o + 153:o + 154])
                        self.inc("a", ins)
                    blk.scalar(s_tk2)

                    def s_tk3(v, chains=chains, tg=tg, hp=hp):
                        self.wt(v, "a")
                        for par in range(2):
                            o = 256 * par
                            v.reciprocal(out=tkb[:, o + 154:o + 155],
                                         in_=tkb[:, o + 153:o + 154])
                        for par in range(2):
                            o = 256 * par
                            hf = 2 * hp + par
                            self.inc("v", v.tensor_scalar_mul(
                                self.g_all[:, tg * 64 + hf * 8: tg * 64 + hf * 8 + 8],
                                tkb[:, o + 160:o + 168], tkb[:, o + 154:o + 155]))
                    blk.vector(s_tk3)
                sim_guard[sslot] = self.n("v")
            self.tap(blk, "idx_all", self.idx_all[:, :])
            self.tap(blk, "g_all", self.g_all[:, :])

    def _b4_chain(self, tg, hf, sslot, par, simb, tkb, tku):
        o = 256 * par
        u = 64 * par
        sx = simb[:, sslot * 4096 + hf * NKEY: sslot * 4096 + (hf + 1) * NKEY]
        sy_ = simb[:, sslot * 4096 + (8 + hf) * NKEY: sslot * 4096 + (9 + hf) * NKEY]
        T = tkb
        U = tku
        ops = [
            lambda v: v.max(out=T[:, o:o + 8], in_=sx),
            lambda v: v.max(out=T[:, o + 8:o + 16], in_=sy_),
            lambda v: v.max_index(out=U[:, u:u + 8], in_max=T[:, o:o + 8], in_values=sx),
            lambda v: v.max_index(out=U[:, u + 8:u + 16], in_max=T[:, o + 8:o + 16], in_values=sy_),
            lambda v: v.tensor_copy(out=T[:, o + 16:o + 24], in_=U[:, u:u + 8]),
            lambda v: v.tensor_copy(out=T[:, o + 24:o + 32], in_=U[:, u + 8:u + 16]),
            lambda v: v.tensor_tensor(
                out=T[:, o + 32:o + 96].rearrange("p (i j) -> p i j", i=8),
                in0=T[:, o:o + 8].unsqueeze(2).to_broadcast([128, 8, 8]),
                in1=T[:, o + 8:o + 16].unsqueeze(1).to_broadcast([128, 8, 8]),
                op=ALU.add),
            lambda v: v.max(out=T[:, o + 96:o + 104], in_=T[:, o + 32:o + 96]),
            lambda v: v.max_index(out=U[:, u + 16:u + 24], in_max=T[:, o + 96:o + 104],
                                  in_values=T[:, o + 32:o + 96]),
            lambda v: v.tensor_scalar(out=U[:, u + 24:u + 32], in0=U[:, u + 16:u + 24],
                                      scalar1=3, scalar2=None,
                                      op0=ALU.logical_shift_right),
            lambda v: v.tensor_scalar(out=U[:, u + 32:u + 40], in0=U[:, u + 16:u + 24],
                                      scalar1=7, scalar2=None, op0=ALU.bitwise_and),
            lambda v: v.tensor_copy(out=T[:, o + 104:o + 112], in_=U[:, u + 24:u + 32]),
            lambda v: v.tensor_copy(out=T[:, o + 112:o + 120], in_=U[:, u + 32:u + 40]),
            lambda v: v.tensor_tensor(
                out=T[:, o + 128:o + 192].rearrange("p (s j) -> p s j", s=8),
                in0=T[:, o + 104:o + 112].unsqueeze(2).to_broadcast([128, 8, 8]),
                in1=self.iota8[:, :].unsqueeze(1).to_broadcast([128, 8, 8]),
                op=ALU.is_equal),
            lambda v: v.tensor_tensor(
                out=T[:, o + 192:o + 256].rearrange("p (s j) -> p s j", s=8),
                in0=T[:, o + 128:o + 192].rearrange("p (s j) -> p s j", s=8),
                in1=T[:, o + 16:o + 24].unsqueeze(1).to_broadcast([128, 8, 8]),
                op=ALU.mult),
            lambda v: v.reduce_sum(out=T[:, o + 120:o + 128],
                                   in_=T[:, o + 192:o + 256].rearrange("p (s j) -> p s j", s=8),
                                   axis=AX.X),
            lambda v: v.tensor_tensor(
                out=T[:, o + 128:o + 192].rearrange("p (s j) -> p s j", s=8),
                in0=T[:, o + 112:o + 120].unsqueeze(2).to_broadcast([128, 8, 8]),
                in1=self.iota8[:, :].unsqueeze(1).to_broadcast([128, 8, 8]),
                op=ALU.is_equal),
            lambda v: v.tensor_tensor(
                out=T[:, o + 192:o + 256].rearrange("p (s j) -> p s j", s=8),
                in0=T[:, o + 128:o + 192].rearrange("p (s j) -> p s j", s=8),
                in1=T[:, o + 24:o + 32].unsqueeze(1).to_broadcast([128, 8, 8]),
                op=ALU.mult),
            lambda v: v.reduce_sum(out=T[:, o + 136:o + 144],
                                   in_=T[:, o + 192:o + 256].rearrange("p (s j) -> p s j", s=8),
                                   axis=AX.X),
            lambda v: v.scalar_tensor_tensor(out=T[:, o + 144:o + 152], in0=T[:, o + 120:o + 128],
                                             scalar=256.0, in1=T[:, o + 136:o + 144],
                                             op0=ALU.mult, op1=ALU.add),
            lambda v: v.tensor_copy(
                out=self.idx_all[:, tg * 64 + hf * 8: tg * 64 + hf * 8 + 8],
                in_=T[:, o + 144:o + 152]),
            lambda v: v.tensor_scalar_mul(T[:, o + 152:o + 153], T[:, o + 96:o + 97], -1.0),
        ]
        return (ops,)

    def b5(self):
        nc = self.nc
        acc_vcount = {}
        gcnt = {}
        with (
            nc.sbuf_tensor("gring", [128, NBUF * 2048], F32) as gring,
            nc.sbuf_tensor("xn5", [128, 2 * C], F32) as xn5,
            nc.sbuf_tensor("acc5", [128, 2 * C], F32) as acc5,
            nc.sbuf_tensor("scr5", [128, C], F32) as scr5,
            nc.sbuf_tensor("hb5", [128, 32], F32) as hb5,
            nc.Block("b5") as blk,
        ):
            xn_guard = [0, 0]
            acc_guard = [0, 0]
            for tg in range(NTILE):
                b, tt = tg // 16, tg % 16
                slot = tg % 2

                def s_ld(sy, tg=tg, b=b, tt=tt, slot=slot, gx=xn_guard[slot],
                         ga=acc_guard[slot]):
                    self.wt(sy, "v", gx)
                    self.inc("ld", sy.dma_start(
                        out=xn5[:, slot * C:(slot + 1) * C],
                        in_=self.x[b, tt * 128:(tt + 1) * 128, :]), 16)
                    self.wt(sy, "st", ga)
                    self.inc("ld", sy.dma_start(
                        out=acc5[:, slot * C:(slot + 1) * C],
                        in_=self.resid_s[b, tt * 128:(tt + 1) * 128, :]), 16)
                blk.sync(s_ld)

                def s_norm(v, tg=tg, slot=slot):
                    self.wt(v, "ld")
                    self.inc("v", v.tensor_scalar(
                        out=xn5[:, slot * C:(slot + 1) * C],
                        in0=xn5[:, slot * C:(slot + 1) * C],
                        scalar1=self.rstd_all[:, tg:tg + 1],
                        scalar2=self.nmr_all[:, tg:tg + 1],
                        op0=ALU.mult, op1=ALU.add))
                blk.vector(s_norm)

                def emit_gathers(grp, tg=tg):
                    for i in range(4):
                        gk = tg * 64 + grp * 4 + i
                        buf = gk % NBUF

                        def s_g(g, gk=gk, buf=buf,
                                gv=acc_vcount.get(gk - NBUF, 0)):
                            if gv:
                                self.wt(g, "v", gv)
                            self.inc("g", g.indirect_dma_start(
                                gring[:, buf * 2048:(buf + 1) * 2048], None,
                                self.tbl[:, :],
                                bass.IndirectOffsetOnAxis(
                                    self.idx_all[:, gk:gk + 1], 0)), 16)
                        blk.gpsimd(s_g)
                        gcnt[gk] = self.n("g")

                def dot(v, grp, i, slot=slot, tg=tg):
                    gk = tg * 64 + grp * 4 + i
                    buf = gk % NBUF
                    hcol = (grp % 4) * 4 + i
                    self.wt(v, "g", gcnt[gk])
                    self.inc("v", v.scalar_tensor_tensor(
                        out=scr5[:, :],
                        in0=gring[:, buf * 2048: buf * 2048 + 1024],
                        scalar=1.0,
                        in1=xn5[:, slot * C:(slot + 1) * C],
                        op0=ALU.bypass, op1=ALU.mult,
                        accum_out=hb5[:, hcol:hcol + 1]))

                def acc(v, grp, i, slot=slot, tg=tg):
                    gk = tg * 64 + grp * 4 + i
                    buf = gk % NBUF
                    wcol = 16 + (grp % 4) * 4 + i
                    self.inc("v", v.scalar_tensor_tensor(
                        out=acc5[:, slot * C:(slot + 1) * C],
                        in0=gring[:, buf * 2048 + 1024: (buf + 1) * 2048],
                        scalar=hb5[:, wcol:wcol + 1],
                        in1=acc5[:, slot * C:(slot + 1) * C],
                        op0=ALU.mult, op1=ALU.add))
                    acc_vcount[gk] = self.n("v")

                def gelu(sc, grp):
                    hc = (grp % 4) * 4
                    self.wt(sc, "v")
                    self.inc("a", sc.activation(out=hb5[:, 16 + hc:20 + hc],
                                                in_=hb5[:, hc:hc + 4],
                                                func=ACT.Gelu_apprx_tanh))

                def wmul(v, grp, tg=tg):
                    hc = (grp % 4) * 4
                    self.wt(v, "a")
                    self.inc("v", v.tensor_mul(
                        hb5[:, 16 + hc:20 + hc], hb5[:, 16 + hc:20 + hc],
                        self.g_all[:, tg * 64 + grp * 4: tg * 64 + grp * 4 + 4]))

                emit_gathers(0)
                emit_gathers(1)

                def s_d01(v):
                    for i in range(4):
                        dot(v, 0, i)
                blk.vector(s_d01)
                blk.scalar(lambda sc: gelu(sc, 0))

                def s_d1(v):
                    for i in range(4):
                        dot(v, 1, i)
                blk.vector(s_d1)
                blk.scalar(lambda sc: gelu(sc, 1))

                for grp in range(2, 16):
                    emit_gathers(grp)

                    def s_body(v, grp=grp):
                        wmul(v, grp - 2)
                        for i in range(4):
                            dot(v, grp, i)
                            acc(v, grp - 2, i)
                    blk.vector(s_body)
                    blk.scalar(lambda sc, grp=grp: gelu(sc, grp))

                def s_tail(v):
                    wmul(v, 14)
                    for i in range(4):
                        self.sep(v)
                        acc(v, 14, i)
                    wmul(v, 15)
                    for i in range(4):
                        self.sep(v)
                        acc(v, 15, i)
                blk.vector(s_tail)

                def s_out(sy, tg=tg, b=b, tt=tt, slot=slot):
                    self.wt(sy, "v")
                    self.inc("st", sy.dma_start(
                        out=self.out[b, tt * 128:(tt + 1) * 128, :],
                        in_=acc5[:, slot * C:(slot + 1) * C]), 16)
                blk.sync(s_out)
                xn_guard[slot] = self.n("v")
                acc_guard[slot] = self.n("st")

            def s_fin(sy):
                self.wt(sy, "st")
                self.wt(sy, "st2")
            blk.sync(s_fin)

    def build(self):
        nc = self.nc
        self.b0()
        if self.stop_after == "b0":
            return self._fin()
        es1 = contextlib.ExitStack()
        xnT = [es1.enter_context(nc.sbuf_tensor(f"xnTc{i}", [128, N], F32)) for i in range(DCH)]
        wqk = [es1.enter_context(nc.sbuf_tensor(f"wqk{i}", [128, 2 * C], F32)) for i in range(DCH)]
        with nc.Block("b1w") as blk:
            def s_w(sy):
                for dch in range(DCH):
                    self.inc("ld", sy.dma_start(out=wqk[dch][:, :],
                                                in_=self.wqkv[dch * 128:(dch + 1) * 128, 0:2 * C]), 16)
            blk.sync(s_w)
        for b in range(NB):
            self.b1(b, wqk, xnT)
        es1.close()
        if self.stop_after == "b1":
            return self._fin()
        self.b2()
        if self.stop_after == "b2":
            return self._fin()
        for b in range(NB):
            self.b3(b)
        if self.stop_after == "b3":
            return self._fin()
        self.b4()
        if self.stop_after == "b4":
            return self._fin()
        self.b5()
        return self._fin()

    def _fin(self):
        nc = self.nc
        if self.taps:
            with nc.Block("tapw") as blk:
                def f(sy):
                    self.wt(sy, "st2")
                blk.sync(f)
        return nc


def build_nc(n_cores=NCORES, taps=(), stop_after=None):
    mk = MK(n_cores=n_cores, taps=taps, stop_after=stop_after)
    mk.build()
    return mk


def make_in_map(x_shard, inputs):
    keys_r = np.ascontiguousarray(
        np.transpose(np.asarray(inputs["L0_ffn_keys"], dtype=np.float32), (3, 2, 0, 1))
    ).reshape(128, 16, NKEY)
    tbl = np.concatenate([
        np.asarray(inputs["L0_ffn_down_embed"], dtype=np.float32),
        np.asarray(inputs["L0_ffn_up_embed"], dtype=np.float32)], axis=1)
    return {
        "x": np.ascontiguousarray(x_shard),
        "wqkv": np.ascontiguousarray(np.asarray(inputs["L0_attn_W_qkv"], np.float32)),
        "bqkv": np.ascontiguousarray(np.asarray(inputs["L0_attn_b_qkv"], np.float32)),
        "wproj": np.ascontiguousarray(np.asarray(inputs["L0_attn_W_proj"], np.float32)),
        "bproj": np.ascontiguousarray(np.asarray(inputs["L0_attn_b_proj"], np.float32)),
        "wq_in": np.ascontiguousarray(np.asarray(inputs["L0_ffn_w_q"], np.float32)),
        "gamma": np.ascontiguousarray(np.asarray(inputs["L0_ffn_bn_gamma"], np.float32)),
        "beta": np.ascontiguousarray(np.asarray(inputs["L0_ffn_bn_beta"], np.float32)),
        "keys": keys_r,
        "tbl": np.ascontiguousarray(tbl),
    }


def kernel(**inputs):
    x = np.asarray(inputs["L0_attn_x"], dtype=np.float32)
    mk = MK(n_cores=NCORES)
    mk.build()
    base = make_in_map(x[0:2], inputs)
    in_maps = []
    for c in range(NCORES):
        m = dict(base)
        m["x"] = np.ascontiguousarray(x[2 * c:2 * c + 2])
        in_maps.append(m)
    res = run_bass_kernel_spmd(mk.nc, in_maps, core_ids=list(range(NCORES)))
    out = np.concatenate([r["out"] for r in res.results], axis=0)
    return out.astype(np.float32)
